# version 08
# baseline (speedup 1.0000x reference)
import math
import numpy as np
import ml_dtypes
import concourse.bass as bass
import concourse.mybir as mybir
from concourse.bass_utils import run_bass_kernel_spmd
from concourse.aot_env import allow_ici_all_to_all
from contextlib import ExitStack

F32 = mybir.dt.float32
BF16 = mybir.dt.bfloat16
I8 = mybir.dt.int8
AF = mybir.ActivationFunctionType
ALU = mybir.AluOpType
BF = ml_dtypes.bfloat16

S, DM, HF = 8192, 4096, 11008
NCORE = 8
TOWN = S // NCORE
KI = DM // 128
HB = HF // 128
HBP = 88
ISQ = 1.0 / math.sqrt(128.0)
CSHIFT = 25.0
EPS_LN = 1e-5
EPS_RS = 1e-30
NTT = S // 512
W1SLOT = HBP // NCORE
W2SLOT = 32 // NCORE

ENGS = ("sync", "tensor", "vector", "scalar", "gpsimd")
CVALS = [0.0, 1.0, -0.5, EPS_LN, ISQ, -CSHIFT, -1.0 / DM, 1.0 / DM]



def _payload_bytes(w):
    w = np.ascontiguousarray(np.asarray(w))
    return np.ascontiguousarray(w.view(np.int8).reshape(w.shape + (4,))[..., 0])


def _host_prep(inputs):
    x = np.ascontiguousarray(np.asarray(inputs["L0_attn_hidden_states"])[0])
    qb_q = _payload_bytes(inputs["L0_attn_qw_q"])
    qb_k = _payload_bytes(inputs["L0_attn_qw_k"])
    qb_v = _payload_bytes(inputs["L0_attn_qw_v"])
    qb_o = _payload_bytes(inputs["L0_attn_qw_o"])
    sc_q = np.ascontiguousarray(np.asarray(inputs["L0_attn_scales_q"], dtype=np.float32))
    sc_k = np.ascontiguousarray(np.asarray(inputs["L0_attn_scales_k"], dtype=np.float32))
    sc_v = np.ascontiguousarray(np.asarray(inputs["L0_attn_scales_v"], dtype=np.float32))
    sc_o = np.ascontiguousarray(np.asarray(inputs["L0_attn_scales_o"], dtype=np.float32))

    def ffn1_layout(w, s):
        pb = _payload_bytes(w)
        pl = np.ascontiguousarray(
            pb.reshape(HF, 32, 64).reshape(HB, 128, 32, 64).transpose(0, 3, 2, 1))
        sb = np.asarray(s, dtype=np.float32).reshape(HF, 32, 2)
        sb = np.ascontiguousarray(
            sb.reshape(HB, 128, 32, 2).transpose(0, 3, 2, 1))
        return pl, sb

    def ffn2_layout(w, s):
        pb = _payload_bytes(w)
        pl = np.ascontiguousarray(
            pb.reshape(DM, HB, 64).reshape(32, 128, HB, 64).transpose(0, 3, 2, 1))
        sb = np.asarray(s, dtype=np.float32).reshape(DM, HB, 2)
        sb = np.ascontiguousarray(
            sb.reshape(32, 128, HB, 2).transpose(0, 3, 2, 1))
        return pl, sb

    p1, sb1 = ffn1_layout(inputs["L0_ffn_w1"], inputs["L0_ffn_s1"])
    p3, sb3 = ffn1_layout(inputs["L0_ffn_w3"], inputs["L0_ffn_s3"])
    p2, sb2 = ffn2_layout(inputs["L0_ffn_w2"], inputs["L0_ffn_s2"])

    inv = 1.0 / (10000.0 ** (np.arange(0, 128, 2, dtype=np.float64) / 128.0))
    ang = np.arange(S, dtype=np.float64)[None, :] * inv[:, None]
    cosT = np.empty((128, S), np.float32)
    sinT2 = np.empty((128, S), np.float32)
    cosT[:64] = np.cos(ang)
    cosT[64:] = np.cos(ang)
    sinT2[:64] = -np.sin(ang)
    sinT2[64:] = np.sin(ang)

    E = np.zeros((32, 32 * 128), dtype=BF)
    for b in range(32):
        E[b, b * 128:(b + 1) * 128] = 1.0
    sel2 = np.zeros((34, 128), dtype=BF)
    for b0 in (0, 32):
        sel2[b0, :64] = 1.0 / 16.0
        sel2[b0 + 1, 64:] = 1.0
    ones1 = np.ones((1, 128), np.float32)
    onespv = np.ones((128, 1), dtype=BF)
    ident = np.eye(128, dtype=BF)
    masks = np.zeros((128, 4 * 512), dtype=BF)
    pg = np.arange(128)[:, None]
    jg = np.arange(512)[None, :]
    for i in range(4):
        masks[:, i * 512:(i + 1) * 512] = (pg <= jg - 128 * i).astype(BF)

    shared = dict(x_all=x, cosT=cosT, sinT2=sinT2, E=E, sel2=sel2, ones1=ones1,
                  ones1b=np.ones((33, 128), BF), onespv=onespv, ident=ident, masks=masks)
    per_core = []
    for c in range(NCORE):
        qcols = slice(4 * c * 128, 4 * c * 128 + 512)
        kcols = slice(c * 128, (c + 1) * 128)
        d = dict(
            x_own=np.ascontiguousarray(x[c * TOWN:(c + 1) * TOWN]),
            qb_q_own=np.ascontiguousarray(qb_q[:, qcols]),
            sc_q_own=np.ascontiguousarray(sc_q[:, qcols]),
            qb_kv_own=np.ascontiguousarray(
                np.concatenate([qb_k[:, kcols], qb_v[:, kcols]], axis=1)),
            sc_kv_own=np.ascontiguousarray(
                np.concatenate([sc_k[:, kcols], sc_v[:, kcols]], axis=1)),
            qb_o_own=np.ascontiguousarray(qb_o[4 * c * 128:4 * (c + 1) * 128, :]),
            sc_o_own=np.ascontiguousarray(sc_o[4 * c:4 * (c + 1), :]),
            p2_own=np.ascontiguousarray(p2[W2SLOT * c:W2SLOT * (c + 1)]),
            sb2_own=np.ascontiguousarray(sb2[W2SLOT * c:W2SLOT * (c + 1)]),
        )
        p1c = np.zeros((W1SLOT,) + p1.shape[1:], p1.dtype)
        sb1c = np.zeros((W1SLOT,) + sb1.shape[1:], sb1.dtype)
        p3c = np.zeros_like(p1c)
        sb3c = np.zeros_like(sb1c)
        for i, sl in enumerate(range(W1SLOT * c, W1SLOT * (c + 1))):
            if sl < HB:
                p1c[i], sb1c[i] = p1[sl], sb1[sl]
                p3c[i], sb3c[i] = p3[sl], sb3[sl]
        d["p1_own"], d["sb1_own"] = p1c, sb1c
        d["p3_own"], d["sb3_own"] = p3c, sb3c
        per_core.append(d)
    return shared, per_core



class _Sem:
    __slots__ = ("h", "n", "name")

    def __init__(self, h, name):
        self.h, self.n, self.name = h, 0, name


class Plan:
    def __init__(self, nc, es):
        self.nc = nc
        self.es = es
        self.th = {e: [] for e in ENGS}
        self.waited = {e: {} for e in ENGS}
        self.nops = {e: 0 for e in ENGS}
        self.dma_sems = set()
        self.all_dsems = []

    def sem(self, name):
        return _Sem(self.es.enter_context(self.nc.semaphore(name)), name)

    def dsem(self, name):
        s = self.sem(name)
        self.all_dsems.append(s)
        return s

    def op(self, eng, fn, inc=None):
        toks = []
        incs = []
        if inc:
            for s, k in inc:
                s.n += k
                incs.append((s, k))
                toks.append((s, s.n))

        def thunk(e, fn=fn, incs=incs):
            ins = fn(e)
            for s, k in incs:
                ins.then_inc(s.h, k)

        self.th[eng].append(thunk)
        self.nops[eng] += 1
        return toks

    def wait(self, eng, *tokens):
        agg = {}
        for t in tokens:
            if t is None:
                continue
            s, v = t
            if s.name not in agg or agg[s.name][1] < v:
                agg[s.name] = (s, v)
        for s, v in agg.values():
            if v <= self.waited[eng].get(s.name, 0):
                continue
            self.waited[eng][s.name] = v

            def thunk(e, h=s.h, v=v):
                e.wait_ge(h, v)

            self.th[eng].append(thunk)

    def finalize(self):
        with self.nc.Block() as block:
            for name, reg in (("sync", block.sync), ("tensor", block.tensor),
                              ("vector", block.vector), ("scalar", block.scalar),
                              ("gpsimd", block.gpsimd)):
                thunks = self.th[name]

                def runner(e, thunks=thunks):
                    for t in thunks:
                        t(e)

                reg(runner)


def _carve(arena, off_bytes, dt, shape):
    esz = mybir.dt.size(dt)
    per_part = int(np.prod(shape[1:])) * esz
    apx = arena[:, off_bytes:off_bytes + per_part].bitcast(dt)
    if len(shape) > 2:
        dims = " ".join(f"d{i}" for i in range(1, len(shape)))
        apx = apx.rearrange(f"p ({dims}) -> p {dims}",
                            **{f"d{i}": shape[i] for i in range(1, len(shape))})
    if shape[0] != 128:
        apx = apx[0:shape[0]]
    return apx



def build_program(debug_taps=()):
    nc = bass.Bass("TRN2", target_bir_lowering=False)
    es = ExitStack()

    def inp(name, shape, dt):
        return nc.dram_tensor(name, list(shape), dt, kind="ExternalInput")

    d_x_all = inp("x_all", (S, DM), F32)
    d_x_own = inp("x_own", (TOWN, DM), F32)
    d_qbq = inp("qb_q_own", (DM, 512), I8)
    d_scq = inp("sc_q_own", (32, 512), F32)
    d_qbkv = inp("qb_kv_own", (DM, 256), I8)
    d_sckv = inp("sc_kv_own", (32, 256), F32)
    d_qbo = inp("qb_o_own", (512, DM), I8)
    d_sco = inp("sc_o_own", (4, DM), F32)
    d_p1 = inp("p1_own", (W1SLOT, 64, 32, 128), I8)
    d_sb1 = inp("sb1_own", (W1SLOT, 2, 32, 128), F32)
    d_p3 = inp("p3_own", (W1SLOT, 64, 32, 128), I8)
    d_sb3 = inp("sb3_own", (W1SLOT, 2, 32, 128), F32)
    d_p2 = inp("p2_own", (W2SLOT, 64, HB, 128), I8)
    d_sb2 = inp("sb2_own", (W2SLOT, 2, HB, 128), F32)
    d_cos = inp("cosT", (128, S), F32)
    d_sin = inp("sinT2", (128, S), F32)
    d_E = inp("E", (32, 32 * 128), BF16)
    d_sel2 = inp("sel2", (34, 128), BF16)
    d_ones1 = inp("ones1", (1, 128), F32)
    d_ones1b = inp("ones1b", (33, 128), BF16)
    d_onespv = inp("onespv", (128, 1), BF16)
    d_ident = inp("ident", (128, 128), BF16)
    d_masks = inp("masks", (128, 4 * 512), BF16)
    d_out = nc.dram_tensor("out_own", [TOWN, DM], F32, kind="ExternalOutput")

    q_scr = nc.dram_tensor("q_scr", [4, 128, S], BF16)
    w1t_in = nc.dram_tensor("w1t_in", [W1SLOT, 128, 4096], BF16)
    w3t_in = nc.dram_tensor("w3t_in", [W1SLOT, 128, 4096], BF16)
    w2t_in = nc.dram_tensor("w2t_in", [W2SLOT, 128, HF], BF16)
    wo_in = nc.dram_tensor("wo_in", [W2SLOT, 128, 4096], BF16)
    w1t_out = nc.dram_tensor("w1t_out", [HBP, 128, 4096], BF16, addr_space="Shared")
    w3t_out = nc.dram_tensor("w3t_out", [HBP, 128, 4096], BF16, addr_space="Shared")
    w2t_out = nc.dram_tensor("w2t_out", [32, 128, HF], BF16, addr_space="Shared")
    wo_out = nc.dram_tensor("wo_out", [32, 128, 4096], BF16, addr_space="Shared")
    a2a_in = nc.dram_tensor("a2a_in", [DM, TOWN], BF16)
    a2a_out = nc.dram_tensor("a2a_out", [DM, TOWN], BF16)
    g_scr = nc.dram_tensor("g_scr", [DM, TOWN], BF16)
    gpart_scr = nc.dram_tensor("gpart_scr", [DM, 512], F32)
    sg_scr = nc.dram_tensor("sg_scr", [DM, TOWN], BF16)
    scr_map = {"q_scr": q_scr, "a2a_in": a2a_in,
               "a2a_out": a2a_out, "g_scr": g_scr, "sg_scr": sg_scr,
               "w1t_out": w1t_out, "w3t_out": w3t_out, "w2t_out": w2t_out,
               "wo_out": wo_out}
    taps = {t: nc.dram_tensor("tap_" + t, list(scr_map[t].shape), scr_map[t].dtype,
                              kind="ExternalOutput") for t in debug_taps}

    sb = lambda name, shape, dt: es.enter_context(nc.sbuf_tensor(name, list(shape), dt))
    t_sel2 = sb("t_sel2", (34, 128), BF16)
    t_ones1 = sb("t_ones1", (1, 128), F32)
    t_ones1b = sb("t_ones1b", (33, 128), BF16)
    t_onespv = sb("t_onespv", (128, 1), BF16)
    t_ident = sb("t_ident", (128, 128), BF16)
    t_masks = sb("t_masks", (128, 4 * 512), BF16)
    t_cpool = sb("t_cpool", (128, 8), F32)
    t_stat = sb("t_stat", (128, 32), F32)
    t_bn = sb("t_bn", (128, 96), F32)
    t_rrow = sb("t_rrow", (1, 1024), F32)
    t_mand = sb("t_mand", (128, 1), mybir.dt.int32)
    t_mxor = sb("t_mxor", (128, 1), mybir.dt.int32)
    t_moff = sb("t_moff", (128, 1), F32)

    AR_W = sb("AR_W", (128, 48 * 1024), I8)
    AR_X = sb("AR_X", (128, 32 * 1024), I8)
    AR_XN = sb("AR_XN", (128, 48 * 1024), I8)
    AR_BIG = sb("AR_BIG", (128, 44 * 1024), I8)
    AR_SM = sb("AR_SM", (128, 26112), I8)
    t_E = _carve(AR_BIG, 0, BF16, (32, 32 * 128))
    t_scq = _carve(AR_BIG, 8 * 1024, F32, (32, 512))
    t_sckv = _carve(AR_BIG, 10 * 1024, F32, (32, 256))
    t_scqb = _carve(AR_BIG, 11 * 1024, BF16, (32, 512))
    t_sckvb = _carve(AR_BIG, 12 * 1024, BF16, (32, 256))

    for i, v in enumerate(CVALS):
        nc.const_aps.register(F32, v, t_cpool[:, i:i + 1])

    pbA = es.enter_context(nc.psum_tensor("pbA", [128, 1024], F32))
    pbB = es.enter_context(nc.psum_tensor("pbB", [128, 1024], F32))
    pbC = es.enter_context(nc.psum_tensor("pbC", [128, 512], F32))
    pbRS = es.enter_context(nc.psum_tensor("pbRS", [128, 512], F32))
    pb4 = es.enter_context(nc.psum_tensor("pb4", [128, 512], F32))
    pb5 = es.enter_context(nc.psum_tensor("pb5", [128, 512], F32))
    ptr = pbC.ap().bitcast(BF16)

    p = Plan(nc, es)
    Epe, Eve, Esc, Epo = p.sem("Epe"), p.sem("Eve"), p.sem("Esc"), p.sem("Epo")
    Ccc = p.sem("Ccc")

    def dma(q, out, in_, s):
        return p.op(q, lambda e, o=out, i=in_: e.dma_start(out=o, in_=i),
                    inc=[(s, 16)])[0]

    def phase_gate():
        toks = [(s, s.n) for s in p.all_dsems if s.n > 0]
        toks += [(s, s.n) for s in (Epe, Eve, Esc, Epo) if s.n > 0]
        for e in ENGS:
            p.wait(e, *toks)

    Lco = p.dsem("Lco")
    Lq = [p.dsem("Lq0"), p.dsem("Lq1")]
    Lx = [p.dsem("Lx0"), p.dsem("Lx1")]
    Lcs = [p.dsem("Lcs0"), p.dsem("Lcs1")]
    Wq = [p.dsem("Wq0"), p.dsem("Wq1")]
    Wk = p.dsem("Wk")
    Wv = p.dsem("Wv")
    LKV = p.dsem("LKV")
    Lqt = [p.dsem("Lqt0"), p.dsem("Lqt1")]
    Lpy = [p.dsem("Lpy0"), p.dsem("Lpy1")]
    Lsb = [p.dsem("Lsb0"), p.dsem("Lsb1")]
    Wctx = [p.dsem("Wctx0"), p.dsem("Wctx1")]
    Wde = [p.dsem("Wde0"), p.dsem("Wde1")]
    Lw1 = [p.dsem("Lw10"), p.dsem("Lw11")]
    Lw2 = [p.dsem("Lw20"), p.dsem("Lw21")]
    Lgp = [p.dsem("Lgp0"), p.dsem("Lgp1")]
    Wg = [p.dsem("Wg0"), p.dsem("Wg1")]
    Wgp = [p.dsem("Wgp0"), p.dsem("Wgp1")]
    Lctx = [p.dsem("Lctx0"), p.dsem("Lctx1"), p.dsem("Lctx2")]
    Lwo = [p.dsem("Lwo0"), p.dsem("Lwo1"), p.dsem("Lwo2")]
    Lgg = [p.dsem("Lgg0"), p.dsem("Lgg1")]
    Wsg = [p.dsem("Wsg0"), p.dsem("Wsg1")]
    Lsg = [p.dsem("Lsg0"), p.dsem("Lsg1")]
    Wout = [p.dsem("Wout0"), p.dsem("Wout1")]
    Wtap = p.dsem("Wtap")

    tk_consts = {}
    for nm, out, in_ in (("E", t_E[:], d_E[:]), ("sel2", t_sel2[:], d_sel2[:]),
                         ("ones1", t_ones1[:], d_ones1[:]),
                         ("ones1b", t_ones1b[:], d_ones1b[:]),
                         ("onespv", t_onespv[:], d_onespv[:]),
                         ("ident", t_ident[:], d_ident[:]),
                         ("masks", t_masks[:], d_masks[:]),
                         ("scq", t_scq[:], d_scq[:]), ("sckv", t_sckv[:], d_sckv[:])):
        tk_consts[nm] = dma("sync", out, in_, Lco)
    for nm in tk_consts:
        tk_consts[nm] = (Lco, Lco.n)

    for i, v in enumerate(CVALS):
        tk = p.op("vector", lambda e, i=i, v=v: e.memset(t_cpool[:, i:i + 1], v),
                  inc=[(Eve, 1)] if i == len(CVALS) - 1 else None)
    tok_cpool = tk[0]
    p.op("vector", lambda e: e.memset(t_mand[0:64, :], -16))
    p.op("vector", lambda e: e.memset(t_mand[64:128, :], 15))
    p.op("vector", lambda e: e.memset(t_mxor[0:64, :], 0))
    p.op("vector", lambda e: e.memset(t_mxor[64:128, :], 8))
    p.op("vector", lambda e: e.memset(t_moff[0:64, :], 0.0))
    tok_mconst = p.op("vector", lambda e: e.memset(t_moff[64:128, :], -8.0),
                      inc=[(Eve, 1)])[0]
    p.wait("vector", tk_consts["scq"], tk_consts["sckv"])
    p.op("vector", lambda e: e.tensor_copy(t_scqb[:], t_scq[:]))
    tok_scb = p.op("vector", lambda e: e.tensor_copy(t_sckvb[:], t_sckv[:]),
                   inc=[(Eve, 1)])[0]

    wq_sb = _carve(AR_W, 0, BF16, (128, 32 * 512))
    wkv_sb = _carve(AR_W, 32 * 1024, BF16, (128, 32 * 256))
    qb_st = [_carve(AR_SM, i * 512, I8, (128, 512)) for i in range(2)]
    kv_st = [_carve(AR_SM, 1024 + i * 256, I8, (128, 256)) for i in range(2)]

    p.wait("tensor", tk_consts["E"], tok_scb)
    tok_p0 = [None, None]
    for ib in range(KI):
        par = ib % 2
        if tok_p0[par] is not None:
            p.wait("sync", tok_p0[par])
        ldq = dma("sync", qb_st[par][:], d_qbq[ib * 128:(ib + 1) * 128, :], Lq[par])
        ldkv = dma("sync", kv_st[par][:], d_qbkv[ib * 128:(ib + 1) * 128, :], Lq[par])
        if tok_p0[1 - par] is not None:
            p.wait("tensor", tok_p0[1 - par])
        p.op("tensor", lambda e, ib=ib: e.matmul(
            pbC[:, 0:512], t_E[:, ib * 128:(ib + 1) * 128], t_scqb[:, :],
            start=True, stop=True))
        tkB = p.op("tensor", lambda e, ib=ib: e.matmul(
            pbRS[:, 0:256], t_E[:, ib * 128:(ib + 1) * 128], t_sckvb[:, :],
            start=True, stop=True), inc=[(Epe, 1)])
        p.wait("vector", ldq, ldkv, tkB[0])
        p.op("vector", lambda e, ib=ib, par=par: e.scalar_tensor_tensor(
            wq_sb[:, ib * 512:(ib + 1) * 512], qb_st[par][:], -8.0, pbC[:, 0:512],
            op0=ALU.add, op1=ALU.mult))
        tok_p0[par] = p.op("vector", lambda e, ib=ib, par=par: e.scalar_tensor_tensor(
            wkv_sb[:, ib * 256:(ib + 1) * 256], kv_st[par][:], -8.0, pbRS[:, 0:256],
            op0=ALU.add, op1=ALU.mult), inc=[(Eve, 1)])[0]

    x_tm = [_carve(AR_X, i * 16 * 1024, F32, (128, 4096)) for i in range(2)]
    xn_tm = [_carve(AR_XN, i * 8 * 1024, BF16, (128, 4096)) for i in range(2)]
    xn_fm = _carve(AR_XN, 16 * 1024, BF16, (128, 32 * 512))
    cos_sl = [_carve(AR_SM, 2048 + i * 2048, F32, (128, 512)) for i in range(2)]
    sin_sl = [_carve(AR_SM, 6144 + i * 2048, F32, (128, 512)) for i in range(2)]
    tmpA = _carve(AR_SM, 10240, F32, (128, 512))
    tmpB = _carve(AR_SM, 12288, F32, (128, 512))
    st_q = [_carve(AR_SM, 14336 + i * 1024, BF16, (128, 512)) for i in range(2)]
    st_v = _carve(AR_SM, 16384, BF16, (128, 512))

    qb_ap = [pbA[:, 0:512], pbA[:, 512:1024], pbB[:, 0:512], pbB[:, 512:1024]]
    xn_fm3 = xn_fm.rearrange("p (ib c) -> p ib c", ib=32)

    tok_xtm = [None, None]
    tok_xntm_w = [None, None]
    tok_xntm_r = [None, None]
    tok_dump = [None, None]
    tok_bn = [None, None]
    tok_stat = [None] * 8
    tok_ptr_r = [None, None]
    tok_ptr_w = [None, None]
    tok_xn_cons = None
    tok_q_free = None
    tok_k_free = None
    tok_v_free = None
    tok_cos = [None, None]
    tok_stq = [None, None]
    tok_ropeadd = None
    tok_stk = None
    tok_stv = None
    tok_qscr = {}
    tok_kscr = {}
    tok_vscr = {}

    K_sb = _carve(AR_BIG, 0, BF16, (128, S))
    V_sb = _carve(AR_BIG, 16 * 1024, BF16, (128, S))
    PT = [_carve(AR_BIG, 32 * 1024 + i * 2048, BF16, (128, 1024)) for i in range(2)]
    q_t = [_carve(AR_BIG, 36 * 1024 + i * 1024, BF16, (128, 512)) for i in range(2)]
    R_sb = [_carve(AR_BIG, 38 * 1024, F32, (128, 512)),
            _carve(AR_BIG, 40 * 1024, F32, (128, 512))]
    ctx_st = [_carve(AR_BIG, 42 * 1024 + i * 1024, BF16, (128, 512)) for i in range(2)]
    pay_st = [_carve(AR_SM, i * 1024, I8, (128, 1024)) for i in range(2)]
    ext_st = [_carve(AR_SM, 17920 + i * 1024, I8, (128, 1024)) for i in range(2)]
    wde_st = [_carve(AR_SM, 19968 + i * 2048, BF16, (128, 1024)) for i in range(2)]
    sbrb_st = [_carve(AR_SM, 24064, BF16, (34, 1024))[i * 32:i * 32 + 2]
               for i in range(2)]

    deq_jobs = []
    for mat, (srcp, srcs, dst, ncols) in (
            ("w1", (d_p1, d_sb1, w1t_in, 4096)),
            ("w3", (d_p3, d_sb3, w3t_in, 4096)),
            ("w2", (d_p2, d_sb2, w2t_in, HF))):
        nsl = W1SLOT if ncols == 4096 else W2SLOT
        for sl in range(nsl):
            coff = 0
            while coff < ncols:
                cw = min(512, ncols - coff)
                deq_jobs.append(("ffn", srcp, srcs, dst, sl, coff, cw))
                coff += cw
    for sl in range(W2SLOT):
        for coff in range(0, 4096, 512):
            deq_jobs.append(("wo", d_qbo, d_sco, wo_in, sl, coff, 512))
    n_w1 = sum(1 for j in deq_jobs if j[3] is w1t_in)
    n_w3 = sum(1 for j in deq_jobs if j[3] is w3t_in)
    n_w2 = sum(1 for j in deq_jobs if j[3] is w2t_in)
    ag_after = {n_w1 - 1: ("w1", w1t_in, w1t_out),
                n_w1 + n_w3 - 1: ("w3", w3t_in, w3t_out),
                n_w1 + n_w3 + n_w2 - 1: ("w2", w2t_in, w2t_out),
                len(deq_jobs) - 1: ("wo", wo_in, wo_out)}

    deq_state = dict(idx=0, ld_idx=0, tok_pay=[None, None], tok_ext=[None, None],
                     tok_wde=[None, None], tok_sbr=[None, None], pend={},
                     tok_B=None, tok_Bcons=None, last_wr=None, ag_toks=[])

    def deq_load(i):
        ds = deq_state
        kind, srcp, srcs, dst, sl, coff, cw = deq_jobs[i]
        par = i % 2
        if i < 2:
            p.wait("scalar", tok_p0[0], tok_p0[1])
        if ds["tok_ext"][par] is not None:
            p.wait("scalar", ds["tok_ext"][par])
        if ds["tok_sbr"][par] is not None:
            p.wait("gpsimd", ds["tok_sbr"][par])
        if kind == "ffn":
            srcp2 = srcp.ap().rearrange("s e k h -> s e (k h)")
            t1 = dma("scalar", pay_st[par][0:64, 0:cw], srcp2[sl, :, coff:coff + cw],
                     Lpy[par])
            t2 = dma("scalar", pay_st[par][64:128, 0:cw], srcp2[sl, :, coff:coff + cw],
                     Lpy[par])
            srcs2 = srcs.ap().rearrange("s h k j -> s h (k j)")
            t3 = dma("gpsimd", sbrb_st[par][:, 0:cw], srcs2[sl, :, coff:coff + cw],
                     Lsb[par])
        else:
            t2 = dma("scalar", pay_st[par][:, 0:cw],
                     srcp[sl * 128:(sl + 1) * 128, coff:coff + cw], Lpy[par])
            t3 = dma("gpsimd", sbrb_st[par][0:1, 0:cw], srcs[sl:sl + 1, coff:coff + cw],
                     Lsb[par])
        ds["pend"][i] = (t2, t3)

    def deq_compute(i):
        ds = deq_state
        kind, srcp, srcs, dst, sl, coff, cw = deq_jobs[i]
        par = i % 2
        t2, t3 = ds["pend"].pop(i)
        if kind == "ffn":
            p.wait("tensor", t3, tk_consts["sel2"])
            if ds["tok_Bcons"] is not None:
                p.wait("tensor", ds["tok_Bcons"])
            tkB = p.op("tensor", lambda e, par=par, cw=cw: e.matmul(
                pbC[:, 0:cw], t_sel2[par * 32:par * 32 + 2, :],
                sbrb_st[par][:, 0:cw], start=True, stop=True), inc=[(Epe, 1)])
            ds["tok_B"] = tkB[0]
            ds["tok_sbr"][par] = tkB[0]
            p.wait("vector", t2, tok_mconst)
            if ds["tok_wde"][par] is not None:
                p.wait("vector", ds["tok_wde"][par])
            tke = p.op("vector", lambda e, par=par, cw=cw: e.tensor_scalar(
                ext_st[par][:, 0:cw], pay_st[par][:, 0:cw], t_mand[:, 0:1],
                t_mxor[:, 0:1], op0=ALU.bitwise_and, op1=ALU.bitwise_xor),
                inc=[(Eve, 1)])[0]
            ds["tok_pay"][par] = tke
            ds["tok_ext"][par] = tke
            p.wait("vector", tke, ds["tok_B"])
            tkm = p.op("vector", lambda e, par=par, cw=cw: e.scalar_tensor_tensor(
                wde_st[par][:, 0:cw], ext_st[par][:, 0:cw], t_moff[:, 0:1],
                pbC[:, 0:cw], op0=ALU.add, op1=ALU.mult),
                inc=[(Eve, 1)])[0]
            ds["tok_Bcons"] = tkm
        else:
            p.wait("tensor", t3, tk_consts["ones1b"])
            if ds["tok_Bcons"] is not None:
                p.wait("tensor", ds["tok_Bcons"])
            tkB = p.op("tensor", lambda e, par=par, cw=cw: e.matmul(
                pbC[:, 0:cw], t_ones1b[par * 32:par * 32 + 1, :],
                sbrb_st[par][0:1, 0:cw], start=True, stop=True), inc=[(Epe, 1)])
            ds["tok_B"] = tkB[0]
            ds["tok_sbr"][par] = tkB[0]
            p.wait("vector", t2, ds["tok_B"])
            if ds["tok_wde"][par] is not None:
                p.wait("vector", ds["tok_wde"][par])
            tkm = p.op("vector", lambda e, par=par, cw=cw: e.scalar_tensor_tensor(
                wde_st[par][:, 0:cw], pay_st[par][:, 0:cw], -8.0, pbC[:, 0:cw],
                op0=ALU.add, op1=ALU.mult), inc=[(Eve, 1)])[0]
            ds["tok_ext"][par] = tkm
            ds["tok_Bcons"] = tkm
        p.wait("gpsimd", tkm)
        twr = dma("gpsimd", dst[sl, :, coff:coff + cw], wde_st[par][:, 0:cw], Wde[par])
        ds["tok_wde"][par] = twr
        ds["last_wr"] = twr
        if i in ag_after:
            nm, src_, dstt = ag_after[i]
            p.wait("gpsimd", (Wde[0], Wde[0].n), (Wde[1], Wde[1].n))
            ds["ag_toks"].append(p.op("gpsimd", lambda e, src_=src_, dstt=dstt:
                                      e.collective_compute(
                "AllGather", ALU.bypass, replica_groups=[list(range(NCORE))],
                ins=[src_.ap().opt()], outs=[dstt.ap().opt()]), inc=[(Ccc, 1)])[0])

    def emit_deq_chunk():
        ds = deq_state
        while ds["ld_idx"] < len(deq_jobs) and ds["ld_idx"] - ds["idx"] < 2:
            deq_load(ds["ld_idx"])
            ds["ld_idx"] += 1
        if ds["idx"] < ds["ld_idx"]:
            deq_compute(ds["idx"])
            ds["idx"] += 1

    tok_qt = [None, None]
    tok_PT_r = [None, None]
    tok_PT_w = [None, None]
    tok_S_r = [None, None]
    tok_S_w = [None, None]
    tok_ctx_free = [None, None]
    tok_rs_free = [None, None]
    tok_R_cons = None
    tok_ctxst = [None, None]
    tok_rrow = [None, None]
    tok_a2a_last = None
    tok_attn_mm = None

    tok_kv_ready = [None] * NTT
    rs_ap = [pbRS[0:1, 0:512], pbRS[32:33, 0:512]]
    ctx_ap = [pb4, pb5]
    Sb_ap = [pbA, pbB]

    def emit_S(ch, pi, tb, ldq):
        Sb = Sb_ap[ch]
        u0, u1 = 2 * pi, 2 * pi + 1
        p.wait("tensor", ldq, tok_kv_ready[tb])
        if tok_S_r[ch] is not None:
            p.wait("tensor", tok_S_r[ch])
        p.op("tensor", lambda e, Sb=Sb, u0=u0, ch=ch: e.matmul(
            Sb[:, 0:512], K_sb[:, u0 * 128:(u0 + 1) * 128], q_t[ch][:],
            start=True, stop=True))
        tkS = p.op("tensor", lambda e, Sb=Sb, u1=u1, ch=ch: e.matmul(
            Sb[:, 512:1024], K_sb[:, u1 * 128:(u1 + 1) * 128], q_t[ch][:],
            start=True, stop=True), inc=[(Epe, 1)])[0]
        tok_S_w[ch] = tkS
        p.wait("scalar", tkS)
        if tok_PT_r[ch] is not None:
            p.wait("scalar", tok_PT_r[ch])
        tkE = p.op("scalar", lambda e, Sb=Sb, ch=ch: e.activation(
            PT[ch][:], Sb[:], AF.Exp, bias=-CSHIFT, scale=ISQ),
            inc=[(Esc, 1)])[0]
        tok_S_r[ch] = tkE
        tok_PT_w[ch] = tkE
        npair = 2 * tb + 2
        if pi >= npair - 2:
            i0 = u0 - 4 * tb
            p.wait("vector", tkE, tk_consts["masks"])
            p.op("vector", lambda e, ch=ch, i0=i0: e.tensor_mul(
                PT[ch][:, 0:512], PT[ch][:, 0:512],
                t_masks[:, i0 * 512:(i0 + 1) * 512]))
            tkM = p.op("vector", lambda e, ch=ch, i0=i0: e.tensor_mul(
                PT[ch][:, 512:1024], PT[ch][:, 512:1024],
                t_masks[:, (i0 + 1) * 512:(i0 + 2) * 512]), inc=[(Eve, 1)])[0]
            tok_PT_w[ch] = tkM

    def emit_PV(ch, pi, tb):
        u0, u1 = 2 * pi, 2 * pi + 1
        npair = 2 * tb + 2
        first_pv = (pi == 0)
        last = (pi == npair - 1)
        p.wait("tensor", tok_PT_w[ch], tok_kv_ready[tb], tk_consts["onespv"])
        if first_pv:
            if tok_ctx_free[ch] is not None:
                p.wait("tensor", tok_ctx_free[ch])
            p.wait("tensor", tok_rs_free[0], tok_rs_free[1])
        p.op("tensor", lambda e, u0=u0, ch=ch, fp=first_pv: e.matmul(
            ctx_ap[ch][:], V_sb[:, u0 * 128:(u0 + 1) * 128], PT[ch][:, 0:512],
            start=fp, stop=False))
        p.op("tensor", lambda e, u1=u1, ch=ch, last=last: e.matmul(
            ctx_ap[ch][:], V_sb[:, u1 * 128:(u1 + 1) * 128], PT[ch][:, 512:1024],
            start=False, stop=last))
        p.op("tensor", lambda e, ch=ch, fp=first_pv: e.matmul(
            rs_ap[ch], t_onespv[:], PT[ch][:, 0:512], start=fp, stop=False))
        tkP = p.op("tensor", lambda e, ch=ch, last=last: e.matmul(
            rs_ap[ch], t_onespv[:], PT[ch][:, 512:1024],
            start=False, stop=last), inc=[(Epe, 1)])[0]
        tok_PT_r[ch] = tkP
        return tkP

    def emit_epilogue(ch, qh, tb):
        nonlocal tok_a2a_last
        p.wait("vector", tok_PT_r[ch])
        if tok_rrow[ch] is not None:
            p.wait("vector", tok_rrow[ch])
        tkv = p.op("vector", lambda e, ch=ch: e.tensor_single_scalar(
            t_rrow[0:1, ch * 512:(ch + 1) * 512], rs_ap[ch], EPS_RS,
            ALU.add), inc=[(Eve, 1)])[0]
        tok_rs_free[ch] = tkv
        p.wait("vector", tkv)
        tkv = p.op("vector", lambda e, ch=ch: e.reciprocal(
            t_rrow[0:1, ch * 512:(ch + 1) * 512],
            t_rrow[0:1, ch * 512:(ch + 1) * 512]), inc=[(Eve, 1)])[0]
        p.wait("tensor", tkv, tk_consts["ones1"])
        if deq_state["tok_Bcons"] is not None:
            p.wait("tensor", deq_state["tok_Bcons"])
        tkR = p.op("tensor", lambda e, ch=ch: e.matmul(
            pbC[:, 0:512], t_ones1[:],
            t_rrow[0:1, ch * 512:(ch + 1) * 512], start=True, stop=True),
            inc=[(Epe, 1)])[0]
        tok_rrow[ch] = tkR
        deq_state["tok_B"] = tkR
        p.wait("vector", tkR)
        tkv = p.op("vector", lambda e, ch=ch: e.tensor_copy(R_sb[ch][:],
                                                            pbC[:, 0:512]),
                   inc=[(Eve, 1)])[0]
        deq_state["tok_Bcons"] = tkv
        p.wait("vector", tkv)
        if tok_ctxst[ch] is not None:
            p.wait("vector", tok_ctxst[ch])
        tkv = p.op("vector", lambda e, ch=ch: e.tensor_mul(
            ctx_st[ch][:], ctx_ap[ch][:], R_sb[ch][:]), inc=[(Eve, 1)])[0]
        tok_ctx_free[ch] = tkv
        p.wait("gpsimd", tkv)
        dd, chh = tb // 2, tb % 2
        tok_ctxst[ch] = dma(
            "gpsimd", a2a_in[dd * 512 + qh * 128:dd * 512 + (qh + 1) * 128,
                             chh * 512:(chh + 1) * 512], ctx_st[ch][:], Wctx[ch])
        tok_a2a_last = tok_ctxst[ch]

    def emit_attn_pairgroup(hp, tb):
        nonlocal tok_attn_mm
        if tb == 0 and hp == 0:
            p.wait("tensor", tok_p0[0], tok_p0[1])
        npair = 2 * tb + 2
        lds = [None, None]
        for ch in range(2):
            qh = 2 * hp + ch
            if tok_qt[ch] is not None:
                p.wait("sync", tok_qt[ch])
            p.wait("sync", tok_qscr[(qh, tb)])
            lds[ch] = dma("sync", q_t[ch][:], q_scr[qh, :, tb * 512:(tb + 1) * 512],
                          Lqt[ch])
        emit_S(0, 0, tb, lds[0])
        emit_S(1, 0, tb, lds[1])
        for pi in range(1, npair):
            emit_PV(0, pi - 1, tb)
            emit_S(0, pi, tb, lds[0])
            emit_PV(1, pi - 1, tb)
            emit_S(1, pi, tb, lds[1])
            if pi % 2 == 1:
                emit_deq_chunk()
        tokA = emit_PV(0, npair - 1, tb)
        tokB = emit_PV(1, npair - 1, tb)
        tok_qt[0] = tokA
        tok_qt[1] = tokB
        tok_attn_mm = tokB
        emit_epilogue(0, 2 * hp, tb)
        emit_epilogue(1, 2 * hp + 1, tb)
        emit_deq_chunk()

    for tt in range(NTT):
        tok_evac_last = None
        for s4 in range(4):
            sub = tt * 4 + s4
            par = sub % 2
            ssl = sub % 8
            stc = t_stat[:, ssl * 4:(ssl + 1) * 4]
            if tok_xtm[par] is not None:
                p.wait("sync", tok_xtm[par])
            ldx = dma("sync", x_tm[par][:], d_x_all[sub * 128:(sub + 1) * 128, :],
                      Lx[par])
            p.wait("scalar", ldx, tok_cpool)
            if tok_stat[ssl] is not None:
                p.wait("scalar", tok_stat[ssl])
            if tok_xntm_r[par] is not None:
                p.wait("scalar", tok_xntm_r[par])
            tki = p.op("scalar", lambda e, par=par, stc=stc: e.activation(
                xn_tm[par][:], x_tm[par][:], AF.Identity, bias=0.0, scale=1.0,
                accum_out=stc[:, 0:1]), inc=[(Esc, 1)])[0]
            p.wait("scalar", tki)
            tok_dump[par] = tks = p.op("scalar", lambda e, par=par, stc=stc:
                                       e.activation(
                xn_tm[par][:], x_tm[par][:], AF.Square, bias=0.0, scale=1.0,
                accum_out=stc[:, 1:2]), inc=[(Esc, 1)])[0]
            p.wait("vector", tks)
            p.op("vector", lambda e, stc=stc: e.tensor_single_scalar(
                stc[:, 2:3], stc[:, 0:1], -1.0 / DM, ALU.mult))
            tkv = p.op("vector", lambda e, stc=stc: e.tensor_single_scalar(
                stc[:, 1:2], stc[:, 1:2], 1.0 / DM, ALU.mult), inc=[(Eve, 1)])[0]
            p.wait("vector", tkv)
            tkm = p.op("vector", lambda e, stc=stc: e.tensor_mul(
                stc[:, 0:1], stc[:, 2:3], stc[:, 2:3]), inc=[(Eve, 1)])[0]
            p.wait("vector", tkm)
            tkv = p.op("vector", lambda e, stc=stc: e.tensor_sub(
                stc[:, 1:2], stc[:, 1:2], stc[:, 0:1]), inc=[(Eve, 1)])[0]
            p.wait("scalar", tkv)
            tkl = p.op("scalar", lambda e, stc=stc: e.activation(
                stc[:, 1:2], stc[:, 1:2], AF.Ln, bias=EPS_LN, scale=1.0),
                inc=[(Esc, 1)])[0]
            p.wait("scalar", tkl)
            tks2 = p.op("scalar", lambda e, stc=stc: e.activation(
                stc[:, 3:4], stc[:, 1:2], AF.Exp, bias=0.0, scale=-0.5),
                inc=[(Esc, 1)])[0]
            p.wait("vector", tks2, tok_dump[par])
            tkv = p.op("vector", lambda e, par=par, stc=stc: e.tensor_scalar(
                xn_tm[par][:], x_tm[par][:], stc[:, 2:3], stc[:, 3:4],
                op0=ALU.add, op1=ALU.mult), inc=[(Eve, 1)])[0]
            tok_xtm[par] = tkv
            tok_stat[ssl] = tkv
            tok_xntm_w[par] = tkv
            p.wait("tensor", tkv)
            if s4 == 0 and tok_xn_cons is not None:
                p.wait("tensor", tok_xn_cons)
            for grp in range(4):
                if deq_state["tok_Bcons"] is not None:
                    p.wait("tensor", deq_state["tok_Bcons"])
                for j in range(8):
                    ib = grp * 8 + j
                    tktr = p.op("tensor", lambda e, par=par, ib=ib, j=j:
                                e.transpose(
                        ptr[:, j * 128:(j + 1) * 128],
                        xn_tm[par][:, ib * 128:(ib + 1) * 128], t_ident[:]),
                        inc=[(Epe, 1)] if j == 7 else None)
                deq_state["tok_B"] = tktr[0]
                p.wait("vector", tktr[0])
                tok_evac_last = p.op("vector", lambda e, s4=s4, grp=grp:
                                     e.tensor_copy(
                    xn_fm3[:, grp * 8:(grp + 1) * 8, s4 * 128:(s4 + 1) * 128],
                    ptr[:, 0:1024]), inc=[(Eve, 1)])[0]
                deq_state["tok_Bcons"] = tok_evac_last
            tok_xntm_r[par] = tktr[0]

        parc = tt % 2
        if tok_cos[parc] is not None:
            p.wait("scalar", tok_cos[parc])
        ldc = dma("scalar", cos_sl[parc][:], d_cos[:, tt * 512:(tt + 1) * 512],
                  Lcs[parc])
        lds = dma("scalar", sin_sl[parc][:], d_sin[:, tt * 512:(tt + 1) * 512],
                  Lcs[parc])

        p.wait("tensor", tok_evac_last)
        if tok_q_free is not None:
            p.wait("tensor", tok_q_free)
        for jc in range(4):
            for ib in range(KI):
                tkq = p.op("tensor", lambda e, jc=jc, ib=ib: e.matmul(
                    qb_ap[jc], wq_sb[:, (ib * 4 + jc) * 128:(ib * 4 + jc + 1) * 128],
                    xn_fm[:, ib * 512:(ib + 1) * 512],
                    start=(ib == 0), stop=(ib == KI - 1)),
                    inc=[(Epe, 1)] if (ib == KI - 1) else None)
        tok_proj_q = tkq[0]
        if tok_k_free is not None:
            p.wait("tensor", tok_k_free)
        for ib in range(KI):
            tkk = p.op("tensor", lambda e, ib=ib: e.matmul(
                pb4[:], wkv_sb[:, ib * 256:ib * 256 + 128],
                xn_fm[:, ib * 512:(ib + 1) * 512],
                start=(ib == 0), stop=(ib == KI - 1)),
                inc=[(Epe, 1)] if (ib == KI - 1) else None)
        tok_proj_k = tkk[0]
        if tok_v_free is not None:
            p.wait("tensor", tok_v_free)
        for ib in range(KI):
            tkv2 = p.op("tensor", lambda e, ib=ib: e.matmul(
                pb5[:], wkv_sb[:, ib * 256 + 128:ib * 256 + 256],
                xn_fm[:, ib * 512:(ib + 1) * 512],
                start=(ib == 0), stop=(ib == KI - 1)),
                inc=[(Epe, 1)] if (ib == KI - 1) else None)
        tok_proj_v = tkv2[0]
        tok_xn_cons = tok_proj_v

        p.wait("vector", tok_proj_q, ldc, lds)
        for jc in range(4):
            parq = jc % 2
            if tok_stq[parq] is not None:
                p.wait("vector", tok_stq[parq])
            if tok_ropeadd is not None:
                p.wait("vector", tok_ropeadd)
            p.op("vector", lambda e, jc=jc, parc=parc: e.tensor_mul(
                tmpA[:], qb_ap[jc], cos_sl[parc][:]))
            p.op("vector", lambda e, jc=jc, parc=parc: e.tensor_mul(
                tmpB[0:64, :], qb_ap[jc][64:128, :], sin_sl[parc][0:64, :]))
            tkr = p.op("vector", lambda e, jc=jc, parc=parc: e.tensor_mul(
                tmpB[64:128, :], qb_ap[jc][0:64, :], sin_sl[parc][64:128, :]),
                inc=[(Eve, 1)])[0]
            p.wait("vector", tkr)
            tkr = p.op("vector", lambda e, parq=parq: e.tensor_add(
                st_q[parq][:], tmpA[:], tmpB[:]), inc=[(Eve, 1)])[0]
            tok_ropeadd = tkr
            p.wait("gpsimd", tkr)
            tok_stq[parq] = dma("gpsimd", q_scr[jc, :, tt * 512:(tt + 1) * 512],
                                st_q[parq][:], Wq[parq])
            tok_qscr[(jc, tt)] = tok_stq[parq]
            if jc == 3:
                tok_q_free = tkr
        p.wait("vector", tok_proj_k, tok_ropeadd)
        if tok_stk is not None:
            p.wait("vector", tok_stk)
        p.op("vector", lambda e, parc=parc: e.tensor_mul(tmpA[:], pb4[:], cos_sl[parc][:]))
        p.op("vector", lambda e, parc=parc: e.tensor_mul(
            tmpB[0:64, :], pb4[64:128, :], sin_sl[parc][0:64, :]))
        tkr = p.op("vector", lambda e, parc=parc: e.tensor_mul(
            tmpB[64:128, :], pb4[0:64, :], sin_sl[parc][64:128, :]), inc=[(Eve, 1)])[0]
        p.wait("vector", tkr)
        tkr = p.op("vector", lambda e, tt=tt: e.tensor_add(
            K_sb[:, tt * 512:(tt + 1) * 512], tmpA[:], tmpB[:]), inc=[(Eve, 1)])[0]
        tok_ropeadd = tkr
        tok_k_free = tkr
        tok_cos[parc] = tkr
        p.wait("vector", tok_proj_v)
        if tok_stv is not None:
            p.wait("vector", tok_stv)
        tkv3 = p.op("vector", lambda e: e.tensor_copy(st_v[:], pb5[:]), inc=[(Eve, 1)])[0]
        tok_v_free = tkv3
        p.wait("tensor", tkv3)
        if deq_state["tok_Bcons"] is not None:
            p.wait("tensor", deq_state["tok_Bcons"])
        for a in range(4):
            tktr = p.op("tensor", lambda e, a=a: e.transpose(
                ptr[:, a * 128:(a + 1) * 128], st_v[:, a * 128:(a + 1) * 128],
                t_ident[:]), inc=[(Epe, 1)] if a == 3 else None)
        deq_state["tok_B"] = tktr[0]
        p.wait("vector", tktr[0])
        tkv3 = p.op("vector", lambda e, tt=tt: e.tensor_copy(
            V_sb[:, tt * 512:(tt + 1) * 512], ptr[:, 0:512]), inc=[(Eve, 1)])[0]
        deq_state["tok_Bcons"] = tkv3
        tok_kv_ready[tt] = tkv3
        for hp in range(2):
            emit_attn_pairgroup(hp, tt)

    while deq_state["idx"] < len(deq_jobs):
        emit_deq_chunk()

    p.wait("gpsimd", (Wctx[0], Wctx[0].n), (Wctx[1], Wctx[1].n))

    def do_a2a(e):
        with allow_ici_all_to_all():
            return e.collective_compute(
                "AllToAll", ALU.bypass, replica_groups=[list(range(NCORE))],
                ins=[a2a_in.ap().opt()], outs=[a2a_out.ap().opt()])

    tok_cc_ctx = p.op("gpsimd", do_a2a, inc=[(Ccc, 1)])[0]

    xn_h = _carve(AR_XN, 16 * 1024, BF16, (128, 32 * 512))
    xn_h3 = xn_h.rearrange("p (ib c) -> p ib c", ib=32)
    h_act = _carve(AR_BIG, 0, BF16, (128, 43 * 512))
    w1t_sb = [_carve(AR_W, i * 8 * 1024, BF16, (128, 4096)) for i in range(2)]
    w3t_sb = [_carve(AR_W, 16 * 1024 + i * 8 * 1024, BF16, (128, 4096)) for i in range(2)]
    w2t_sb = [_carve(AR_X, i * 11008, BF16, (128, 43 * 128)) for i in range(2)]
    ha_st = [_carve(AR_W, 32 * 1024 + i * 1024, BF16, (128, 512)) for i in range(2)]
    g_st = [_carve(AR_W, 34 * 1024 + i * 1024, BF16, (128, 512)) for i in range(2)]
    gp_st = [_carve(AR_W, 36 * 1024 + i * 2048, F32, (128, 512)) for i in range(2)]
    sum_st = [_carve(AR_W, 40 * 1024 + i * 2048, F32, (128, 512)) for i in range(2)]

    tok_ag_w1 = (Ccc, 1)
    tok_ag_w3 = (Ccc, 2)
    tok_ag_w2 = (Ccc, 3)
    tok_ag_wo = (Ccc, 4)
    tok_cc_all = (Ccc, 5)

    tok_w1sb = [None, None]
    tok_hact_r = None
    tok_ab_free = [None, None]
    tok_g_free = [None, None]
    tok_ha = [None, None]
    tok_gst = [None, None]
    tok_gpst = [None, None]
    tok_sumst = [None, None]
    tok_w2sb = [None, None]
    tok_gpart = [None] * 32
    tok_gscr = {}
    tok_xnh_cons = None
    p.wait("vector", tok_qt[0], tok_qt[1], tok_ctx_free[0], tok_ctx_free[1],
           tok_rrow[0], tok_rrow[1], tok_ctxst[0], tok_ctxst[1],
           (Wctx[0], Wctx[0].n), (Wctx[1], Wctx[1].n))
    p.wait("tensor", tok_S_r[0], tok_S_r[1], tok_qt[0], tok_qt[1])
    if deq_state["tok_Bcons"] is not None:
        p.wait("tensor", deq_state["tok_Bcons"])

    for tp in range(2):
        tok_apply = None
        for s4 in range(4):
            par = s4 % 2
            ssl = s4 % 8
            stc = t_stat[:, ssl * 4:(ssl + 1) * 4]
            if tok_xtm[par] is not None:
                p.wait("sync", tok_xtm[par])
            if s4 == 0 and tok_xnh_cons is not None:
                p.wait("sync", tok_xnh_cons)
            if s4 == 0 and tp == 0:
                p.wait("tensor", tok_xn_cons)
            ldx = dma("sync", x_tm[par][:],
                      d_x_own[tp * 512 + s4 * 128:tp * 512 + (s4 + 1) * 128, :],
                      Lx[par])
            p.wait("vector", ldx)
            if tok_bn[par] is not None:
                p.wait("vector", tok_bn[par])
            if tok_stat[ssl] is not None:
                p.wait("vector", tok_stat[ssl])
            for j in range(8):
                toks = p.op("vector", lambda e, par=par, j=j: e.bn_stats(
                    t_bn[:, par * 48 + j * 6:par * 48 + (j + 1) * 6],
                    x_tm[par][:, j * 512:(j + 1) * 512]),
                    inc=[(Eve, 1)] if j == 7 else None)
            p.wait("vector", toks[0])
            tkagg = p.op("vector", lambda e, par=par, stc=stc: e.bn_aggr(
                stc[:, 0:2], t_bn[:, par * 48:par * 48 + 48]), inc=[(Eve, 1)])[0]
            tok_bn[par] = tkagg
            p.wait("vector", tkagg)
            tkv = p.op("vector", lambda e, stc=stc: e.tensor_single_scalar(
                stc[:, 2:3], stc[:, 0:1], -1.0, ALU.mult), inc=[(Eve, 1)])[0]
            p.wait("scalar", tkv)
            tkl = p.op("scalar", lambda e, stc=stc: e.activation(
                stc[:, 1:2], stc[:, 1:2], AF.Ln, bias=EPS_LN, scale=1.0),
                inc=[(Esc, 1)])[0]
            p.wait("scalar", tkl)
            tks2 = p.op("scalar", lambda e, stc=stc: e.activation(
                stc[:, 3:4], stc[:, 1:2], AF.Exp, bias=0.0, scale=-0.5),
                inc=[(Esc, 1)])[0]
            p.wait("vector", tks2, tok_dump[par])
            tkv = p.op("vector", lambda e, par=par, stc=stc: e.tensor_scalar(
                xn_tm[par][:], x_tm[par][:], stc[:, 2:3], stc[:, 3:4],
                op0=ALU.add, op1=ALU.mult), inc=[(Eve, 1)])[0]
            tok_xtm[par] = tkv
            tok_stat[ssl] = tkv
            p.wait("tensor", tkv)
            if s4 == 0 and tok_xnh_cons is not None:
                p.wait("tensor", tok_xnh_cons)
            if s4 == 0 and tp == 0:
                if deq_state["tok_Bcons"] is not None:
                    p.wait("tensor", deq_state["tok_Bcons"])
            for grp in range(4):
                if tok_ptr_r[0] is not None:
                    p.wait("tensor", tok_ptr_r[0])
                for j in range(8):
                    ib = grp * 8 + j
                    tktr = p.op("tensor", lambda e, par=par, ib=ib, j=j:
                                e.transpose(
                        ptr[:, j * 128:(j + 1) * 128],
                        xn_tm[par][:, ib * 128:(ib + 1) * 128], t_ident[:]),
                        inc=[(Epe, 1)] if j == 7 else None)
                tok_ptr_w[0] = tktr[0]
                p.wait("vector", tktr[0])
                tok_apply = p.op("vector", lambda e, s4=s4, grp=grp:
                                 e.tensor_copy(
                    xn_h3[:, grp * 8:(grp + 1) * 8, s4 * 128:(s4 + 1) * 128],
                    ptr[:, 0:1024]), inc=[(Eve, 1)])[0]
                tok_ptr_r[0] = tok_apply
            tok_xntm_r[par] = tok_ptr_w[0]
        tok_xnh_ready = tok_apply

        ab_ap = [(pbA[:, 0:512], pbB[:, 0:512]), (pbA[:, 512:1024], pbB[:, 512:1024])]
        g_ap = [pb4[:], pb5[:]]
        for hhalf in range(2):
            for hh in range(43):
                hb = hhalf * 43 + hh
                par = hh % 2
                if tok_w1sb[par] is not None:
                    p.wait("sync", tok_w1sb[par])
                p.wait("sync", tok_ag_w1, tok_ag_w3)
                ld1 = dma("sync", w1t_sb[par][:], w1t_out[hb, :, :], Lw1[par])
                ld3 = dma("sync", w3t_sb[par][:], w3t_out[hb, :, :], Lw1[par])
                a_ap, b_ap = ab_ap[par]
                p.wait("tensor", ld1, ld3, tok_xnh_ready)
                if tok_ab_free[par] is not None:
                    p.wait("tensor", tok_ab_free[par])
                if hb == 0 and tp == 0:
                    p.wait("tensor", tok_S_r[0])
                for ib in range(KI):
                    p.op("tensor", lambda e, par=par, ib=ib, a_ap=a_ap: e.matmul(
                        a_ap, w1t_sb[par][:, ib * 128:(ib + 1) * 128],
                        xn_h[:, ib * 512:(ib + 1) * 512],
                        start=(ib == 0), stop=(ib == KI - 1)))
                for ib in range(KI):
                    tkb = p.op("tensor", lambda e, par=par, ib=ib, b_ap=b_ap: e.matmul(
                        b_ap, w3t_sb[par][:, ib * 128:(ib + 1) * 128],
                        xn_h[:, ib * 512:(ib + 1) * 512],
                        start=(ib == 0), stop=(ib == KI - 1)),
                        inc=[(Epe, 1)] if ib == KI - 1 else None)
                tok_w1sb[par] = tkb[0]
                p.wait("scalar", tkb[0])
                if tok_ha[par] is not None:
                    p.wait("scalar", tok_ha[par])
                tks = p.op("scalar", lambda e, par=par, a_ap=a_ap: e.activation(
                    ha_st[par][:], a_ap, AF.Silu, bias=0.0, scale=1.0),
                    inc=[(Esc, 1)])[0]
                p.wait("vector", tks)
                if tok_hact_r is not None and hh < 2:
                    p.wait("vector", tok_hact_r)
                tkv = p.op("vector", lambda e, par=par, hh=hh, b_ap=b_ap: e.tensor_mul(
                    h_act[:, hh * 512:(hh + 1) * 512], ha_st[par][:], b_ap),
                    inc=[(Eve, 1)])[0]
                tok_ha[par] = tkv
                tok_ab_free[par] = tkv
            tok_hact_ready = tkv

            for jb in range(32):
                par = jb % 2
                p.wait("sync", tok_ag_w2)
                if jb == 0:
                    p.wait("sync", tok_xtm[0], tok_xtm[1])
                if tok_w2sb[par] is not None:
                    p.wait("sync", tok_w2sb[par])
                ldw2 = dma("sync", w2t_sb[par][:],
                           w2t_out[jb, :, hhalf * 43 * 128:(hhalf * 43 + 43) * 128],
                           Lw2[par])
                p.wait("tensor", ldw2, tok_hact_ready)
                if tok_g_free[par] is not None:
                    p.wait("tensor", tok_g_free[par])
                if jb < 2 and tp == 0 and hhalf == 0:
                    p.wait("tensor", tok_ctx_free[0], tok_rs_free[0], tok_ctx_free[1], tok_rs_free[1])
                for kq in range(43):
                    tk2 = p.op("tensor", lambda e, par=par, kq=kq: e.matmul(
                        g_ap[par], w2t_sb[par][:, kq * 128:(kq + 1) * 128],
                        h_act[:, kq * 512:(kq + 1) * 512],
                        start=(kq == 0), stop=(kq == 42)),
                        inc=[(Epe, 1)] if kq == 42 else None)
                tok_w2sb[par] = tk2[0]
                tok_hact_r = tk2[0]
                if hhalf == 0:
                    p.wait("vector", tk2[0])
                    if tok_gpst[par] is not None:
                        p.wait("vector", tok_gpst[par])
                    if tok_gpart[jb] is not None:
                        p.wait("vector", tok_gpart[jb])
                    tkv = p.op("vector", lambda e, par=par: e.tensor_copy(
                        gp_st[par][:], g_ap[par]), inc=[(Eve, 1)])[0]
                    tok_g_free[par] = tkv
                    p.wait("gpsimd", tkv)
                    tok_gpst[par] = dma("gpsimd", gpart_scr[jb * 128:(jb + 1) * 128, :],
                                        gp_st[par][:], Wgp[par])
                    tok_gpart[jb] = tok_gpst[par]
                else:
                    if tok_sumst[par] is not None:
                        p.wait("scalar", tok_sumst[par])
                    p.wait("scalar", tok_gpart[jb], tok_gpst[par])
                    ldgp = dma("scalar", gp_st[par][:],
                               gpart_scr[jb * 128:(jb + 1) * 128, :], Lgp[par])
                    p.wait("vector", tk2[0], ldgp)
                    if tok_sumst[par] is not None:
                        p.wait("vector", tok_sumst[par])
                    tkv = p.op("vector", lambda e, par=par: e.tensor_add(
                        sum_st[par][:], gp_st[par][:], g_ap[par]), inc=[(Eve, 1)])[0]
                    tok_g_free[par] = tkv
                    tok_gpart[jb] = tkv
                    p.wait("scalar", tkv)
                    if tok_gst[par] is not None:
                        p.wait("scalar", tok_gst[par])
                    tks = p.op("scalar", lambda e, par=par: e.activation(
                        g_st[par][:], sum_st[par][:], AF.Silu, bias=0.0, scale=1.0),
                        inc=[(Esc, 1)])[0]
                    tok_sumst[par] = tks
                    p.wait("gpsimd", tks)
                    tok_gst[par] = dma("gpsimd", g_scr[jb * 128:(jb + 1) * 128,
                                                       tp * 512:(tp + 1) * 512],
                                       g_st[par][:], Wg[par])
                    tok_gscr[(jb, tp)] = tok_gst[par]
        tok_xnh_cons = tok_hact_r

    ctx_t = [_carve(AR_BIG, 0 + i * 1024, BF16, (128, 512)) for i in range(3)]
    wo_t = [_carve(AR_BIG, 3072 + i * 1536, BF16, (128, 768)) for i in range(3)]
    gg_t = [_carve(AR_BIG, 7680 + i * 1024, BF16, (128, 512)) for i in range(2)]
    sg_st = [_carve(AR_BIG, 9728 + i * 1024, BF16, (128, 512)) for i in range(2)]

    jgroups = [list(range(6 * i, min(6 * i + 6, 32))) for i in range(6)]
    sbanks = [pbA[:, 0:512], pbA[:, 512:1024], pbB[:, 0:512], pbB[:, 512:1024],
              pb4[:], pb5[:]]
    tok_sbank_free = [None] * 6
    tok_ctxt = [None, None, None]
    tok_wot = [None, None, None]
    tok_ggt = [None, None]
    tok_sgst = [None, None]
    tok_sg_last = None
    phase_gate()
    p.wait("sync", tok_cc_ctx, tok_ag_wo)

    for th in range(2):
        for gi, jcs in enumerate(jgroups):
            for ib in range(KI):
                par = ib % 3
                if tok_ctxt[par] is not None:
                    p.wait("sync", tok_ctxt[par])
                ldc = dma("sync", ctx_t[par][:],
                          a2a_out[ib * 128:(ib + 1) * 128, th * 512:(th + 1) * 512],
                          Lctx[par])
                if tok_wot[par] is not None:
                    p.wait("scalar", tok_wot[par])
                ldw = dma("scalar", wo_t[par][:, 0:len(jcs) * 128],
                          wo_out[ib, :, jcs[0] * 128:(jcs[0] + len(jcs)) * 128],
                          Lwo[par])
                p.wait("tensor", ldc, ldw)
                if ib == 0:
                    for idx in range(len(jcs)):
                        if tok_sbank_free[idx] is not None:
                            p.wait("tensor", tok_sbank_free[idx])
                    if th == 0 and gi == 0:
                        p.wait("tensor", tok_g_free[0], tok_g_free[1],
                               tok_ab_free[0], tok_ab_free[1])
                for idx in range(len(jcs)):
                    tko = p.op("tensor", lambda e, idx=idx, par=par, ib=ib: e.matmul(
                        sbanks[idx], wo_t[par][:, idx * 128:(idx + 1) * 128],
                        ctx_t[par][:], start=(ib == 0), stop=(ib == KI - 1)),
                        inc=[(Epe, 1)] if (idx == len(jcs) - 1) else None)
                tok_ctxt[par] = tko[0]
                tok_wot[par] = tko[0]
            tok_omm = tko[0]
            for idx, jc in enumerate(jcs):
                par = (gi * 6 + idx) % 2
                if tok_ggt[par] is not None:
                    p.wait("scalar", tok_ggt[par])
                ldg = dma("scalar", gg_t[par][:],
                          g_scr[jc * 128:(jc + 1) * 128, th * 512:(th + 1) * 512],
                          Lgg[par])
                p.wait("vector", tok_omm, ldg)
                if tok_sgst[par] is not None:
                    p.wait("vector", tok_sgst[par])
                tkv = p.op("vector", lambda e, idx=idx, par=par: e.tensor_mul(
                    sg_st[par][:], sbanks[idx], gg_t[par][:]), inc=[(Eve, 1)])[0]
                tok_sbank_free[idx] = tkv
                tok_ggt[par] = tkv
                p.wait("gpsimd", tkv)
                tok_sgst[par] = dma("gpsimd", sg_scr[jc * 128:(jc + 1) * 128,
                                                     th * 512:(th + 1) * 512],
                                    sg_st[par][:], Wsg[par])
                tok_sg_last = tok_sgst[par]

    sgt_t = [_carve(AR_SM, 17408 + i * 256, BF16, (128, 128)) for i in range(2)]
    out_st = [_carve(AR_BIG, 11776 + i * 16 * 1024, F32, (128, 4096)) for i in range(2)]
    tok_out = [None, None]
    tok_sgt = [None, None]
    tok_outwr = None
    phase_gate()

    for ts in range(8):
        par = ts % 2
        if tok_xtm[par] is not None:
            p.wait("sync", tok_xtm[par])
        ldx = dma("sync", x_tm[par][:], d_x_own[ts * 128:(ts + 1) * 128, :], Lx[par])
        if tok_out[par] is not None:
            p.wait("vector", tok_out[par])
        for grp in range(4):
            if tok_ptr_r[0] is not None:
                p.wait("tensor", tok_ptr_r[0])
            for j in range(8):
                jc = grp * 8 + j
                parj = jc % 2
                if tok_sgt[parj] is not None:
                    p.wait("scalar", tok_sgt[parj])
                ldsg = dma("scalar", sgt_t[parj][:],
                           sg_scr[jc * 128:(jc + 1) * 128, ts * 128:(ts + 1) * 128],
                           Lsg[parj])
                p.wait("tensor", ldsg)
                tktr = p.op("tensor", lambda e, j=j, parj=parj: e.transpose(
                    ptr[:, j * 128:(j + 1) * 128],
                    sgt_t[parj][:], t_ident[:]),
                    inc=[(Epe, 1)])
                tok_sgt[parj] = tktr[0]
            p.wait("vector", tktr[0], ldx)
            tkv = p.op("vector", lambda e, par=par, grp=grp: e.tensor_add(
                out_st[par][:, grp * 1024:(grp + 1) * 1024],
                ptr[:, 0:1024],
                x_tm[par][:, grp * 1024:(grp + 1) * 1024]), inc=[(Eve, 1)])[0]
            tok_ptr_r[0] = tkv
        tok_xtm[par] = tkv
        p.wait("gpsimd", tkv)
        tok_outwr = dma("gpsimd", d_out[ts * 128:(ts + 1) * 128, :], out_st[par][:],
                        Wout[par])
        tok_out[par] = tok_outwr

    tap_tok = None
    for t, dstt in taps.items():
        p.wait("gpsimd", tok_outwr)
        if t in ("a2a_out",):
            p.wait("gpsimd", tok_cc_ctx)
        tap_tok = dma("gpsimd", dstt.ap(), scr_map[t].ap(), Wtap)

    p.wait("gpsimd", tok_outwr)
    if tap_tok is not None:
        p.wait("gpsimd", tap_tok)

    p.finalize()
    es.close()
    return nc



_NC_CACHE = {}


def kernel(**inputs):
    shared, per_core = _host_prep(inputs)
    key = ()
    if key not in _NC_CACHE:
        _NC_CACHE[key] = build_program()
    nc = _NC_CACHE[key]
    in_maps = []
    for c in range(NCORE):
        m = dict(shared)
        m.update(per_core[c])
        in_maps.append(m)
    res = run_bass_kernel_spmd(nc, in_maps, core_ids=list(range(NCORE)))
    out = np.concatenate([res.results[c]["out_own"] for c in range(NCORE)], axis=0)
    return np.ascontiguousarray(out[None, ...])


# version 09
# speedup vs baseline: 1.0501x; 1.0501x over previous
import math
import numpy as np
import ml_dtypes
import concourse.bass as bass
import concourse.mybir as mybir
from concourse.bass_utils import run_bass_kernel_spmd
from concourse.aot_env import allow_ici_all_to_all
from contextlib import ExitStack

F32 = mybir.dt.float32
BF16 = mybir.dt.bfloat16
I8 = mybir.dt.int8
AF = mybir.ActivationFunctionType
ALU = mybir.AluOpType
BF = ml_dtypes.bfloat16

S, DM, HF = 8192, 4096, 11008
NCORE = 8
TOWN = S // NCORE
KI = DM // 128
HB = HF // 128
HBP = 88
ISQ = 1.0 / math.sqrt(128.0)
CSHIFT = 25.0
EPS_LN = 1e-5
EPS_RS = 1e-30
NTT = S // 512
W1SLOT = HBP // NCORE
W2SLOT = 32 // NCORE

ENGS = ("sync", "tensor", "vector", "scalar", "gpsimd")
CVALS = [0.0, 1.0, -0.5, EPS_LN, ISQ, -CSHIFT, -1.0 / DM, 1.0 / DM]



def _payload_bytes(w):
    w = np.ascontiguousarray(np.asarray(w))
    return np.ascontiguousarray(w.view(np.int8).reshape(w.shape + (4,))[..., 0])


def _host_prep(inputs):
    x = np.ascontiguousarray(np.asarray(inputs["L0_attn_hidden_states"])[0])
    qb_q = _payload_bytes(inputs["L0_attn_qw_q"])
    qb_k = _payload_bytes(inputs["L0_attn_qw_k"])
    qb_v = _payload_bytes(inputs["L0_attn_qw_v"])
    qb_o = _payload_bytes(inputs["L0_attn_qw_o"])
    sc_q = np.ascontiguousarray(np.asarray(inputs["L0_attn_scales_q"], dtype=np.float32))
    sc_k = np.ascontiguousarray(np.asarray(inputs["L0_attn_scales_k"], dtype=np.float32))
    sc_v = np.ascontiguousarray(np.asarray(inputs["L0_attn_scales_v"], dtype=np.float32))
    sc_o = np.ascontiguousarray(np.asarray(inputs["L0_attn_scales_o"], dtype=np.float32))

    def ffn1_layout(w, s):
        pb = _payload_bytes(w)
        pl = np.ascontiguousarray(
            pb.reshape(HF, 32, 64).reshape(HB, 128, 32, 64).transpose(0, 3, 2, 1))
        sb = np.asarray(s, dtype=np.float32).reshape(HF, 32, 2)
        sb = np.ascontiguousarray(
            sb.reshape(HB, 128, 32, 2).transpose(0, 3, 2, 1))
        return pl, sb

    def ffn2_layout(w, s):
        pb = _payload_bytes(w)
        pl = np.ascontiguousarray(
            pb.reshape(DM, HB, 64).reshape(32, 128, HB, 64).transpose(0, 3, 2, 1))
        sb = np.asarray(s, dtype=np.float32).reshape(DM, HB, 2)
        sb = np.ascontiguousarray(
            sb.reshape(32, 128, HB, 2).transpose(0, 3, 2, 1))
        return pl, sb

    p1, sb1 = ffn1_layout(inputs["L0_ffn_w1"], inputs["L0_ffn_s1"])
    p3, sb3 = ffn1_layout(inputs["L0_ffn_w3"], inputs["L0_ffn_s3"])
    p2, sb2 = ffn2_layout(inputs["L0_ffn_w2"], inputs["L0_ffn_s2"])

    inv = 1.0 / (10000.0 ** (np.arange(0, 128, 2, dtype=np.float64) / 128.0))
    ang = np.arange(S, dtype=np.float64)[None, :] * inv[:, None]
    cosT = np.empty((128, S), np.float32)
    sinT2 = np.empty((128, S), np.float32)
    cosT[:64] = np.cos(ang)
    cosT[64:] = np.cos(ang)
    sinT2[:64] = -np.sin(ang)
    sinT2[64:] = np.sin(ang)

    E = np.zeros((32, 32 * 128), dtype=BF)
    for b in range(32):
        E[b, b * 128:(b + 1) * 128] = 1.0
    sel2 = np.zeros((34, 128), dtype=BF)
    for b0 in (0, 32):
        sel2[b0, :64] = 1.0 / 16.0
        sel2[b0 + 1, 64:] = 1.0
    ones1 = np.ones((1, 128), np.float32)
    onespv = np.ones((128, 1), dtype=BF)
    ident = np.eye(128, dtype=BF)
    masks = np.zeros((128, 4 * 512), dtype=BF)
    pg = np.arange(128)[:, None]
    jg = np.arange(512)[None, :]
    for i in range(4):
        masks[:, i * 512:(i + 1) * 512] = (pg <= jg - 128 * i).astype(BF)

    shared = dict(x_all=x, cosT=cosT, sinT2=sinT2, E=E, sel2=sel2, ones1=ones1,
                  ones1b=np.ones((33, 128), BF), onespv=onespv, ident=ident, masks=masks)
    per_core = []
    for c in range(NCORE):
        qcols = slice(4 * c * 128, 4 * c * 128 + 512)
        kcols = slice(c * 128, (c + 1) * 128)
        d = dict(
            x_own=np.ascontiguousarray(x[c * TOWN:(c + 1) * TOWN]),
            qb_q_own=np.ascontiguousarray(qb_q[:, qcols]),
            sc_q_own=np.ascontiguousarray(sc_q[:, qcols]),
            qb_kv_own=np.ascontiguousarray(
                np.concatenate([qb_k[:, kcols], qb_v[:, kcols]], axis=1)),
            sc_kv_own=np.ascontiguousarray(
                np.concatenate([sc_k[:, kcols], sc_v[:, kcols]], axis=1)),
            qb_o_own=np.ascontiguousarray(qb_o[4 * c * 128:4 * (c + 1) * 128, :]),
            sc_o_own=np.ascontiguousarray(sc_o[4 * c:4 * (c + 1), :]),
            p2_own=np.ascontiguousarray(p2[W2SLOT * c:W2SLOT * (c + 1)]),
            sb2_own=np.ascontiguousarray(sb2[W2SLOT * c:W2SLOT * (c + 1)]),
        )
        p1c = np.zeros((W1SLOT,) + p1.shape[1:], p1.dtype)
        sb1c = np.zeros((W1SLOT,) + sb1.shape[1:], sb1.dtype)
        p3c = np.zeros_like(p1c)
        sb3c = np.zeros_like(sb1c)
        for i, sl in enumerate(range(W1SLOT * c, W1SLOT * (c + 1))):
            if sl < HB:
                p1c[i], sb1c[i] = p1[sl], sb1[sl]
                p3c[i], sb3c[i] = p3[sl], sb3[sl]
        d["p1_own"], d["sb1_own"] = p1c, sb1c
        d["p3_own"], d["sb3_own"] = p3c, sb3c
        per_core.append(d)
    return shared, per_core



class _Sem:
    __slots__ = ("h", "n", "name")

    def __init__(self, h, name):
        self.h, self.n, self.name = h, 0, name


class Plan:
    def __init__(self, nc, es):
        self.nc = nc
        self.es = es
        self.th = {e: [] for e in ENGS}
        self.waited = {e: {} for e in ENGS}
        self.nops = {e: 0 for e in ENGS}
        self.dma_sems = set()
        self.all_dsems = []

    def sem(self, name):
        return _Sem(self.es.enter_context(self.nc.semaphore(name)), name)

    def dsem(self, name):
        s = self.sem(name)
        self.all_dsems.append(s)
        return s

    def op(self, eng, fn, inc=None):
        toks = []
        incs = []
        if inc:
            for s, k in inc:
                s.n += k
                incs.append((s, k))
                toks.append((s, s.n))

        def thunk(e, fn=fn, incs=incs):
            ins = fn(e)
            for s, k in incs:
                ins.then_inc(s.h, k)

        self.th[eng].append(thunk)
        self.nops[eng] += 1
        return toks

    def wait(self, eng, *tokens):
        agg = {}
        for t in tokens:
            if t is None:
                continue
            s, v = t
            if s.name not in agg or agg[s.name][1] < v:
                agg[s.name] = (s, v)
        for s, v in agg.values():
            if v <= self.waited[eng].get(s.name, 0):
                continue
            self.waited[eng][s.name] = v

            def thunk(e, h=s.h, v=v):
                e.wait_ge(h, v)

            self.th[eng].append(thunk)

    def finalize(self):
        with self.nc.Block() as block:
            for name, reg in (("sync", block.sync), ("tensor", block.tensor),
                              ("vector", block.vector), ("scalar", block.scalar),
                              ("gpsimd", block.gpsimd)):
                thunks = self.th[name]

                def runner(e, thunks=thunks):
                    for t in thunks:
                        t(e)

                reg(runner)


def _carve(arena, off_bytes, dt, shape):
    esz = mybir.dt.size(dt)
    per_part = int(np.prod(shape[1:])) * esz
    apx = arena[:, off_bytes:off_bytes + per_part].bitcast(dt)
    if len(shape) > 2:
        dims = " ".join(f"d{i}" for i in range(1, len(shape)))
        apx = apx.rearrange(f"p ({dims}) -> p {dims}",
                            **{f"d{i}": shape[i] for i in range(1, len(shape))})
    if shape[0] != 128:
        apx = apx[0:shape[0]]
    return apx



def build_program(debug_taps=()):
    nc = bass.Bass("TRN2", target_bir_lowering=False)
    es = ExitStack()

    def inp(name, shape, dt):
        return nc.dram_tensor(name, list(shape), dt, kind="ExternalInput")

    d_x_all = inp("x_all", (S, DM), F32)
    d_x_own = inp("x_own", (TOWN, DM), F32)
    d_qbq = inp("qb_q_own", (DM, 512), I8)
    d_scq = inp("sc_q_own", (32, 512), F32)
    d_qbkv = inp("qb_kv_own", (DM, 256), I8)
    d_sckv = inp("sc_kv_own", (32, 256), F32)
    d_qbo = inp("qb_o_own", (512, DM), I8)
    d_sco = inp("sc_o_own", (4, DM), F32)
    d_p1 = inp("p1_own", (W1SLOT, 64, 32, 128), I8)
    d_sb1 = inp("sb1_own", (W1SLOT, 2, 32, 128), F32)
    d_p3 = inp("p3_own", (W1SLOT, 64, 32, 128), I8)
    d_sb3 = inp("sb3_own", (W1SLOT, 2, 32, 128), F32)
    d_p2 = inp("p2_own", (W2SLOT, 64, HB, 128), I8)
    d_sb2 = inp("sb2_own", (W2SLOT, 2, HB, 128), F32)
    d_cos = inp("cosT", (128, S), F32)
    d_sin = inp("sinT2", (128, S), F32)
    d_E = inp("E", (32, 32 * 128), BF16)
    d_sel2 = inp("sel2", (34, 128), BF16)
    d_ones1 = inp("ones1", (1, 128), F32)
    d_ones1b = inp("ones1b", (33, 128), BF16)
    d_onespv = inp("onespv", (128, 1), BF16)
    d_ident = inp("ident", (128, 128), BF16)
    d_masks = inp("masks", (128, 4 * 512), BF16)
    d_out = nc.dram_tensor("out_own", [TOWN, DM], F32, kind="ExternalOutput")

    q_scr = nc.dram_tensor("q_scr", [4, 128, S], BF16)
    w1t_in = nc.dram_tensor("w1t_in", [W1SLOT, 128, 4096], BF16)
    w3t_in = nc.dram_tensor("w3t_in", [W1SLOT, 128, 4096], BF16)
    w2t_in = nc.dram_tensor("w2t_in", [W2SLOT, 128, HF], BF16)
    wo_in = nc.dram_tensor("wo_in", [W2SLOT, 128, 4096], BF16)
    w1t_out = nc.dram_tensor("w1t_out", [HBP, 128, 4096], BF16, addr_space="Shared")
    w3t_out = nc.dram_tensor("w3t_out", [HBP, 128, 4096], BF16, addr_space="Shared")
    w2t_out = nc.dram_tensor("w2t_out", [32, 128, HF], BF16, addr_space="Shared")
    wo_out = nc.dram_tensor("wo_out", [32, 128, 4096], BF16, addr_space="Shared")
    a2a_in = nc.dram_tensor("a2a_in", [DM, TOWN], BF16)
    a2a_out = nc.dram_tensor("a2a_out", [DM, TOWN], BF16)
    g_scr = nc.dram_tensor("g_scr", [DM, TOWN], BF16)
    gpart_scr = nc.dram_tensor("gpart_scr", [DM, 512], F32)
    sg_scr = nc.dram_tensor("sg_scr", [DM, TOWN], BF16)
    scr_map = {"q_scr": q_scr, "a2a_in": a2a_in,
               "a2a_out": a2a_out, "g_scr": g_scr, "sg_scr": sg_scr,
               "w1t_out": w1t_out, "w3t_out": w3t_out, "w2t_out": w2t_out,
               "wo_out": wo_out}
    taps = {t: nc.dram_tensor("tap_" + t, list(scr_map[t].shape), scr_map[t].dtype,
                              kind="ExternalOutput") for t in debug_taps}

    sb = lambda name, shape, dt: es.enter_context(nc.sbuf_tensor(name, list(shape), dt))
    t_sel2 = sb("t_sel2", (34, 128), BF16)
    t_ones1 = sb("t_ones1", (1, 128), F32)
    t_ones1b = sb("t_ones1b", (33, 128), BF16)
    t_onespv = sb("t_onespv", (128, 1), BF16)
    t_ident = sb("t_ident", (128, 128), BF16)
    t_masks = sb("t_masks", (128, 4 * 512), BF16)
    t_cpool = sb("t_cpool", (128, 8), F32)
    t_stat = sb("t_stat", (128, 32), F32)
    t_bn = sb("t_bn", (128, 96), F32)
    t_rrow = sb("t_rrow", (1, 1024), F32)
    t_mand = sb("t_mand", (128, 1), mybir.dt.int32)
    t_mxor = sb("t_mxor", (128, 1), mybir.dt.int32)
    t_moff = sb("t_moff", (128, 1), F32)

    AR_W = sb("AR_W", (128, 48 * 1024), I8)
    AR_X = sb("AR_X", (128, 32 * 1024), I8)
    AR_XN = sb("AR_XN", (128, 48 * 1024), I8)
    AR_BIG = sb("AR_BIG", (128, 44 * 1024), I8)
    AR_SM = sb("AR_SM", (128, 26112), I8)
    t_E = _carve(AR_BIG, 0, BF16, (32, 32 * 128))
    t_scq = _carve(AR_BIG, 8 * 1024, F32, (32, 512))
    t_sckv = _carve(AR_BIG, 10 * 1024, F32, (32, 256))
    t_scqb = _carve(AR_BIG, 11 * 1024, BF16, (32, 512))
    t_sckvb = _carve(AR_BIG, 12 * 1024, BF16, (32, 256))

    for i, v in enumerate(CVALS):
        nc.const_aps.register(F32, v, t_cpool[:, i:i + 1])

    pbA = es.enter_context(nc.psum_tensor("pbA", [128, 1024], F32))
    pbB = es.enter_context(nc.psum_tensor("pbB", [128, 1024], F32))
    pbC = es.enter_context(nc.psum_tensor("pbC", [128, 512], F32))
    pbRS = es.enter_context(nc.psum_tensor("pbRS", [128, 512], F32))
    pb4 = es.enter_context(nc.psum_tensor("pb4", [128, 512], F32))
    pb5 = es.enter_context(nc.psum_tensor("pb5", [128, 512], F32))
    ptr = pbC.ap().bitcast(BF16)

    p = Plan(nc, es)
    Epe, Eve, Esc, Epo = p.sem("Epe"), p.sem("Eve"), p.sem("Esc"), p.sem("Epo")
    Ccc = p.sem("Ccc")

    def dma(q, out, in_, s):
        return p.op(q, lambda e, o=out, i=in_: e.dma_start(out=o, in_=i),
                    inc=[(s, 16)])[0]

    def phase_gate():
        toks = [(s, s.n) for s in p.all_dsems if s.n > 0]
        toks += [(s, s.n) for s in (Epe, Eve, Esc, Epo) if s.n > 0]
        for e in ENGS:
            p.wait(e, *toks)

    Lco = p.dsem("Lco")
    Lq = [p.dsem("Lq0"), p.dsem("Lq1")]
    Lx = [p.dsem("Lx0"), p.dsem("Lx1")]
    Lcs = [p.dsem("Lcs0"), p.dsem("Lcs1")]
    Wq = [p.dsem("Wq0"), p.dsem("Wq1")]
    Wk = p.dsem("Wk")
    Wv = p.dsem("Wv")
    LKV = p.dsem("LKV")
    Lqt = [p.dsem("Lqt0"), p.dsem("Lqt1")]
    Lpy = [p.dsem("Lpy0"), p.dsem("Lpy1")]
    Lsb = [p.dsem("Lsb0"), p.dsem("Lsb1")]
    Wctx = [p.dsem("Wctx0"), p.dsem("Wctx1")]
    Wde = [p.dsem("Wde0"), p.dsem("Wde1")]
    Lw1 = [p.dsem("Lw10"), p.dsem("Lw11")]
    Lw2 = [p.dsem("Lw20"), p.dsem("Lw21")]
    Lgp = [p.dsem("Lgp0"), p.dsem("Lgp1")]
    Wg = [p.dsem("Wg0"), p.dsem("Wg1")]
    Wgp = [p.dsem("Wgp0"), p.dsem("Wgp1")]
    Lctx = [p.dsem("Lctx0"), p.dsem("Lctx1"), p.dsem("Lctx2")]
    Lwo = [p.dsem("Lwo0"), p.dsem("Lwo1"), p.dsem("Lwo2")]
    Lgg = [p.dsem("Lgg0"), p.dsem("Lgg1")]
    Wsg = [p.dsem("Wsg0"), p.dsem("Wsg1")]
    Lsg = [p.dsem("Lsg0"), p.dsem("Lsg1")]
    Wout = [p.dsem("Wout0"), p.dsem("Wout1")]
    Wtap = p.dsem("Wtap")

    tk_consts = {}
    for nm, out, in_ in (("E", t_E[:], d_E[:]), ("sel2", t_sel2[:], d_sel2[:]),
                         ("ones1", t_ones1[:], d_ones1[:]),
                         ("ones1b", t_ones1b[:], d_ones1b[:]),
                         ("onespv", t_onespv[:], d_onespv[:]),
                         ("ident", t_ident[:], d_ident[:]),
                         ("masks", t_masks[:], d_masks[:]),
                         ("scq", t_scq[:], d_scq[:]), ("sckv", t_sckv[:], d_sckv[:])):
        tk_consts[nm] = dma("sync", out, in_, Lco)
    for nm in tk_consts:
        tk_consts[nm] = (Lco, Lco.n)

    for i, v in enumerate(CVALS):
        tk = p.op("vector", lambda e, i=i, v=v: e.memset(t_cpool[:, i:i + 1], v),
                  inc=[(Eve, 1)] if i == len(CVALS) - 1 else None)
    tok_cpool = tk[0]
    p.op("vector", lambda e: e.memset(t_mand[0:64, :], -16))
    p.op("vector", lambda e: e.memset(t_mand[64:128, :], 15))
    p.op("vector", lambda e: e.memset(t_mxor[0:64, :], 0))
    p.op("vector", lambda e: e.memset(t_mxor[64:128, :], 8))
    p.op("vector", lambda e: e.memset(t_moff[0:64, :], 0.0))
    tok_mconst = p.op("vector", lambda e: e.memset(t_moff[64:128, :], -8.0),
                      inc=[(Eve, 1)])[0]
    p.wait("vector", tk_consts["scq"], tk_consts["sckv"])
    p.op("vector", lambda e: e.tensor_copy(t_scqb[:], t_scq[:]))
    tok_scb = p.op("vector", lambda e: e.tensor_copy(t_sckvb[:], t_sckv[:]),
                   inc=[(Eve, 1)])[0]

    wq_sb = _carve(AR_W, 0, BF16, (128, 32 * 512))
    wkv_sb = _carve(AR_W, 32 * 1024, BF16, (128, 32 * 256))
    qb_st = [_carve(AR_SM, i * 512, I8, (128, 512)) for i in range(2)]
    kv_st = [_carve(AR_SM, 1024 + i * 256, I8, (128, 256)) for i in range(2)]

    p.wait("tensor", tk_consts["E"], tok_scb)
    tok_p0 = [None, None]
    for ib in range(KI):
        par = ib % 2
        if tok_p0[par] is not None:
            p.wait("sync", tok_p0[par])
        ldq = dma("sync", qb_st[par][:], d_qbq[ib * 128:(ib + 1) * 128, :], Lq[par])
        ldkv = dma("sync", kv_st[par][:], d_qbkv[ib * 128:(ib + 1) * 128, :], Lq[par])
        if tok_p0[1 - par] is not None:
            p.wait("tensor", tok_p0[1 - par])
        p.op("tensor", lambda e, ib=ib: e.matmul(
            pbC[:, 0:512], t_E[:, ib * 128:(ib + 1) * 128], t_scqb[:, :],
            start=True, stop=True))
        tkB = p.op("tensor", lambda e, ib=ib: e.matmul(
            pbRS[:, 0:256], t_E[:, ib * 128:(ib + 1) * 128], t_sckvb[:, :],
            start=True, stop=True), inc=[(Epe, 1)])
        p.wait("vector", ldq, ldkv, tkB[0])
        p.op("vector", lambda e, ib=ib, par=par: e.scalar_tensor_tensor(
            wq_sb[:, ib * 512:(ib + 1) * 512], qb_st[par][:], -8.0, pbC[:, 0:512],
            op0=ALU.add, op1=ALU.mult))
        tok_p0[par] = p.op("vector", lambda e, ib=ib, par=par: e.scalar_tensor_tensor(
            wkv_sb[:, ib * 256:(ib + 1) * 256], kv_st[par][:], -8.0, pbRS[:, 0:256],
            op0=ALU.add, op1=ALU.mult), inc=[(Eve, 1)])[0]

    x_tm = [_carve(AR_X, i * 16 * 1024, F32, (128, 4096)) for i in range(2)]
    xn_tm = [_carve(AR_XN, i * 8 * 1024, BF16, (128, 4096)) for i in range(2)]
    xn_fm = _carve(AR_XN, 16 * 1024, BF16, (128, 32 * 512))
    cos_sl = [_carve(AR_SM, 2048 + i * 2048, F32, (128, 512)) for i in range(2)]
    sin_sl = [_carve(AR_SM, 6144 + i * 2048, F32, (128, 512)) for i in range(2)]
    tmpA = _carve(AR_SM, 10240, F32, (128, 512))
    tmpB = _carve(AR_SM, 12288, F32, (128, 512))
    st_q = [_carve(AR_SM, 14336 + i * 1024, BF16, (128, 512)) for i in range(2)]
    st_v = _carve(AR_SM, 16384, BF16, (128, 512))

    qb_ap = [pbA[:, 0:512], pbA[:, 512:1024], pbB[:, 0:512], pbB[:, 512:1024]]
    xn_fm3 = xn_fm.rearrange("p (ib c) -> p ib c", ib=32)

    tok_xtm = [None, None]
    tok_xntm_w = [None, None]
    tok_xntm_r = [None, None]
    tok_dump = [None, None]
    tok_bn = [None, None]
    tok_stat = [None] * 8
    tok_ptr_r = [None, None]
    tok_ptr_w = [None, None]
    tok_xn_cons = None
    tok_q_free = None
    tok_k_free = None
    tok_v_free = None
    tok_cos = [None, None]
    tok_stq = [None, None]
    tok_ropeadd = None
    tok_stk = None
    tok_stv = None
    tok_qscr = {}
    tok_kscr = {}
    tok_vscr = {}

    K_sb = _carve(AR_BIG, 0, BF16, (128, S))
    V_sb = _carve(AR_BIG, 16 * 1024, BF16, (128, S))
    PT = [_carve(AR_BIG, 32 * 1024 + i * 2048, BF16, (128, 1024)) for i in range(2)]
    q_t = [_carve(AR_BIG, 36 * 1024 + i * 1024, BF16, (128, 512)) for i in range(2)]
    R_sb = [_carve(AR_BIG, 38 * 1024, F32, (128, 512)),
            _carve(AR_BIG, 40 * 1024, F32, (128, 512))]
    ctx_st = [_carve(AR_BIG, 42 * 1024 + i * 1024, BF16, (128, 512)) for i in range(2)]
    pay_st = [_carve(AR_SM, i * 1024, I8, (128, 1024)) for i in range(2)]
    ext_st = [_carve(AR_SM, 17920 + i * 1024, I8, (128, 1024)) for i in range(2)]
    wde_st = [_carve(AR_SM, 19968 + i * 2048, BF16, (128, 1024)) for i in range(2)]
    sbrb_st = [_carve(AR_SM, 24064, BF16, (34, 1024))[i * 32:i * 32 + 2]
               for i in range(2)]

    deq_jobs = []
    for mat, (srcp, srcs, dst, ncols) in (
            ("w1", (d_p1, d_sb1, w1t_in, 4096)),
            ("w3", (d_p3, d_sb3, w3t_in, 4096)),
            ("w2", (d_p2, d_sb2, w2t_in, HF))):
        nsl = W1SLOT if ncols == 4096 else W2SLOT
        for sl in range(nsl):
            coff = 0
            while coff < ncols:
                cw = min(512, ncols - coff)
                deq_jobs.append(("ffn", srcp, srcs, dst, sl, coff, cw))
                coff += cw
    for sl in range(W2SLOT):
        for coff in range(0, 4096, 512):
            deq_jobs.append(("wo", d_qbo, d_sco, wo_in, sl, coff, 512))
    n_w1 = sum(1 for j in deq_jobs if j[3] is w1t_in)
    n_w3 = sum(1 for j in deq_jobs if j[3] is w3t_in)
    n_w2 = sum(1 for j in deq_jobs if j[3] is w2t_in)
    ag_after = {n_w1 - 1: ("w1", w1t_in, w1t_out),
                n_w1 + n_w3 - 1: ("w3", w3t_in, w3t_out),
                n_w1 + n_w3 + n_w2 - 1: ("w2", w2t_in, w2t_out),
                len(deq_jobs) - 1: ("wo", wo_in, wo_out)}

    deq_state = dict(idx=0, ld_idx=0, tok_pay=[None, None], tok_ext=[None, None],
                     tok_wde=[None, None], tok_sbr=[None, None], pend={},
                     tok_B=None, tok_Bcons=None, last_wr=None, ag_toks=[])

    def deq_load(i):
        ds = deq_state
        kind, srcp, srcs, dst, sl, coff, cw = deq_jobs[i]
        par = i % 2
        if i < 2:
            p.wait("scalar", tok_p0[0], tok_p0[1])
        if ds["tok_ext"][par] is not None:
            p.wait("scalar", ds["tok_ext"][par])
        if ds["tok_sbr"][par] is not None:
            p.wait("gpsimd", ds["tok_sbr"][par])
        if kind == "ffn":
            srcp2 = srcp.ap().rearrange("s e k h -> s e (k h)")
            t1 = dma("scalar", pay_st[par][0:64, 0:cw], srcp2[sl, :, coff:coff + cw],
                     Lpy[par])
            t2 = dma("scalar", pay_st[par][64:128, 0:cw], srcp2[sl, :, coff:coff + cw],
                     Lpy[par])
            srcs2 = srcs.ap().rearrange("s h k j -> s h (k j)")
            t3 = dma("gpsimd", sbrb_st[par][:, 0:cw], srcs2[sl, :, coff:coff + cw],
                     Lsb[par])
        else:
            t2 = dma("scalar", pay_st[par][:, 0:cw],
                     srcp[sl * 128:(sl + 1) * 128, coff:coff + cw], Lpy[par])
            t3 = dma("gpsimd", sbrb_st[par][0:1, 0:cw], srcs[sl:sl + 1, coff:coff + cw],
                     Lsb[par])
        ds["pend"][i] = (t2, t3)

    def deq_compute(i):
        ds = deq_state
        kind, srcp, srcs, dst, sl, coff, cw = deq_jobs[i]
        par = i % 2
        t2, t3 = ds["pend"].pop(i)
        if kind == "ffn":
            p.wait("tensor", t3, tk_consts["sel2"])
            if ds["tok_Bcons"] is not None:
                p.wait("tensor", ds["tok_Bcons"])
            tkB = p.op("tensor", lambda e, par=par, cw=cw: e.matmul(
                pbC[:, 0:cw], t_sel2[par * 32:par * 32 + 2, :],
                sbrb_st[par][:, 0:cw], start=True, stop=True), inc=[(Epe, 1)])
            ds["tok_B"] = tkB[0]
            ds["tok_sbr"][par] = tkB[0]
            p.wait("vector", t2, tok_mconst)
            if ds["tok_wde"][par] is not None:
                p.wait("vector", ds["tok_wde"][par])
            tke = p.op("vector", lambda e, par=par, cw=cw: e.tensor_scalar(
                ext_st[par][:, 0:cw], pay_st[par][:, 0:cw], t_mand[:, 0:1],
                t_mxor[:, 0:1], op0=ALU.bitwise_and, op1=ALU.bitwise_xor),
                inc=[(Eve, 1)])[0]
            ds["tok_pay"][par] = tke
            ds["tok_ext"][par] = tke
            p.wait("vector", tke, ds["tok_B"])
            tkm = p.op("vector", lambda e, par=par, cw=cw: e.scalar_tensor_tensor(
                wde_st[par][:, 0:cw], ext_st[par][:, 0:cw], t_moff[:, 0:1],
                pbC[:, 0:cw], op0=ALU.add, op1=ALU.mult),
                inc=[(Eve, 1)])[0]
            ds["tok_Bcons"] = tkm
        else:
            p.wait("tensor", t3, tk_consts["ones1b"])
            if ds["tok_Bcons"] is not None:
                p.wait("tensor", ds["tok_Bcons"])
            tkB = p.op("tensor", lambda e, par=par, cw=cw: e.matmul(
                pbC[:, 0:cw], t_ones1b[par * 32:par * 32 + 1, :],
                sbrb_st[par][0:1, 0:cw], start=True, stop=True), inc=[(Epe, 1)])
            ds["tok_B"] = tkB[0]
            ds["tok_sbr"][par] = tkB[0]
            p.wait("vector", t2, ds["tok_B"])
            if ds["tok_wde"][par] is not None:
                p.wait("vector", ds["tok_wde"][par])
            tkm = p.op("vector", lambda e, par=par, cw=cw: e.scalar_tensor_tensor(
                wde_st[par][:, 0:cw], pay_st[par][:, 0:cw], -8.0, pbC[:, 0:cw],
                op0=ALU.add, op1=ALU.mult), inc=[(Eve, 1)])[0]
            ds["tok_ext"][par] = tkm
            ds["tok_Bcons"] = tkm
        p.wait("gpsimd", tkm)
        twr = dma("gpsimd", dst[sl, :, coff:coff + cw], wde_st[par][:, 0:cw], Wde[par])
        ds["tok_wde"][par] = twr
        ds["last_wr"] = twr
        if i in ag_after:
            nm, src_, dstt = ag_after[i]
            p.wait("gpsimd", (Wde[0], Wde[0].n), (Wde[1], Wde[1].n))
            ds["ag_toks"].append(p.op("gpsimd", lambda e, src_=src_, dstt=dstt:
                                      e.collective_compute(
                "AllGather", ALU.bypass, replica_groups=[list(range(NCORE))],
                ins=[src_.ap().opt()], outs=[dstt.ap().opt()]), inc=[(Ccc, 1)])[0])

    def emit_deq_chunk():
        ds = deq_state
        while ds["ld_idx"] < len(deq_jobs) and ds["ld_idx"] - ds["idx"] < 2:
            deq_load(ds["ld_idx"])
            ds["ld_idx"] += 1
        if ds["idx"] < ds["ld_idx"]:
            deq_compute(ds["idx"])
            ds["idx"] += 1
        elif ds.get("extra_gen") is not None:
            try:
                next(ds["extra_gen"])
            except StopIteration:
                ds["extra_gen"] = None

    tok_qt = [None, None]
    tok_PT_r = [None, None]
    tok_PT_w = [None, None]
    tok_S_r = [None, None]
    tok_S_w = [None, None]
    tok_ctx_free = [None, None]
    tok_rs_free = [None, None]
    tok_R_cons = None
    tok_ctxst = [None, None]
    tok_rrow = [None, None]
    tok_a2a_last = None
    tok_attn_mm = None

    tok_kv_ready = [None] * NTT
    rs_ap = [pbRS[0:1, 0:512], pbRS[32:33, 0:512]]
    ctx_ap = [pb4, pb5]
    Sb_ap = [pbA, pbB]

    def emit_S(ch, pi, tb, ldq):
        Sb = Sb_ap[ch]
        u0, u1 = 2 * pi, 2 * pi + 1
        p.wait("tensor", ldq, tok_kv_ready[tb])
        if tok_S_r[ch] is not None:
            p.wait("tensor", tok_S_r[ch])
        p.op("tensor", lambda e, Sb=Sb, u0=u0, ch=ch: e.matmul(
            Sb[:, 0:512], K_sb[:, u0 * 128:(u0 + 1) * 128], q_t[ch][:],
            start=True, stop=True))
        tkS = p.op("tensor", lambda e, Sb=Sb, u1=u1, ch=ch: e.matmul(
            Sb[:, 512:1024], K_sb[:, u1 * 128:(u1 + 1) * 128], q_t[ch][:],
            start=True, stop=True), inc=[(Epe, 1)])[0]
        tok_S_w[ch] = tkS
        p.wait("scalar", tkS)
        if tok_PT_r[ch] is not None:
            p.wait("scalar", tok_PT_r[ch])
        tkE = p.op("scalar", lambda e, Sb=Sb, ch=ch: e.activation(
            PT[ch][:], Sb[:], AF.Exp, bias=-CSHIFT, scale=ISQ),
            inc=[(Esc, 1)])[0]
        tok_S_r[ch] = tkE
        tok_PT_w[ch] = tkE
        npair = 2 * tb + 2
        if pi >= npair - 2:
            i0 = u0 - 4 * tb
            p.wait("vector", tkE, tk_consts["masks"])
            p.op("vector", lambda e, ch=ch, i0=i0: e.tensor_mul(
                PT[ch][:, 0:512], PT[ch][:, 0:512],
                t_masks[:, i0 * 512:(i0 + 1) * 512]))
            tkM = p.op("vector", lambda e, ch=ch, i0=i0: e.tensor_mul(
                PT[ch][:, 512:1024], PT[ch][:, 512:1024],
                t_masks[:, (i0 + 1) * 512:(i0 + 2) * 512]), inc=[(Eve, 1)])[0]
            tok_PT_w[ch] = tkM

    def emit_PV(ch, pi, tb):
        u0, u1 = 2 * pi, 2 * pi + 1
        npair = 2 * tb + 2
        first_pv = (pi == 0)
        last = (pi == npair - 1)
        p.wait("tensor", tok_PT_w[ch], tok_kv_ready[tb], tk_consts["onespv"])
        if first_pv:
            if tok_ctx_free[ch] is not None:
                p.wait("tensor", tok_ctx_free[ch])
            p.wait("tensor", tok_rs_free[0], tok_rs_free[1])
        p.op("tensor", lambda e, u0=u0, ch=ch, fp=first_pv: e.matmul(
            ctx_ap[ch][:], V_sb[:, u0 * 128:(u0 + 1) * 128], PT[ch][:, 0:512],
            start=fp, stop=False))
        p.op("tensor", lambda e, u1=u1, ch=ch, last=last: e.matmul(
            ctx_ap[ch][:], V_sb[:, u1 * 128:(u1 + 1) * 128], PT[ch][:, 512:1024],
            start=False, stop=last))
        p.op("tensor", lambda e, ch=ch, fp=first_pv: e.matmul(
            rs_ap[ch], t_onespv[:], PT[ch][:, 0:512], start=fp, stop=False))
        tkP = p.op("tensor", lambda e, ch=ch, last=last: e.matmul(
            rs_ap[ch], t_onespv[:], PT[ch][:, 512:1024],
            start=False, stop=last), inc=[(Epe, 1)])[0]
        tok_PT_r[ch] = tkP
        return tkP

    def emit_epilogue(ch, qh, tb):
        nonlocal tok_a2a_last
        p.wait("vector", tok_PT_r[ch])
        if tok_rrow[ch] is not None:
            p.wait("vector", tok_rrow[ch])
        tkv = p.op("vector", lambda e, ch=ch: e.tensor_single_scalar(
            t_rrow[0:1, ch * 512:(ch + 1) * 512], rs_ap[ch], EPS_RS,
            ALU.add), inc=[(Eve, 1)])[0]
        tok_rs_free[ch] = tkv
        p.wait("vector", tkv)
        tkv = p.op("vector", lambda e, ch=ch: e.reciprocal(
            t_rrow[0:1, ch * 512:(ch + 1) * 512],
            t_rrow[0:1, ch * 512:(ch + 1) * 512]), inc=[(Eve, 1)])[0]
        p.wait("tensor", tkv, tk_consts["ones1"])
        if deq_state["tok_Bcons"] is not None:
            p.wait("tensor", deq_state["tok_Bcons"])
        tkR = p.op("tensor", lambda e, ch=ch: e.matmul(
            pbC[:, 0:512], t_ones1[:],
            t_rrow[0:1, ch * 512:(ch + 1) * 512], start=True, stop=True),
            inc=[(Epe, 1)])[0]
        tok_rrow[ch] = tkR
        deq_state["tok_B"] = tkR
        p.wait("vector", tkR)
        tkv = p.op("vector", lambda e, ch=ch: e.tensor_copy(R_sb[ch][:],
                                                            pbC[:, 0:512]),
                   inc=[(Eve, 1)])[0]
        deq_state["tok_Bcons"] = tkv
        p.wait("vector", tkv)
        if tok_ctxst[ch] is not None:
            p.wait("vector", tok_ctxst[ch])
        tkv = p.op("vector", lambda e, ch=ch: e.tensor_mul(
            ctx_st[ch][:], ctx_ap[ch][:], R_sb[ch][:]), inc=[(Eve, 1)])[0]
        tok_ctx_free[ch] = tkv
        p.wait("gpsimd", tkv)
        dd, chh = tb // 2, tb % 2
        tok_ctxst[ch] = dma(
            "gpsimd", a2a_in[dd * 512 + qh * 128:dd * 512 + (qh + 1) * 128,
                             chh * 512:(chh + 1) * 512], ctx_st[ch][:], Wctx[ch])
        tok_a2a_last = tok_ctxst[ch]

    def emit_attn_pairgroup(hp, tb):
        nonlocal tok_attn_mm
        if tb == 0 and hp == 0:
            p.wait("tensor", tok_p0[0], tok_p0[1])
        npair = 2 * tb + 2
        lds = [None, None]
        for ch in range(2):
            qh = 2 * hp + ch
            if tok_qt[ch] is not None:
                p.wait("sync", tok_qt[ch])
            p.wait("sync", tok_qscr[(qh, tb)])
            lds[ch] = dma("sync", q_t[ch][:], q_scr[qh, :, tb * 512:(tb + 1) * 512],
                          Lqt[ch])
        emit_S(0, 0, tb, lds[0])
        emit_S(1, 0, tb, lds[1])
        for pi in range(1, npair):
            emit_PV(0, pi - 1, tb)
            emit_S(0, pi, tb, lds[0])
            emit_PV(1, pi - 1, tb)
            emit_S(1, pi, tb, lds[1])
            if pi % 2 == 1:
                emit_deq_chunk()
        tokA = emit_PV(0, npair - 1, tb)
        tokB = emit_PV(1, npair - 1, tb)
        tok_qt[0] = tokA
        tok_qt[1] = tokB
        tok_attn_mm = tokB
        emit_epilogue(0, 2 * hp, tb)
        emit_epilogue(1, 2 * hp + 1, tb)
        emit_deq_chunk()

    xn_h = _carve(AR_XN, 16 * 1024, BF16, (128, 32 * 512))
    xn_h3 = xn_h.rearrange("p (ib c) -> p ib c", ib=32)

    p4_state = {}
    p4_pulled = {}
    tok_xnh_cons = None

    def p4_ln(tp):
        nonlocal tok_xnh_cons, tok_xn_cons
        tok_apply = None
        tok_apply = None
        for s4 in range(4):
            par = s4 % 2
            ssl = s4 % 8
            stc = t_stat[:, ssl * 4:(ssl + 1) * 4]
            if tok_xtm[par] is not None:
                p.wait("sync", tok_xtm[par])
            if s4 == 0 and tok_xnh_cons is not None:
                p.wait("sync", tok_xnh_cons)
            if s4 == 0 and tp == 0:
                p.wait("tensor", tok_xn_cons)
            ldx = dma("sync", x_tm[par][:],
                      d_x_own[tp * 512 + s4 * 128:tp * 512 + (s4 + 1) * 128, :],
                      Lx[par])
            p.wait("vector", ldx)
            if tok_bn[par] is not None:
                p.wait("vector", tok_bn[par])
            if tok_stat[ssl] is not None:
                p.wait("vector", tok_stat[ssl])
            for j in range(8):
                toks = p.op("vector", lambda e, par=par, j=j: e.bn_stats(
                    t_bn[:, par * 48 + j * 6:par * 48 + (j + 1) * 6],
                    x_tm[par][:, j * 512:(j + 1) * 512]),
                    inc=[(Eve, 1)] if j == 7 else None)
            p.wait("vector", toks[0])
            tkagg = p.op("vector", lambda e, par=par, stc=stc: e.bn_aggr(
                stc[:, 0:2], t_bn[:, par * 48:par * 48 + 48]), inc=[(Eve, 1)])[0]
            tok_bn[par] = tkagg
            p.wait("vector", tkagg)
            tkv = p.op("vector", lambda e, stc=stc: e.tensor_single_scalar(
                stc[:, 2:3], stc[:, 0:1], -1.0, ALU.mult), inc=[(Eve, 1)])[0]
            p.wait("scalar", tkv)
            tkl = p.op("scalar", lambda e, stc=stc: e.activation(
                stc[:, 1:2], stc[:, 1:2], AF.Ln, bias=EPS_LN, scale=1.0),
                inc=[(Esc, 1)])[0]
            p.wait("scalar", tkl)
            tks2 = p.op("scalar", lambda e, stc=stc: e.activation(
                stc[:, 3:4], stc[:, 1:2], AF.Exp, bias=0.0, scale=-0.5),
                inc=[(Esc, 1)])[0]
            yield
            p.wait("vector", tks2)
            tkv = p.op("vector", lambda e, par=par, stc=stc: e.tensor_scalar(
                xn_tm[par][:], x_tm[par][:], stc[:, 2:3], stc[:, 3:4],
                op0=ALU.add, op1=ALU.mult), inc=[(Eve, 1)])[0]
            tok_xtm[par] = tkv
            tok_stat[ssl] = tkv
            p.wait("tensor", tkv)
            if s4 == 0 and tok_xnh_cons is not None:
                p.wait("tensor", tok_xnh_cons)
            if s4 == 0 and tp == 0:
                if deq_state["tok_Bcons"] is not None:
                    p.wait("tensor", deq_state["tok_Bcons"])
            for grp in range(4):
                if deq_state["tok_Bcons"] is not None:
                    p.wait("tensor", deq_state["tok_Bcons"])
                for j in range(8):
                    ib = grp * 8 + j
                    tktr = p.op("tensor", lambda e, par=par, ib=ib, j=j:
                                e.transpose(
                        ptr[:, j * 128:(j + 1) * 128],
                        xn_tm[par][:, ib * 128:(ib + 1) * 128], t_ident[:]),
                        inc=[(Epe, 1)] if j == 7 else None)
                deq_state["tok_B"] = tktr[0]
                p.wait("vector", tktr[0])
                tok_apply = p.op("vector", lambda e, s4=s4, grp=grp:
                                 e.tensor_copy(
                    xn_h3[:, grp * 8:(grp + 1) * 8, s4 * 128:(s4 + 1) * 128],
                    ptr[:, 0:1024]), inc=[(Eve, 1)])[0]
                deq_state["tok_Bcons"] = tok_apply
                yield
            tok_xntm_r[par] = tktr[0]
        p4_state[tp] = tok_apply

    for tt in range(NTT):
        tok_evac_last = None
        for s4 in range(4):
            sub = tt * 4 + s4
            par = sub % 2
            ssl = sub % 8
            stc = t_stat[:, ssl * 4:(ssl + 1) * 4]
            if tok_xtm[par] is not None:
                p.wait("sync", tok_xtm[par])
            ldx = dma("sync", x_tm[par][:], d_x_all[sub * 128:(sub + 1) * 128, :],
                      Lx[par])
            p.wait("scalar", ldx, tok_cpool)
            if tok_stat[ssl] is not None:
                p.wait("scalar", tok_stat[ssl])
            if tok_xntm_r[par] is not None:
                p.wait("scalar", tok_xntm_r[par])
            tki = p.op("scalar", lambda e, par=par, stc=stc: e.activation(
                xn_tm[par][:], x_tm[par][:], AF.Identity, bias=0.0, scale=1.0,
                accum_out=stc[:, 0:1]), inc=[(Esc, 1)])[0]
            p.wait("scalar", tki)
            tok_dump[par] = tks = p.op("scalar", lambda e, par=par, stc=stc:
                                       e.activation(
                xn_tm[par][:], x_tm[par][:], AF.Square, bias=0.0, scale=1.0,
                accum_out=stc[:, 1:2]), inc=[(Esc, 1)])[0]
            p.wait("vector", tks)
            p.op("vector", lambda e, stc=stc: e.tensor_single_scalar(
                stc[:, 2:3], stc[:, 0:1], -1.0 / DM, ALU.mult))
            tkv = p.op("vector", lambda e, stc=stc: e.tensor_single_scalar(
                stc[:, 1:2], stc[:, 1:2], 1.0 / DM, ALU.mult), inc=[(Eve, 1)])[0]
            p.wait("vector", tkv)
            tkm = p.op("vector", lambda e, stc=stc: e.tensor_mul(
                stc[:, 0:1], stc[:, 2:3], stc[:, 2:3]), inc=[(Eve, 1)])[0]
            p.wait("vector", tkm)
            tkv = p.op("vector", lambda e, stc=stc: e.tensor_sub(
                stc[:, 1:2], stc[:, 1:2], stc[:, 0:1]), inc=[(Eve, 1)])[0]
            p.wait("scalar", tkv)
            tkl = p.op("scalar", lambda e, stc=stc: e.activation(
                stc[:, 1:2], stc[:, 1:2], AF.Ln, bias=EPS_LN, scale=1.0),
                inc=[(Esc, 1)])[0]
            p.wait("scalar", tkl)
            tks2 = p.op("scalar", lambda e, stc=stc: e.activation(
                stc[:, 3:4], stc[:, 1:2], AF.Exp, bias=0.0, scale=-0.5),
                inc=[(Esc, 1)])[0]
            p.wait("vector", tks2, tok_dump[par])
            tkv = p.op("vector", lambda e, par=par, stc=stc: e.tensor_scalar(
                xn_tm[par][:], x_tm[par][:], stc[:, 2:3], stc[:, 3:4],
                op0=ALU.add, op1=ALU.mult), inc=[(Eve, 1)])[0]
            tok_xtm[par] = tkv
            tok_stat[ssl] = tkv
            tok_xntm_w[par] = tkv
            p.wait("tensor", tkv)
            if s4 == 0 and tok_xn_cons is not None:
                p.wait("tensor", tok_xn_cons)
            for grp in range(4):
                if deq_state["tok_Bcons"] is not None:
                    p.wait("tensor", deq_state["tok_Bcons"])
                for j in range(8):
                    ib = grp * 8 + j
                    tktr = p.op("tensor", lambda e, par=par, ib=ib, j=j:
                                e.transpose(
                        ptr[:, j * 128:(j + 1) * 128],
                        xn_tm[par][:, ib * 128:(ib + 1) * 128], t_ident[:]),
                        inc=[(Epe, 1)] if j == 7 else None)
                deq_state["tok_B"] = tktr[0]
                p.wait("vector", tktr[0])
                tok_evac_last = p.op("vector", lambda e, s4=s4, grp=grp:
                                     e.tensor_copy(
                    xn_fm3[:, grp * 8:(grp + 1) * 8, s4 * 128:(s4 + 1) * 128],
                    ptr[:, 0:1024]), inc=[(Eve, 1)])[0]
                deq_state["tok_Bcons"] = tok_evac_last
            tok_xntm_r[par] = tktr[0]

        parc = tt % 2
        if tok_cos[parc] is not None:
            p.wait("scalar", tok_cos[parc])
        ldc = dma("scalar", cos_sl[parc][:], d_cos[:, tt * 512:(tt + 1) * 512],
                  Lcs[parc])
        lds = dma("scalar", sin_sl[parc][:], d_sin[:, tt * 512:(tt + 1) * 512],
                  Lcs[parc])

        p.wait("tensor", tok_evac_last)
        if tok_q_free is not None:
            p.wait("tensor", tok_q_free)
        for jc in range(4):
            for ib in range(KI):
                tkq = p.op("tensor", lambda e, jc=jc, ib=ib: e.matmul(
                    qb_ap[jc], wq_sb[:, (ib * 4 + jc) * 128:(ib * 4 + jc + 1) * 128],
                    xn_fm[:, ib * 512:(ib + 1) * 512],
                    start=(ib == 0), stop=(ib == KI - 1)),
                    inc=[(Epe, 1)] if (ib == KI - 1) else None)
        tok_proj_q = tkq[0]
        if tok_k_free is not None:
            p.wait("tensor", tok_k_free)
        for ib in range(KI):
            tkk = p.op("tensor", lambda e, ib=ib: e.matmul(
                pb4[:], wkv_sb[:, ib * 256:ib * 256 + 128],
                xn_fm[:, ib * 512:(ib + 1) * 512],
                start=(ib == 0), stop=(ib == KI - 1)),
                inc=[(Epe, 1)] if (ib == KI - 1) else None)
        tok_proj_k = tkk[0]
        if tok_v_free is not None:
            p.wait("tensor", tok_v_free)
        for ib in range(KI):
            tkv2 = p.op("tensor", lambda e, ib=ib: e.matmul(
                pb5[:], wkv_sb[:, ib * 256 + 128:ib * 256 + 256],
                xn_fm[:, ib * 512:(ib + 1) * 512],
                start=(ib == 0), stop=(ib == KI - 1)),
                inc=[(Epe, 1)] if (ib == KI - 1) else None)
        tok_proj_v = tkv2[0]
        tok_xn_cons = tok_proj_v

        p.wait("vector", tok_proj_q, ldc, lds)
        for jc in range(4):
            parq = jc % 2
            if tok_stq[parq] is not None:
                p.wait("vector", tok_stq[parq])
            if tok_ropeadd is not None:
                p.wait("vector", tok_ropeadd)
            p.op("vector", lambda e, jc=jc, parc=parc: e.tensor_mul(
                tmpA[:], qb_ap[jc], cos_sl[parc][:]))
            p.op("vector", lambda e, jc=jc, parc=parc: e.tensor_mul(
                tmpB[0:64, :], qb_ap[jc][64:128, :], sin_sl[parc][0:64, :]))
            tkr = p.op("vector", lambda e, jc=jc, parc=parc: e.tensor_mul(
                tmpB[64:128, :], qb_ap[jc][0:64, :], sin_sl[parc][64:128, :]),
                inc=[(Eve, 1)])[0]
            p.wait("vector", tkr)
            tkr = p.op("vector", lambda e, parq=parq: e.tensor_add(
                st_q[parq][:], tmpA[:], tmpB[:]), inc=[(Eve, 1)])[0]
            tok_ropeadd = tkr
            p.wait("gpsimd", tkr)
            tok_stq[parq] = dma("gpsimd", q_scr[jc, :, tt * 512:(tt + 1) * 512],
                                st_q[parq][:], Wq[parq])
            tok_qscr[(jc, tt)] = tok_stq[parq]
            if jc == 3:
                tok_q_free = tkr
        p.wait("vector", tok_proj_k, tok_ropeadd)
        if tok_stk is not None:
            p.wait("vector", tok_stk)
        p.op("vector", lambda e, parc=parc: e.tensor_mul(tmpA[:], pb4[:], cos_sl[parc][:]))
        p.op("vector", lambda e, parc=parc: e.tensor_mul(
            tmpB[0:64, :], pb4[64:128, :], sin_sl[parc][0:64, :]))
        tkr = p.op("vector", lambda e, parc=parc: e.tensor_mul(
            tmpB[64:128, :], pb4[0:64, :], sin_sl[parc][64:128, :]), inc=[(Eve, 1)])[0]
        p.wait("vector", tkr)
        tkr = p.op("vector", lambda e, tt=tt: e.tensor_add(
            K_sb[:, tt * 512:(tt + 1) * 512], tmpA[:], tmpB[:]), inc=[(Eve, 1)])[0]
        tok_ropeadd = tkr
        tok_k_free = tkr
        tok_cos[parc] = tkr
        p.wait("vector", tok_proj_v)
        if tok_stv is not None:
            p.wait("vector", tok_stv)
        tkv3 = p.op("vector", lambda e: e.tensor_copy(st_v[:], pb5[:]), inc=[(Eve, 1)])[0]
        tok_v_free = tkv3
        p.wait("tensor", tkv3)
        if deq_state["tok_Bcons"] is not None:
            p.wait("tensor", deq_state["tok_Bcons"])
        for a in range(4):
            tktr = p.op("tensor", lambda e, a=a: e.transpose(
                ptr[:, a * 128:(a + 1) * 128], st_v[:, a * 128:(a + 1) * 128],
                t_ident[:]), inc=[(Epe, 1)] if a == 3 else None)
        deq_state["tok_B"] = tktr[0]
        p.wait("vector", tktr[0])
        tkv3 = p.op("vector", lambda e, tt=tt: e.tensor_copy(
            V_sb[:, tt * 512:(tt + 1) * 512], ptr[:, 0:512]), inc=[(Eve, 1)])[0]
        deq_state["tok_Bcons"] = tkv3
        tok_kv_ready[tt] = tkv3
        if tt == NTT - 1:
            g0 = p4_ln(0)
            p4_pulled[0] = g0
            deq_state["extra_gen"] = g0
        for hp in range(2):
            emit_attn_pairgroup(hp, tt)

    while deq_state["idx"] < len(deq_jobs):
        emit_deq_chunk()

    p.wait("gpsimd", (Wctx[0], Wctx[0].n), (Wctx[1], Wctx[1].n))

    def do_a2a(e):
        with allow_ici_all_to_all():
            return e.collective_compute(
                "AllToAll", ALU.bypass, replica_groups=[list(range(NCORE))],
                ins=[a2a_in.ap().opt()], outs=[a2a_out.ap().opt()])

    tok_cc_ctx = p.op("gpsimd", do_a2a, inc=[(Ccc, 1)])[0]

    h_act = _carve(AR_BIG, 0, BF16, (128, 43 * 512))
    w1t_sb = [_carve(AR_W, i * 8 * 1024, BF16, (128, 4096)) for i in range(2)]
    w3t_sb = [_carve(AR_W, 16 * 1024 + i * 8 * 1024, BF16, (128, 4096)) for i in range(2)]
    w2t_sb = [_carve(AR_X, i * 11008, BF16, (128, 43 * 128)) for i in range(2)]
    ha_st = [_carve(AR_W, 32 * 1024 + i * 1024, BF16, (128, 512)) for i in range(2)]
    g_st = [_carve(AR_W, 34 * 1024 + i * 1024, BF16, (128, 512)) for i in range(2)]
    gp_st = [_carve(AR_W, 36 * 1024 + i * 2048, F32, (128, 512)) for i in range(2)]
    sum_st = [_carve(AR_W, 40 * 1024 + i * 2048, F32, (128, 512)) for i in range(2)]

    tok_ag_w1 = (Ccc, 1)
    tok_ag_w3 = (Ccc, 2)
    tok_ag_w2 = (Ccc, 3)
    tok_ag_wo = (Ccc, 4)
    tok_cc_all = (Ccc, 5)

    tok_w1sb = [None, None]
    tok_hact_r = None
    tok_ab_free = [None, None]
    tok_g_free = [None, None]
    tok_ha = [None, None]
    tok_gst = [None, None]
    tok_gpst = [None, None]
    tok_sumst = [None, None]
    tok_w2sb = [None, None]
    tok_gpart = [None] * 32
    tok_gscr = {}
    p.wait("vector", tok_qt[0], tok_qt[1], tok_ctx_free[0], tok_ctx_free[1],
           tok_rrow[0], tok_rrow[1], tok_ctxst[0], tok_ctxst[1],
           (Wctx[0], Wctx[0].n), (Wctx[1], Wctx[1].n))
    p.wait("tensor", tok_S_r[0], tok_S_r[1], tok_qt[0], tok_qt[1])
    if deq_state["tok_Bcons"] is not None:
        p.wait("tensor", deq_state["tok_Bcons"])

    for tp in range(2):
        if tp in p4_pulled:
            for _ in p4_pulled[tp]:
                pass
        else:
            for _ in p4_ln(tp):
                pass
        tok_xnh_ready = p4_state[tp]

        ab_ap = [(pbA[:, 0:512], pbB[:, 0:512]), (pbA[:, 512:1024], pbB[:, 512:1024])]
        g_ap = [pb4[:], pb5[:]]
        for hhalf in range(2):
            for hh in range(43):
                hb = hhalf * 43 + hh
                par = hh % 2
                if tok_w1sb[par] is not None:
                    p.wait("sync", tok_w1sb[par])
                p.wait("sync", tok_ag_w1, tok_ag_w3)
                ld1 = dma("sync", w1t_sb[par][:], w1t_out[hb, :, :], Lw1[par])
                ld3 = dma("sync", w3t_sb[par][:], w3t_out[hb, :, :], Lw1[par])
                a_ap, b_ap = ab_ap[par]
                p.wait("tensor", ld1, ld3, tok_xnh_ready)
                if tok_ab_free[par] is not None:
                    p.wait("tensor", tok_ab_free[par])
                if hb == 0 and tp == 0:
                    p.wait("tensor", tok_S_r[0])
                for ib in range(KI):
                    p.op("tensor", lambda e, par=par, ib=ib, a_ap=a_ap: e.matmul(
                        a_ap, w1t_sb[par][:, ib * 128:(ib + 1) * 128],
                        xn_h[:, ib * 512:(ib + 1) * 512],
                        start=(ib == 0), stop=(ib == KI - 1)))
                for ib in range(KI):
                    tkb = p.op("tensor", lambda e, par=par, ib=ib, b_ap=b_ap: e.matmul(
                        b_ap, w3t_sb[par][:, ib * 128:(ib + 1) * 128],
                        xn_h[:, ib * 512:(ib + 1) * 512],
                        start=(ib == 0), stop=(ib == KI - 1)),
                        inc=[(Epe, 1)] if ib == KI - 1 else None)
                tok_w1sb[par] = tkb[0]
                p.wait("scalar", tkb[0])
                if tok_ha[par] is not None:
                    p.wait("scalar", tok_ha[par])
                tks = p.op("scalar", lambda e, par=par, a_ap=a_ap: e.activation(
                    ha_st[par][:], a_ap, AF.Silu, bias=0.0, scale=1.0),
                    inc=[(Esc, 1)])[0]
                p.wait("vector", tks)
                if tok_hact_r is not None and hh < 2:
                    p.wait("vector", tok_hact_r)
                tkv = p.op("vector", lambda e, par=par, hh=hh, b_ap=b_ap: e.tensor_mul(
                    h_act[:, hh * 512:(hh + 1) * 512], ha_st[par][:], b_ap),
                    inc=[(Eve, 1)])[0]
                tok_ha[par] = tkv
                tok_ab_free[par] = tkv
            tok_hact_ready = tkv

            for jb in range(32):
                par = jb % 2
                p.wait("sync", tok_ag_w2)
                if jb == 0:
                    p.wait("sync", tok_xtm[0], tok_xtm[1])
                if tok_w2sb[par] is not None:
                    p.wait("sync", tok_w2sb[par])
                ldw2 = dma("sync", w2t_sb[par][:],
                           w2t_out[jb, :, hhalf * 43 * 128:(hhalf * 43 + 43) * 128],
                           Lw2[par])
                p.wait("tensor", ldw2, tok_hact_ready)
                if tok_g_free[par] is not None:
                    p.wait("tensor", tok_g_free[par])
                if jb < 2 and tp == 0 and hhalf == 0:
                    p.wait("tensor", tok_ctx_free[0], tok_rs_free[0], tok_ctx_free[1], tok_rs_free[1])
                for kq in range(43):
                    tk2 = p.op("tensor", lambda e, par=par, kq=kq: e.matmul(
                        g_ap[par], w2t_sb[par][:, kq * 128:(kq + 1) * 128],
                        h_act[:, kq * 512:(kq + 1) * 512],
                        start=(kq == 0), stop=(kq == 42)),
                        inc=[(Epe, 1)] if kq == 42 else None)
                tok_w2sb[par] = tk2[0]
                tok_hact_r = tk2[0]
                if hhalf == 0:
                    p.wait("vector", tk2[0])
                    if tok_gpst[par] is not None:
                        p.wait("vector", tok_gpst[par])
                    if tok_gpart[jb] is not None:
                        p.wait("vector", tok_gpart[jb])
                    tkv = p.op("vector", lambda e, par=par: e.tensor_copy(
                        gp_st[par][:], g_ap[par]), inc=[(Eve, 1)])[0]
                    tok_g_free[par] = tkv
                    p.wait("gpsimd", tkv)
                    tok_gpst[par] = dma("gpsimd", gpart_scr[jb * 128:(jb + 1) * 128, :],
                                        gp_st[par][:], Wgp[par])
                    tok_gpart[jb] = tok_gpst[par]
                else:
                    if tok_sumst[par] is not None:
                        p.wait("scalar", tok_sumst[par])
                    p.wait("scalar", tok_gpart[jb], tok_gpst[par])
                    ldgp = dma("scalar", gp_st[par][:],
                               gpart_scr[jb * 128:(jb + 1) * 128, :], Lgp[par])
                    p.wait("vector", tk2[0], ldgp)
                    if tok_sumst[par] is not None:
                        p.wait("vector", tok_sumst[par])
                    tkv = p.op("vector", lambda e, par=par: e.tensor_add(
                        sum_st[par][:], gp_st[par][:], g_ap[par]), inc=[(Eve, 1)])[0]
                    tok_g_free[par] = tkv
                    tok_gpart[jb] = tkv
                    p.wait("scalar", tkv)
                    if tok_gst[par] is not None:
                        p.wait("scalar", tok_gst[par])
                    tks = p.op("scalar", lambda e, par=par: e.activation(
                        g_st[par][:], sum_st[par][:], AF.Silu, bias=0.0, scale=1.0),
                        inc=[(Esc, 1)])[0]
                    tok_sumst[par] = tks
                    p.wait("gpsimd", tks)
                    tok_gst[par] = dma("gpsimd", g_scr[jb * 128:(jb + 1) * 128,
                                                       tp * 512:(tp + 1) * 512],
                                       g_st[par][:], Wg[par])
                    tok_gscr[(jb, tp)] = tok_gst[par]
        tok_xnh_cons = tok_hact_r

    ctx_t = [_carve(AR_BIG, 0 + i * 1024, BF16, (128, 512)) for i in range(3)]
    wo_t = [_carve(AR_BIG, 3072 + i * 1536, BF16, (128, 768)) for i in range(3)]
    gg_t = [_carve(AR_BIG, 7680 + i * 1024, BF16, (128, 512)) for i in range(2)]
    sg_st = [_carve(AR_BIG, 9728 + i * 1024, BF16, (128, 512)) for i in range(2)]

    jgroups = [list(range(6 * i, min(6 * i + 6, 32))) for i in range(6)]
    sbanks = [pbA[:, 0:512], pbA[:, 512:1024], pbB[:, 0:512], pbB[:, 512:1024],
              pb4[:], pb5[:]]
    tok_sbank_free = [None] * 6
    tok_ctxt = [None, None, None]
    tok_wot = [None, None, None]
    tok_ggt = [None, None]
    tok_sgst = [None, None]
    tok_sg_last = None
    phase_gate()
    p.wait("sync", tok_cc_ctx, tok_ag_wo)

    for th in range(2):
        for gi, jcs in enumerate(jgroups):
            for ib in range(KI):
                par = ib % 3
                if tok_ctxt[par] is not None:
                    p.wait("sync", tok_ctxt[par])
                ldc = dma("sync", ctx_t[par][:],
                          a2a_out[ib * 128:(ib + 1) * 128, th * 512:(th + 1) * 512],
                          Lctx[par])
                if tok_wot[par] is not None:
                    p.wait("scalar", tok_wot[par])
                ldw = dma("scalar", wo_t[par][:, 0:len(jcs) * 128],
                          wo_out[ib, :, jcs[0] * 128:(jcs[0] + len(jcs)) * 128],
                          Lwo[par])
                p.wait("tensor", ldc, ldw)
                if ib == 0:
                    for idx in range(len(jcs)):
                        if tok_sbank_free[idx] is not None:
                            p.wait("tensor", tok_sbank_free[idx])
                    if th == 0 and gi == 0:
                        p.wait("tensor", tok_g_free[0], tok_g_free[1],
                               tok_ab_free[0], tok_ab_free[1])
                for idx in range(len(jcs)):
                    tko = p.op("tensor", lambda e, idx=idx, par=par, ib=ib: e.matmul(
                        sbanks[idx], wo_t[par][:, idx * 128:(idx + 1) * 128],
                        ctx_t[par][:], start=(ib == 0), stop=(ib == KI - 1)),
                        inc=[(Epe, 1)] if (idx == len(jcs) - 1) else None)
                tok_ctxt[par] = tko[0]
                tok_wot[par] = tko[0]
            tok_omm = tko[0]
            for idx, jc in enumerate(jcs):
                par = (gi * 6 + idx) % 2
                if tok_ggt[par] is not None:
                    p.wait("scalar", tok_ggt[par])
                ldg = dma("scalar", gg_t[par][:],
                          g_scr[jc * 128:(jc + 1) * 128, th * 512:(th + 1) * 512],
                          Lgg[par])
                p.wait("vector", tok_omm, ldg)
                if tok_sgst[par] is not None:
                    p.wait("vector", tok_sgst[par])
                tkv = p.op("vector", lambda e, idx=idx, par=par: e.tensor_mul(
                    sg_st[par][:], sbanks[idx], gg_t[par][:]), inc=[(Eve, 1)])[0]
                tok_sbank_free[idx] = tkv
                tok_ggt[par] = tkv
                p.wait("gpsimd", tkv)
                tok_sgst[par] = dma("gpsimd", sg_scr[jc * 128:(jc + 1) * 128,
                                                     th * 512:(th + 1) * 512],
                                    sg_st[par][:], Wsg[par])
                tok_sg_last = tok_sgst[par]

    sgt_t = [_carve(AR_SM, 17408 + i * 256, BF16, (128, 128)) for i in range(2)]
    out_st = [_carve(AR_BIG, 11776 + i * 16 * 1024, F32, (128, 4096)) for i in range(2)]
    tok_out = [None, None]
    tok_sgt = [None, None]
    tok_outwr = None
    phase_gate()

    for ts in range(8):
        par = ts % 2
        if tok_xtm[par] is not None:
            p.wait("sync", tok_xtm[par])
        ldx = dma("sync", x_tm[par][:], d_x_own[ts * 128:(ts + 1) * 128, :], Lx[par])
        if tok_out[par] is not None:
            p.wait("vector", tok_out[par])
        for grp in range(4):
            if tok_ptr_r[0] is not None:
                p.wait("tensor", tok_ptr_r[0])
            for j in range(8):
                jc = grp * 8 + j
                parj = jc % 2
                if tok_sgt[parj] is not None:
                    p.wait("scalar", tok_sgt[parj])
                ldsg = dma("scalar", sgt_t[parj][:],
                           sg_scr[jc * 128:(jc + 1) * 128, ts * 128:(ts + 1) * 128],
                           Lsg[parj])
                p.wait("tensor", ldsg)
                tktr = p.op("tensor", lambda e, j=j, parj=parj: e.transpose(
                    ptr[:, j * 128:(j + 1) * 128],
                    sgt_t[parj][:], t_ident[:]),
                    inc=[(Epe, 1)])
                tok_sgt[parj] = tktr[0]
            p.wait("vector", tktr[0], ldx)
            tkv = p.op("vector", lambda e, par=par, grp=grp: e.tensor_add(
                out_st[par][:, grp * 1024:(grp + 1) * 1024],
                ptr[:, 0:1024],
                x_tm[par][:, grp * 1024:(grp + 1) * 1024]), inc=[(Eve, 1)])[0]
            tok_ptr_r[0] = tkv
        tok_xtm[par] = tkv
        p.wait("gpsimd", tkv)
        tok_outwr = dma("gpsimd", d_out[ts * 128:(ts + 1) * 128, :], out_st[par][:],
                        Wout[par])
        tok_out[par] = tok_outwr

    tap_tok = None
    for t, dstt in taps.items():
        p.wait("gpsimd", tok_outwr)
        if t in ("a2a_out",):
            p.wait("gpsimd", tok_cc_ctx)
        tap_tok = dma("gpsimd", dstt.ap(), scr_map[t].ap(), Wtap)

    p.wait("gpsimd", tok_outwr)
    if tap_tok is not None:
        p.wait("gpsimd", tap_tok)

    p.finalize()
    es.close()
    return nc



_NC_CACHE = {}


def kernel(**inputs):
    shared, per_core = _host_prep(inputs)
    key = ()
    if key not in _NC_CACHE:
        _NC_CACHE[key] = build_program()
    nc = _NC_CACHE[key]
    in_maps = []
    for c in range(NCORE):
        m = dict(shared)
        m.update(per_core[c])
        in_maps.append(m)
    res = run_bass_kernel_spmd(nc, in_maps, core_ids=list(range(NCORE)))
    out = np.concatenate([res.results[c]["out_own"] for c in range(NCORE)], axis=0)
    return np.ascontiguousarray(out[None, ...])


# version 10
# speedup vs baseline: 1.0505x; 1.0004x over previous
import math
import numpy as np
import ml_dtypes
import concourse.bass as bass
import concourse.mybir as mybir
from concourse.bass_utils import run_bass_kernel_spmd
from concourse.aot_env import allow_ici_all_to_all
from contextlib import ExitStack

F32 = mybir.dt.float32
BF16 = mybir.dt.bfloat16
I8 = mybir.dt.int8
AF = mybir.ActivationFunctionType
ALU = mybir.AluOpType
BF = ml_dtypes.bfloat16

S, DM, HF = 8192, 4096, 11008
NCORE = 8
TOWN = S // NCORE
KI = DM // 128
HB = HF // 128
HBP = 88
ISQ = 1.0 / math.sqrt(128.0)
CSHIFT = 25.0
EPS_LN = 1e-5
EPS_RS = 1e-30
NTT = S // 512
W1SLOT = HBP // NCORE
W2SLOT = 32 // NCORE

ENGS = ("sync", "tensor", "vector", "scalar", "gpsimd")
CVALS = [0.0, 1.0, -0.5, EPS_LN, ISQ, -CSHIFT, -1.0 / DM, 1.0 / DM]



def _payload_bytes(w):
    w = np.ascontiguousarray(np.asarray(w))
    return np.ascontiguousarray(w.view(np.int8).reshape(w.shape + (4,))[..., 0])


def _host_prep(inputs):
    x = np.ascontiguousarray(np.asarray(inputs["L0_attn_hidden_states"])[0])
    qb_q = _payload_bytes(inputs["L0_attn_qw_q"])
    qb_k = _payload_bytes(inputs["L0_attn_qw_k"])
    qb_v = _payload_bytes(inputs["L0_attn_qw_v"])
    qb_o = _payload_bytes(inputs["L0_attn_qw_o"])
    sc_q = np.ascontiguousarray(np.asarray(inputs["L0_attn_scales_q"], dtype=np.float32))
    sc_k = np.ascontiguousarray(np.asarray(inputs["L0_attn_scales_k"], dtype=np.float32))
    sc_v = np.ascontiguousarray(np.asarray(inputs["L0_attn_scales_v"], dtype=np.float32))
    sc_o = np.ascontiguousarray(np.asarray(inputs["L0_attn_scales_o"], dtype=np.float32))

    def ffn1_layout(w, s):
        pb = _payload_bytes(w)
        pl = np.ascontiguousarray(
            pb.reshape(HF, 32, 64).reshape(HB, 128, 32, 64).transpose(0, 3, 2, 1))
        sb = np.asarray(s, dtype=np.float32).reshape(HF, 32, 2)
        sb = np.ascontiguousarray(
            sb.reshape(HB, 128, 32, 2).transpose(0, 3, 2, 1))
        return pl, sb

    def ffn2_layout(w, s):
        pb = _payload_bytes(w)
        pl = np.ascontiguousarray(
            pb.reshape(DM, HB, 64).reshape(32, 128, HB, 64).transpose(0, 3, 2, 1))
        sb = np.asarray(s, dtype=np.float32).reshape(DM, HB, 2)
        sb = np.ascontiguousarray(
            sb.reshape(32, 128, HB, 2).transpose(0, 3, 2, 1))
        return pl, sb

    p1, sb1 = ffn1_layout(inputs["L0_ffn_w1"], inputs["L0_ffn_s1"])
    p3, sb3 = ffn1_layout(inputs["L0_ffn_w3"], inputs["L0_ffn_s3"])
    p2, sb2 = ffn2_layout(inputs["L0_ffn_w2"], inputs["L0_ffn_s2"])

    inv = 1.0 / (10000.0 ** (np.arange(0, 128, 2, dtype=np.float64) / 128.0))
    ang = np.arange(S, dtype=np.float64)[None, :] * inv[:, None]
    cosT = np.empty((128, S), np.float32)
    sinT2 = np.empty((128, S), np.float32)
    cosT[:64] = np.cos(ang)
    cosT[64:] = np.cos(ang)
    sinT2[:64] = -np.sin(ang)
    sinT2[64:] = np.sin(ang)

    E = np.zeros((32, 32 * 128), dtype=BF)
    for b in range(32):
        E[b, b * 128:(b + 1) * 128] = 1.0
    sel2 = np.zeros((34, 128), dtype=BF)
    for b0 in (0, 32):
        sel2[b0, :64] = 1.0 / 16.0
        sel2[b0 + 1, 64:] = 1.0
    ones1 = np.ones((1, 128), np.float32)
    onespv = np.ones((128, 1), dtype=BF)
    ident = np.eye(128, dtype=BF)
    masks = np.zeros((128, 4 * 512), dtype=BF)
    pg = np.arange(128)[:, None]
    jg = np.arange(512)[None, :]
    for i in range(4):
        masks[:, i * 512:(i + 1) * 512] = (pg <= jg - 128 * i).astype(BF)

    shared = dict(x_all=x, cosT=cosT, sinT2=sinT2, E=E, sel2=sel2, ones1=ones1,
                  ones1b=np.ones((33, 128), BF), onespv=onespv, ident=ident, masks=masks)
    per_core = []
    for c in range(NCORE):
        qcols = slice(4 * c * 128, 4 * c * 128 + 512)
        kcols = slice(c * 128, (c + 1) * 128)
        d = dict(
            x_own=np.ascontiguousarray(x[c * TOWN:(c + 1) * TOWN]),
            qb_q_own=np.ascontiguousarray(qb_q[:, qcols]),
            sc_q_own=np.ascontiguousarray(sc_q[:, qcols]),
            qb_kv_own=np.ascontiguousarray(
                np.concatenate([qb_k[:, kcols], qb_v[:, kcols]], axis=1)),
            sc_kv_own=np.ascontiguousarray(
                np.concatenate([sc_k[:, kcols], sc_v[:, kcols]], axis=1)),
            qb_o_own=np.ascontiguousarray(qb_o[4 * c * 128:4 * (c + 1) * 128, :]),
            sc_o_own=np.ascontiguousarray(sc_o[4 * c:4 * (c + 1), :]),
            p2_own=np.ascontiguousarray(p2[W2SLOT * c:W2SLOT * (c + 1)]),
            sb2_own=np.ascontiguousarray(sb2[W2SLOT * c:W2SLOT * (c + 1)]),
        )
        p1c = np.zeros((W1SLOT,) + p1.shape[1:], p1.dtype)
        sb1c = np.zeros((W1SLOT,) + sb1.shape[1:], sb1.dtype)
        p3c = np.zeros_like(p1c)
        sb3c = np.zeros_like(sb1c)
        for i, sl in enumerate(range(W1SLOT * c, W1SLOT * (c + 1))):
            if sl < HB:
                p1c[i], sb1c[i] = p1[sl], sb1[sl]
                p3c[i], sb3c[i] = p3[sl], sb3[sl]
        d["p1_own"], d["sb1_own"] = p1c, sb1c
        d["p3_own"], d["sb3_own"] = p3c, sb3c
        per_core.append(d)
    return shared, per_core



class _Sem:
    __slots__ = ("h", "n", "name")

    def __init__(self, h, name):
        self.h, self.n, self.name = h, 0, name


class Plan:
    def __init__(self, nc, es):
        self.nc = nc
        self.es = es
        self.th = {e: [] for e in ENGS}
        self.waited = {e: {} for e in ENGS}
        self.nops = {e: 0 for e in ENGS}
        self.dma_sems = set()
        self.all_dsems = []

    def sem(self, name):
        return _Sem(self.es.enter_context(self.nc.semaphore(name)), name)

    def dsem(self, name):
        s = self.sem(name)
        self.all_dsems.append(s)
        return s

    def op(self, eng, fn, inc=None):
        toks = []
        incs = []
        if inc:
            for s, k in inc:
                s.n += k
                incs.append((s, k))
                toks.append((s, s.n))

        def thunk(e, fn=fn, incs=incs):
            ins = fn(e)
            for s, k in incs:
                ins.then_inc(s.h, k)

        self.th[eng].append(thunk)
        self.nops[eng] += 1
        return toks

    def wait(self, eng, *tokens):
        agg = {}
        for t in tokens:
            if t is None:
                continue
            s, v = t
            if s.name not in agg or agg[s.name][1] < v:
                agg[s.name] = (s, v)
        for s, v in agg.values():
            if v <= self.waited[eng].get(s.name, 0):
                continue
            self.waited[eng][s.name] = v

            def thunk(e, h=s.h, v=v):
                e.wait_ge(h, v)

            self.th[eng].append(thunk)

    def finalize(self):
        with self.nc.Block() as block:
            for name, reg in (("sync", block.sync), ("tensor", block.tensor),
                              ("vector", block.vector), ("scalar", block.scalar),
                              ("gpsimd", block.gpsimd)):
                thunks = self.th[name]

                def runner(e, thunks=thunks):
                    for t in thunks:
                        t(e)

                reg(runner)


def _carve(arena, off_bytes, dt, shape):
    esz = mybir.dt.size(dt)
    per_part = int(np.prod(shape[1:])) * esz
    apx = arena[:, off_bytes:off_bytes + per_part].bitcast(dt)
    if len(shape) > 2:
        dims = " ".join(f"d{i}" for i in range(1, len(shape)))
        apx = apx.rearrange(f"p ({dims}) -> p {dims}",
                            **{f"d{i}": shape[i] for i in range(1, len(shape))})
    if shape[0] != 128:
        apx = apx[0:shape[0]]
    return apx



def build_program(debug_taps=()):
    nc = bass.Bass("TRN2", target_bir_lowering=False)
    es = ExitStack()

    def inp(name, shape, dt):
        return nc.dram_tensor(name, list(shape), dt, kind="ExternalInput")

    d_x_all = inp("x_all", (S, DM), F32)
    d_x_own = inp("x_own", (TOWN, DM), F32)
    d_qbq = inp("qb_q_own", (DM, 512), I8)
    d_scq = inp("sc_q_own", (32, 512), F32)
    d_qbkv = inp("qb_kv_own", (DM, 256), I8)
    d_sckv = inp("sc_kv_own", (32, 256), F32)
    d_qbo = inp("qb_o_own", (512, DM), I8)
    d_sco = inp("sc_o_own", (4, DM), F32)
    d_p1 = inp("p1_own", (W1SLOT, 64, 32, 128), I8)
    d_sb1 = inp("sb1_own", (W1SLOT, 2, 32, 128), F32)
    d_p3 = inp("p3_own", (W1SLOT, 64, 32, 128), I8)
    d_sb3 = inp("sb3_own", (W1SLOT, 2, 32, 128), F32)
    d_p2 = inp("p2_own", (W2SLOT, 64, HB, 128), I8)
    d_sb2 = inp("sb2_own", (W2SLOT, 2, HB, 128), F32)
    d_cos = inp("cosT", (128, S), F32)
    d_sin = inp("sinT2", (128, S), F32)
    d_E = inp("E", (32, 32 * 128), BF16)
    d_sel2 = inp("sel2", (34, 128), BF16)
    d_ones1 = inp("ones1", (1, 128), F32)
    d_ones1b = inp("ones1b", (33, 128), BF16)
    d_onespv = inp("onespv", (128, 1), BF16)
    d_ident = inp("ident", (128, 128), BF16)
    d_masks = inp("masks", (128, 4 * 512), BF16)
    d_out = nc.dram_tensor("out_own", [TOWN, DM], F32, kind="ExternalOutput")

    q_scr = nc.dram_tensor("q_scr", [4, 128, S], BF16)
    w1t_in = nc.dram_tensor("w1t_in", [W1SLOT, 128, 4096], BF16)
    w3t_in = nc.dram_tensor("w3t_in", [W1SLOT, 128, 4096], BF16)
    w2t_in = nc.dram_tensor("w2t_in", [W2SLOT, 128, HF], BF16)
    wo_in = nc.dram_tensor("wo_in", [W2SLOT, 128, 4096], BF16)
    w1t_out = nc.dram_tensor("w1t_out", [HBP, 128, 4096], BF16, addr_space="Shared")
    w3t_out = nc.dram_tensor("w3t_out", [HBP, 128, 4096], BF16, addr_space="Shared")
    w2t_out = nc.dram_tensor("w2t_out", [32, 128, HF], BF16, addr_space="Shared")
    wo_out = nc.dram_tensor("wo_out", [32, 128, 4096], BF16, addr_space="Shared")
    a2a_in = nc.dram_tensor("a2a_in", [DM, TOWN], BF16)
    a2a_out = nc.dram_tensor("a2a_out", [DM, TOWN], BF16)
    g_scr = nc.dram_tensor("g_scr", [DM, TOWN], BF16)
    gpart_scr = nc.dram_tensor("gpart_scr", [DM, 512], F32)
    sg_scr = nc.dram_tensor("sg_scr", [DM, TOWN], BF16)
    scr_map = {"q_scr": q_scr, "a2a_in": a2a_in,
               "a2a_out": a2a_out, "g_scr": g_scr, "sg_scr": sg_scr,
               "w1t_out": w1t_out, "w3t_out": w3t_out, "w2t_out": w2t_out,
               "wo_out": wo_out}
    taps = {t: nc.dram_tensor("tap_" + t, list(scr_map[t].shape), scr_map[t].dtype,
                              kind="ExternalOutput") for t in debug_taps}

    sb = lambda name, shape, dt: es.enter_context(nc.sbuf_tensor(name, list(shape), dt))
    t_sel2 = sb("t_sel2", (34, 128), BF16)
    t_ones1 = sb("t_ones1", (1, 128), F32)
    t_ones1b = sb("t_ones1b", (33, 128), BF16)
    t_onespv = sb("t_onespv", (128, 1), BF16)
    t_ident = sb("t_ident", (128, 128), BF16)
    t_masks = sb("t_masks", (128, 4 * 512), BF16)
    t_cpool = sb("t_cpool", (128, 8), F32)
    t_stat = sb("t_stat", (128, 32), F32)
    t_bn = sb("t_bn", (128, 96), F32)
    t_rrow = sb("t_rrow", (1, 1024), F32)
    t_mand = sb("t_mand", (128, 1), mybir.dt.int32)
    t_mxor = sb("t_mxor", (128, 1), mybir.dt.int32)
    t_moff = sb("t_moff", (128, 1), F32)

    AR_W = sb("AR_W", (128, 48 * 1024), I8)
    AR_X = sb("AR_X", (128, 32 * 1024), I8)
    AR_XN = sb("AR_XN", (128, 48 * 1024), I8)
    AR_BIG = sb("AR_BIG", (128, 44 * 1024), I8)
    AR_SM = sb("AR_SM", (128, 26112), I8)
    t_E = _carve(AR_BIG, 0, BF16, (32, 32 * 128))
    t_scq = _carve(AR_BIG, 8 * 1024, F32, (32, 512))
    t_sckv = _carve(AR_BIG, 10 * 1024, F32, (32, 256))
    t_scqb = _carve(AR_BIG, 11 * 1024, BF16, (32, 512))
    t_sckvb = _carve(AR_BIG, 12 * 1024, BF16, (32, 256))

    for i, v in enumerate(CVALS):
        nc.const_aps.register(F32, v, t_cpool[:, i:i + 1])

    pbA = es.enter_context(nc.psum_tensor("pbA", [128, 1024], F32))
    pbB = es.enter_context(nc.psum_tensor("pbB", [128, 1024], F32))
    pbC = es.enter_context(nc.psum_tensor("pbC", [128, 512], F32))
    pbRS = es.enter_context(nc.psum_tensor("pbRS", [128, 512], F32))
    pb4 = es.enter_context(nc.psum_tensor("pb4", [128, 512], F32))
    pb5 = es.enter_context(nc.psum_tensor("pb5", [128, 512], F32))
    ptr = pbC.ap().bitcast(BF16)

    p = Plan(nc, es)
    Epe, Eve, Esc, Epo = p.sem("Epe"), p.sem("Eve"), p.sem("Esc"), p.sem("Epo")
    Ccc = p.sem("Ccc")

    def dma(q, out, in_, s):
        return p.op(q, lambda e, o=out, i=in_: e.dma_start(out=o, in_=i),
                    inc=[(s, 16)])[0]

    def phase_gate():
        toks = [(s, s.n) for s in p.all_dsems if s.n > 0]
        toks += [(s, s.n) for s in (Epe, Eve, Esc, Epo) if s.n > 0]
        for e in ENGS:
            p.wait(e, *toks)

    Lco = p.dsem("Lco")
    Lq = [p.dsem("Lq0"), p.dsem("Lq1")]
    Lx = [p.dsem("Lx0"), p.dsem("Lx1")]
    Lcs = [p.dsem("Lcs0"), p.dsem("Lcs1")]
    Wq = [p.dsem("Wq0"), p.dsem("Wq1")]
    Wk = p.dsem("Wk")
    Wv = p.dsem("Wv")
    LKV = p.dsem("LKV")
    Lqt = [p.dsem("Lqt0"), p.dsem("Lqt1")]
    Lpy = [p.dsem("Lpy0"), p.dsem("Lpy1")]
    Lsb = [p.dsem("Lsb0"), p.dsem("Lsb1")]
    Wctx = [p.dsem("Wctx0"), p.dsem("Wctx1")]
    Wde = [p.dsem("Wde0"), p.dsem("Wde1")]
    Lw1 = [p.dsem("Lw10"), p.dsem("Lw11")]
    Lw2 = [p.dsem("Lw20"), p.dsem("Lw21")]
    Lgp = [p.dsem("Lgp0"), p.dsem("Lgp1")]
    Wg = [p.dsem("Wg0"), p.dsem("Wg1")]
    Wgp = [p.dsem("Wgp0"), p.dsem("Wgp1")]
    Lctx = [p.dsem("Lctx0"), p.dsem("Lctx1"), p.dsem("Lctx2")]
    Lwo = [p.dsem("Lwo0"), p.dsem("Lwo1"), p.dsem("Lwo2")]
    Lgg = [p.dsem("Lgg0"), p.dsem("Lgg1")]
    Wsg = [p.dsem("Wsg0"), p.dsem("Wsg1")]
    Lsg = [p.dsem("Lsg0"), p.dsem("Lsg1")]
    Wout = [p.dsem("Wout0"), p.dsem("Wout1")]
    Wtap = p.dsem("Wtap")

    tk_consts = {}
    for nm, out, in_ in (("E", t_E[:], d_E[:]), ("sel2", t_sel2[:], d_sel2[:]),
                         ("ones1", t_ones1[:], d_ones1[:]),
                         ("ones1b", t_ones1b[:], d_ones1b[:]),
                         ("onespv", t_onespv[:], d_onespv[:]),
                         ("ident", t_ident[:], d_ident[:]),
                         ("masks", t_masks[:], d_masks[:]),
                         ("scq", t_scq[:], d_scq[:]), ("sckv", t_sckv[:], d_sckv[:])):
        tk_consts[nm] = dma("sync", out, in_, Lco)
    for nm in tk_consts:
        tk_consts[nm] = (Lco, Lco.n)

    for i, v in enumerate(CVALS):
        tk = p.op("vector", lambda e, i=i, v=v: e.memset(t_cpool[:, i:i + 1], v),
                  inc=[(Eve, 1)] if i == len(CVALS) - 1 else None)
    tok_cpool = tk[0]
    p.op("vector", lambda e: e.memset(t_mand[0:64, :], -16))
    p.op("vector", lambda e: e.memset(t_mand[64:128, :], 15))
    p.op("vector", lambda e: e.memset(t_mxor[0:64, :], 0))
    p.op("vector", lambda e: e.memset(t_mxor[64:128, :], 8))
    p.op("vector", lambda e: e.memset(t_moff[0:64, :], 0.0))
    tok_mconst = p.op("vector", lambda e: e.memset(t_moff[64:128, :], -8.0),
                      inc=[(Eve, 1)])[0]
    p.wait("vector", tk_consts["scq"], tk_consts["sckv"])
    p.op("vector", lambda e: e.tensor_copy(t_scqb[:], t_scq[:]))
    tok_scb = p.op("vector", lambda e: e.tensor_copy(t_sckvb[:], t_sckv[:]),
                   inc=[(Eve, 1)])[0]

    wq_sb = _carve(AR_W, 0, BF16, (128, 32 * 512))
    wkv_sb = _carve(AR_W, 32 * 1024, BF16, (128, 32 * 256))
    qb_st = [_carve(AR_SM, i * 512, I8, (128, 512)) for i in range(2)]
    kv_st = [_carve(AR_SM, 1024 + i * 256, I8, (128, 256)) for i in range(2)]

    p.wait("tensor", tk_consts["E"], tok_scb)
    tok_p0 = [None, None]
    for ib in range(KI):
        par = ib % 2
        if tok_p0[par] is not None:
            p.wait("sync", tok_p0[par])
        ldq = dma("sync", qb_st[par][:], d_qbq[ib * 128:(ib + 1) * 128, :], Lq[par])
        ldkv = dma("sync", kv_st[par][:], d_qbkv[ib * 128:(ib + 1) * 128, :], Lq[par])
        if tok_p0[1 - par] is not None:
            p.wait("tensor", tok_p0[1 - par])
        p.op("tensor", lambda e, ib=ib: e.matmul(
            pbC[:, 0:512], t_E[:, ib * 128:(ib + 1) * 128], t_scqb[:, :],
            start=True, stop=True))
        tkB = p.op("tensor", lambda e, ib=ib: e.matmul(
            pbRS[:, 0:256], t_E[:, ib * 128:(ib + 1) * 128], t_sckvb[:, :],
            start=True, stop=True), inc=[(Epe, 1)])
        p.wait("vector", ldq, ldkv, tkB[0])
        p.op("vector", lambda e, ib=ib, par=par: e.scalar_tensor_tensor(
            wq_sb[:, ib * 512:(ib + 1) * 512], qb_st[par][:], -8.0, pbC[:, 0:512],
            op0=ALU.add, op1=ALU.mult))
        tok_p0[par] = p.op("vector", lambda e, ib=ib, par=par: e.scalar_tensor_tensor(
            wkv_sb[:, ib * 256:(ib + 1) * 256], kv_st[par][:], -8.0, pbRS[:, 0:256],
            op0=ALU.add, op1=ALU.mult), inc=[(Eve, 1)])[0]

    x_tm = [_carve(AR_X, i * 16 * 1024, F32, (128, 4096)) for i in range(2)]
    xn_tm = [_carve(AR_XN, i * 8 * 1024, BF16, (128, 4096)) for i in range(2)]
    xn_fm = _carve(AR_XN, 16 * 1024, BF16, (128, 32 * 512))
    cos_sl = [_carve(AR_SM, 2048 + i * 2048, F32, (128, 512)) for i in range(2)]
    sin_sl = [_carve(AR_SM, 6144 + i * 2048, F32, (128, 512)) for i in range(2)]
    tmpA = _carve(AR_SM, 10240, F32, (128, 512))
    tmpB = _carve(AR_SM, 12288, F32, (128, 512))
    st_q = [_carve(AR_SM, 14336 + i * 1024, BF16, (128, 512)) for i in range(2)]
    st_v = _carve(AR_SM, 16384, BF16, (128, 512))

    qb_ap = [pbA[:, 0:512], pbA[:, 512:1024], pbB[:, 0:512], pbB[:, 512:1024]]
    xn_fm3 = xn_fm.rearrange("p (ib c) -> p ib c", ib=32)

    tok_xtm = [None, None]
    tok_xntm_w = [None, None]
    tok_xntm_r = [None, None]
    tok_dump = [None, None]
    tok_bn = [None, None]
    tok_stat = [None] * 8
    tok_ptr_r = [None, None]
    tok_ptr_w = [None, None]
    tok_xn_cons = None
    tok_q_free = None
    tok_k_free = None
    tok_v_free = None
    tok_cos = [None, None]
    tok_stq = [None, None]
    tok_ropeadd = None
    tok_stk = None
    tok_stv = None
    tok_qscr = {}
    tok_kscr = {}
    tok_vscr = {}

    K_sb = _carve(AR_BIG, 0, BF16, (128, S))
    V_sb = _carve(AR_BIG, 16 * 1024, BF16, (128, S))
    PT = [_carve(AR_BIG, 32 * 1024 + i * 2048, BF16, (128, 1024)) for i in range(2)]
    q_t = [_carve(AR_BIG, 36 * 1024 + i * 1024, BF16, (128, 512)) for i in range(2)]
    R_sb = [_carve(AR_BIG, 38 * 1024, F32, (128, 512)),
            _carve(AR_BIG, 40 * 1024, F32, (128, 512))]
    ctx_st = [_carve(AR_BIG, 42 * 1024 + i * 1024, BF16, (128, 512)) for i in range(2)]
    pay_st = [_carve(AR_SM, i * 1024, I8, (128, 1024)) for i in range(2)]
    ext_st = [_carve(AR_SM, 17920 + i * 1024, I8, (128, 1024)) for i in range(2)]
    wde_st = [_carve(AR_SM, 19968 + i * 2048, BF16, (128, 1024)) for i in range(2)]
    sbrb_st = [_carve(AR_SM, 24064, BF16, (34, 1024))[i * 32:i * 32 + 2]
               for i in range(2)]

    deq_jobs = []
    for mat, (srcp, srcs, dst, ncols) in (
            ("w1", (d_p1, d_sb1, w1t_in, 4096)),
            ("w3", (d_p3, d_sb3, w3t_in, 4096)),
            ("w2", (d_p2, d_sb2, w2t_in, HF))):
        nsl = W1SLOT if ncols == 4096 else W2SLOT
        for sl in range(nsl):
            coff = 0
            while coff < ncols:
                cw = min(512, ncols - coff)
                deq_jobs.append(("ffn", srcp, srcs, dst, sl, coff, cw))
                coff += cw
    for sl in range(W2SLOT):
        for coff in range(0, 4096, 512):
            deq_jobs.append(("wo", d_qbo, d_sco, wo_in, sl, coff, 512))
    n_w1 = sum(1 for j in deq_jobs if j[3] is w1t_in)
    n_w3 = sum(1 for j in deq_jobs if j[3] is w3t_in)
    n_w2 = sum(1 for j in deq_jobs if j[3] is w2t_in)
    ag_after = {n_w1 - 1: ("w1", w1t_in, w1t_out),
                n_w1 + n_w3 - 1: ("w3", w3t_in, w3t_out),
                n_w1 + n_w3 + n_w2 - 1: ("w2", w2t_in, w2t_out),
                len(deq_jobs) - 1: ("wo", wo_in, wo_out)}

    deq_state = dict(idx=0, ld_idx=0, tok_pay=[None, None], tok_ext=[None, None],
                     tok_wde=[None, None], tok_sbr=[None, None], pend={},
                     tok_B=None, tok_Bcons=None, last_wr=None, ag_toks=[])

    def deq_load(i):
        ds = deq_state
        kind, srcp, srcs, dst, sl, coff, cw = deq_jobs[i]
        par = i % 2
        if i < 2:
            p.wait("scalar", tok_p0[0], tok_p0[1])
        if ds["tok_ext"][par] is not None:
            p.wait("scalar", ds["tok_ext"][par])
        if ds["tok_sbr"][par] is not None:
            p.wait("gpsimd", ds["tok_sbr"][par])
        if kind == "ffn":
            srcp2 = srcp.ap().rearrange("s e k h -> s e (k h)")
            t1 = dma("scalar", pay_st[par][0:64, 0:cw], srcp2[sl, :, coff:coff + cw],
                     Lpy[par])
            t2 = dma("scalar", pay_st[par][64:128, 0:cw], srcp2[sl, :, coff:coff + cw],
                     Lpy[par])
            srcs2 = srcs.ap().rearrange("s h k j -> s h (k j)")
            t3 = dma("gpsimd", sbrb_st[par][:, 0:cw], srcs2[sl, :, coff:coff + cw],
                     Lsb[par])
        else:
            t2 = dma("scalar", pay_st[par][:, 0:cw],
                     srcp[sl * 128:(sl + 1) * 128, coff:coff + cw], Lpy[par])
            t3 = dma("gpsimd", sbrb_st[par][0:1, 0:cw], srcs[sl:sl + 1, coff:coff + cw],
                     Lsb[par])
        ds["pend"][i] = (t2, t3)

    def deq_compute(i):
        ds = deq_state
        kind, srcp, srcs, dst, sl, coff, cw = deq_jobs[i]
        par = i % 2
        t2, t3 = ds["pend"].pop(i)
        if kind == "ffn":
            p.wait("tensor", t3, tk_consts["sel2"])
            if ds["tok_Bcons"] is not None:
                p.wait("tensor", ds["tok_Bcons"])
            tkB = p.op("tensor", lambda e, par=par, cw=cw: e.matmul(
                pbC[:, 0:cw], t_sel2[par * 32:par * 32 + 2, :],
                sbrb_st[par][:, 0:cw], start=True, stop=True), inc=[(Epe, 1)])
            ds["tok_B"] = tkB[0]
            ds["tok_sbr"][par] = tkB[0]
            p.wait("vector", t2, tok_mconst)
            if ds["tok_wde"][par] is not None:
                p.wait("vector", ds["tok_wde"][par])
            tke = p.op("vector", lambda e, par=par, cw=cw: e.tensor_scalar(
                ext_st[par][:, 0:cw], pay_st[par][:, 0:cw], t_mand[:, 0:1],
                t_mxor[:, 0:1], op0=ALU.bitwise_and, op1=ALU.bitwise_xor),
                inc=[(Eve, 1)])[0]
            ds["tok_pay"][par] = tke
            ds["tok_ext"][par] = tke
            p.wait("vector", tke, ds["tok_B"])
            tkm = p.op("vector", lambda e, par=par, cw=cw: e.scalar_tensor_tensor(
                wde_st[par][:, 0:cw], ext_st[par][:, 0:cw], t_moff[:, 0:1],
                pbC[:, 0:cw], op0=ALU.add, op1=ALU.mult),
                inc=[(Eve, 1)])[0]
            ds["tok_Bcons"] = tkm
        else:
            p.wait("tensor", t3, tk_consts["ones1b"])
            if ds["tok_Bcons"] is not None:
                p.wait("tensor", ds["tok_Bcons"])
            tkB = p.op("tensor", lambda e, par=par, cw=cw: e.matmul(
                pbC[:, 0:cw], t_ones1b[par * 32:par * 32 + 1, :],
                sbrb_st[par][0:1, 0:cw], start=True, stop=True), inc=[(Epe, 1)])
            ds["tok_B"] = tkB[0]
            ds["tok_sbr"][par] = tkB[0]
            p.wait("vector", t2, ds["tok_B"])
            if ds["tok_wde"][par] is not None:
                p.wait("vector", ds["tok_wde"][par])
            tkm = p.op("vector", lambda e, par=par, cw=cw: e.scalar_tensor_tensor(
                wde_st[par][:, 0:cw], pay_st[par][:, 0:cw], -8.0, pbC[:, 0:cw],
                op0=ALU.add, op1=ALU.mult), inc=[(Eve, 1)])[0]
            ds["tok_ext"][par] = tkm
            ds["tok_Bcons"] = tkm
        p.wait("gpsimd", tkm)
        twr = dma("gpsimd", dst[sl, :, coff:coff + cw], wde_st[par][:, 0:cw], Wde[par])
        ds["tok_wde"][par] = twr
        ds["last_wr"] = twr
        if i in ag_after:
            nm, src_, dstt = ag_after[i]
            p.wait("gpsimd", (Wde[0], Wde[0].n), (Wde[1], Wde[1].n))
            ds["ag_toks"].append(p.op("gpsimd", lambda e, src_=src_, dstt=dstt:
                                      e.collective_compute(
                "AllGather", ALU.bypass, replica_groups=[list(range(NCORE))],
                ins=[src_.ap().opt()], outs=[dstt.ap().opt()]), inc=[(Ccc, 1)])[0])

    def emit_deq_chunk():
        ds = deq_state
        if ds.get("extra_gen") is not None:
            try:
                next(ds["extra_gen"])
            except StopIteration:
                ds["extra_gen"] = None
        while ds["ld_idx"] < len(deq_jobs) and ds["ld_idx"] - ds["idx"] < 2:
            deq_load(ds["ld_idx"])
            ds["ld_idx"] += 1
        if ds["idx"] < ds["ld_idx"]:
            deq_compute(ds["idx"])
            ds["idx"] += 1

    tok_qt = [None, None]
    tok_PT_r = [None, None]
    tok_PT_w = [None, None]
    tok_S_r = [None, None]
    tok_S_w = [None, None]
    tok_ctx_free = [None, None]
    tok_rs_free = [None, None]
    tok_R_cons = None
    tok_ctxst = [None, None]
    tok_rrow = [None, None]
    tok_a2a_last = None
    tok_attn_mm = None

    tok_kv_ready = [None] * NTT
    rs_ap = [pbRS[0:1, 0:512], pbRS[32:33, 0:512]]
    ctx_ap = [pb4, pb5]
    Sb_ap = [pbA, pbB]

    def emit_S(ch, pi, tb, ldq):
        Sb = Sb_ap[ch]
        u0, u1 = 2 * pi, 2 * pi + 1
        p.wait("tensor", ldq, tok_kv_ready[tb])
        if tok_S_r[ch] is not None:
            p.wait("tensor", tok_S_r[ch])
        p.op("tensor", lambda e, Sb=Sb, u0=u0, ch=ch: e.matmul(
            Sb[:, 0:512], K_sb[:, u0 * 128:(u0 + 1) * 128], q_t[ch][:],
            start=True, stop=True))
        tkS = p.op("tensor", lambda e, Sb=Sb, u1=u1, ch=ch: e.matmul(
            Sb[:, 512:1024], K_sb[:, u1 * 128:(u1 + 1) * 128], q_t[ch][:],
            start=True, stop=True), inc=[(Epe, 1)])[0]
        tok_S_w[ch] = tkS
        p.wait("scalar", tkS)
        if tok_PT_r[ch] is not None:
            p.wait("scalar", tok_PT_r[ch])
        tkE = p.op("scalar", lambda e, Sb=Sb, ch=ch: e.activation(
            PT[ch][:], Sb[:], AF.Exp, bias=-CSHIFT, scale=ISQ),
            inc=[(Esc, 1)])[0]
        tok_S_r[ch] = tkE
        tok_PT_w[ch] = tkE
        npair = 2 * tb + 2
        if pi >= npair - 2:
            i0 = u0 - 4 * tb
            p.wait("vector", tkE, tk_consts["masks"])
            p.op("vector", lambda e, ch=ch, i0=i0: e.tensor_mul(
                PT[ch][:, 0:512], PT[ch][:, 0:512],
                t_masks[:, i0 * 512:(i0 + 1) * 512]))
            tkM = p.op("vector", lambda e, ch=ch, i0=i0: e.tensor_mul(
                PT[ch][:, 512:1024], PT[ch][:, 512:1024],
                t_masks[:, (i0 + 1) * 512:(i0 + 2) * 512]), inc=[(Eve, 1)])[0]
            tok_PT_w[ch] = tkM

    def emit_PV(ch, pi, tb):
        u0, u1 = 2 * pi, 2 * pi + 1
        npair = 2 * tb + 2
        first_pv = (pi == 0)
        last = (pi == npair - 1)
        p.wait("tensor", tok_PT_w[ch], tok_kv_ready[tb], tk_consts["onespv"])
        if first_pv:
            if tok_ctx_free[ch] is not None:
                p.wait("tensor", tok_ctx_free[ch])
            p.wait("tensor", tok_rs_free[0], tok_rs_free[1])
        p.op("tensor", lambda e, u0=u0, ch=ch, fp=first_pv: e.matmul(
            ctx_ap[ch][:], V_sb[:, u0 * 128:(u0 + 1) * 128], PT[ch][:, 0:512],
            start=fp, stop=False))
        p.op("tensor", lambda e, u1=u1, ch=ch, last=last: e.matmul(
            ctx_ap[ch][:], V_sb[:, u1 * 128:(u1 + 1) * 128], PT[ch][:, 512:1024],
            start=False, stop=last))
        p.op("tensor", lambda e, ch=ch, fp=first_pv: e.matmul(
            rs_ap[ch], t_onespv[:], PT[ch][:, 0:512], start=fp, stop=False))
        tkP = p.op("tensor", lambda e, ch=ch, last=last: e.matmul(
            rs_ap[ch], t_onespv[:], PT[ch][:, 512:1024],
            start=False, stop=last), inc=[(Epe, 1)])[0]
        tok_PT_r[ch] = tkP
        return tkP

    def emit_epilogue(ch, qh, tb):
        nonlocal tok_a2a_last
        p.wait("vector", tok_PT_r[ch])
        if tok_rrow[ch] is not None:
            p.wait("vector", tok_rrow[ch])
        tkv = p.op("vector", lambda e, ch=ch: e.tensor_single_scalar(
            t_rrow[0:1, ch * 512:(ch + 1) * 512], rs_ap[ch], EPS_RS,
            ALU.add), inc=[(Eve, 1)])[0]
        tok_rs_free[ch] = tkv
        p.wait("vector", tkv)
        tkv = p.op("vector", lambda e, ch=ch: e.reciprocal(
            t_rrow[0:1, ch * 512:(ch + 1) * 512],
            t_rrow[0:1, ch * 512:(ch + 1) * 512]), inc=[(Eve, 1)])[0]
        p.wait("tensor", tkv, tk_consts["ones1"])
        if deq_state["tok_Bcons"] is not None:
            p.wait("tensor", deq_state["tok_Bcons"])
        tkR = p.op("tensor", lambda e, ch=ch: e.matmul(
            pbC[:, 0:512], t_ones1[:],
            t_rrow[0:1, ch * 512:(ch + 1) * 512], start=True, stop=True),
            inc=[(Epe, 1)])[0]
        tok_rrow[ch] = tkR
        deq_state["tok_B"] = tkR
        p.wait("vector", tkR)
        tkv = p.op("vector", lambda e, ch=ch: e.tensor_copy(R_sb[ch][:],
                                                            pbC[:, 0:512]),
                   inc=[(Eve, 1)])[0]
        deq_state["tok_Bcons"] = tkv
        p.wait("vector", tkv)
        if tok_ctxst[ch] is not None:
            p.wait("vector", tok_ctxst[ch])
        tkv = p.op("vector", lambda e, ch=ch: e.tensor_mul(
            ctx_st[ch][:], ctx_ap[ch][:], R_sb[ch][:]), inc=[(Eve, 1)])[0]
        tok_ctx_free[ch] = tkv
        p.wait("gpsimd", tkv)
        dd, chh = tb // 2, tb % 2
        tok_ctxst[ch] = dma(
            "gpsimd", a2a_in[dd * 512 + qh * 128:dd * 512 + (qh + 1) * 128,
                             chh * 512:(chh + 1) * 512], ctx_st[ch][:], Wctx[ch])
        tok_a2a_last = tok_ctxst[ch]

    def emit_attn_pairgroup(hp, tb):
        nonlocal tok_attn_mm
        if tb == 0 and hp == 0:
            p.wait("tensor", tok_p0[0], tok_p0[1])
        npair = 2 * tb + 2
        lds = [None, None]
        for ch in range(2):
            qh = 2 * hp + ch
            if tok_qt[ch] is not None:
                p.wait("sync", tok_qt[ch])
            p.wait("sync", tok_qscr[(qh, tb)])
            lds[ch] = dma("sync", q_t[ch][:], q_scr[qh, :, tb * 512:(tb + 1) * 512],
                          Lqt[ch])
        emit_S(0, 0, tb, lds[0])
        emit_S(1, 0, tb, lds[1])
        for pi in range(1, npair):
            emit_PV(0, pi - 1, tb)
            emit_S(0, pi, tb, lds[0])
            emit_PV(1, pi - 1, tb)
            emit_S(1, pi, tb, lds[1])
            if pi % 2 == 1:
                emit_deq_chunk()
        tokA = emit_PV(0, npair - 1, tb)
        tokB = emit_PV(1, npair - 1, tb)
        tok_qt[0] = tokA
        tok_qt[1] = tokB
        tok_attn_mm = tokB
        emit_epilogue(0, 2 * hp, tb)
        emit_epilogue(1, 2 * hp + 1, tb)
        emit_deq_chunk()

    xn_h = _carve(AR_XN, 16 * 1024, BF16, (128, 32 * 512))
    xn_h3 = xn_h.rearrange("p (ib c) -> p ib c", ib=32)

    p4_state = {}
    p4_pulled = {}
    tok_xnh_cons = None

    def p4_ln(tp):
        nonlocal tok_xnh_cons, tok_xn_cons
        tok_apply = None
        tok_apply = None
        for s4 in range(4):
            par = s4 % 2
            ssl = s4 % 8
            stc = t_stat[:, ssl * 4:(ssl + 1) * 4]
            if tok_xtm[par] is not None:
                p.wait("sync", tok_xtm[par])
            if s4 == 0 and tok_xnh_cons is not None:
                p.wait("sync", tok_xnh_cons)
            if s4 == 0 and tp == 0:
                p.wait("tensor", tok_xn_cons)
            ldx = dma("sync", x_tm[par][:],
                      d_x_own[tp * 512 + s4 * 128:tp * 512 + (s4 + 1) * 128, :],
                      Lx[par])
            p.wait("vector", ldx)
            if tok_bn[par] is not None:
                p.wait("vector", tok_bn[par])
            if tok_stat[ssl] is not None:
                p.wait("vector", tok_stat[ssl])
            for j in range(8):
                toks = p.op("vector", lambda e, par=par, j=j: e.bn_stats(
                    t_bn[:, par * 48 + j * 6:par * 48 + (j + 1) * 6],
                    x_tm[par][:, j * 512:(j + 1) * 512]),
                    inc=[(Eve, 1)] if j == 7 else None)
            p.wait("vector", toks[0])
            tkagg = p.op("vector", lambda e, par=par, stc=stc: e.bn_aggr(
                stc[:, 0:2], t_bn[:, par * 48:par * 48 + 48]), inc=[(Eve, 1)])[0]
            tok_bn[par] = tkagg
            p.wait("vector", tkagg)
            tkv = p.op("vector", lambda e, stc=stc: e.tensor_single_scalar(
                stc[:, 2:3], stc[:, 0:1], -1.0, ALU.mult), inc=[(Eve, 1)])[0]
            p.wait("scalar", tkv)
            tkl = p.op("scalar", lambda e, stc=stc: e.activation(
                stc[:, 1:2], stc[:, 1:2], AF.Ln, bias=EPS_LN, scale=1.0),
                inc=[(Esc, 1)])[0]
            p.wait("scalar", tkl)
            tks2 = p.op("scalar", lambda e, stc=stc: e.activation(
                stc[:, 3:4], stc[:, 1:2], AF.Exp, bias=0.0, scale=-0.5),
                inc=[(Esc, 1)])[0]
            yield
            p.wait("vector", tks2)
            tkv = p.op("vector", lambda e, par=par, stc=stc: e.tensor_scalar(
                xn_tm[par][:], x_tm[par][:], stc[:, 2:3], stc[:, 3:4],
                op0=ALU.add, op1=ALU.mult), inc=[(Eve, 1)])[0]
            tok_xtm[par] = tkv
            tok_stat[ssl] = tkv
            p.wait("tensor", tkv)
            if s4 == 0 and tok_xnh_cons is not None:
                p.wait("tensor", tok_xnh_cons)
            if s4 == 0 and tp == 0:
                if deq_state["tok_Bcons"] is not None:
                    p.wait("tensor", deq_state["tok_Bcons"])
            for grp in range(4):
                if deq_state["tok_Bcons"] is not None:
                    p.wait("tensor", deq_state["tok_Bcons"])
                for j in range(8):
                    ib = grp * 8 + j
                    tktr = p.op("tensor", lambda e, par=par, ib=ib, j=j:
                                e.transpose(
                        ptr[:, j * 128:(j + 1) * 128],
                        xn_tm[par][:, ib * 128:(ib + 1) * 128], t_ident[:]),
                        inc=[(Epe, 1)] if j == 7 else None)
                deq_state["tok_B"] = tktr[0]
                p.wait("vector", tktr[0])
                tok_apply = p.op("vector", lambda e, s4=s4, grp=grp:
                                 e.tensor_copy(
                    xn_h3[:, grp * 8:(grp + 1) * 8, s4 * 128:(s4 + 1) * 128],
                    ptr[:, 0:1024]), inc=[(Eve, 1)])[0]
                deq_state["tok_Bcons"] = tok_apply
                yield
            tok_xntm_r[par] = tktr[0]
        p4_state[tp] = tok_apply

    tok_evac_last = None

    def p1_ln(tt):
        nonlocal tok_evac_last, tok_xn_cons
        tok_evac_last = None
        for s4 in range(4):
            sub = tt * 4 + s4
            par = sub % 2
            ssl = sub % 8
            stc = t_stat[:, ssl * 4:(ssl + 1) * 4]
            if tok_xtm[par] is not None:
                p.wait("sync", tok_xtm[par])
            ldx = dma("sync", x_tm[par][:], d_x_all[sub * 128:(sub + 1) * 128, :],
                      Lx[par])
            p.wait("scalar", ldx, tok_cpool)
            if tok_stat[ssl] is not None:
                p.wait("scalar", tok_stat[ssl])
            if tok_xntm_r[par] is not None:
                p.wait("scalar", tok_xntm_r[par])
            tki = p.op("scalar", lambda e, par=par, stc=stc: e.activation(
                xn_tm[par][:], x_tm[par][:], AF.Identity, bias=0.0, scale=1.0,
                accum_out=stc[:, 0:1]), inc=[(Esc, 1)])[0]
            p.wait("scalar", tki)
            tok_dump[par] = tks = p.op("scalar", lambda e, par=par, stc=stc:
                                       e.activation(
                xn_tm[par][:], x_tm[par][:], AF.Square, bias=0.0, scale=1.0,
                accum_out=stc[:, 1:2]), inc=[(Esc, 1)])[0]
            p.wait("vector", tks)
            p.op("vector", lambda e, stc=stc: e.tensor_single_scalar(
                stc[:, 2:3], stc[:, 0:1], -1.0 / DM, ALU.mult))
            tkv = p.op("vector", lambda e, stc=stc: e.tensor_single_scalar(
                stc[:, 1:2], stc[:, 1:2], 1.0 / DM, ALU.mult), inc=[(Eve, 1)])[0]
            p.wait("vector", tkv)
            tkm = p.op("vector", lambda e, stc=stc: e.tensor_mul(
                stc[:, 0:1], stc[:, 2:3], stc[:, 2:3]), inc=[(Eve, 1)])[0]
            p.wait("vector", tkm)
            tkv = p.op("vector", lambda e, stc=stc: e.tensor_sub(
                stc[:, 1:2], stc[:, 1:2], stc[:, 0:1]), inc=[(Eve, 1)])[0]
            p.wait("scalar", tkv)
            tkl = p.op("scalar", lambda e, stc=stc: e.activation(
                stc[:, 1:2], stc[:, 1:2], AF.Ln, bias=EPS_LN, scale=1.0),
                inc=[(Esc, 1)])[0]
            p.wait("scalar", tkl)
            tks2 = p.op("scalar", lambda e, stc=stc: e.activation(
                stc[:, 3:4], stc[:, 1:2], AF.Exp, bias=0.0, scale=-0.5),
                inc=[(Esc, 1)])[0]
            p.wait("vector", tks2, tok_dump[par])
            tkv = p.op("vector", lambda e, par=par, stc=stc: e.tensor_scalar(
                xn_tm[par][:], x_tm[par][:], stc[:, 2:3], stc[:, 3:4],
                op0=ALU.add, op1=ALU.mult), inc=[(Eve, 1)])[0]
            tok_xtm[par] = tkv
            tok_stat[ssl] = tkv
            tok_xntm_w[par] = tkv
            yield
            p.wait("tensor", tkv)
            if s4 == 0 and tok_xn_cons is not None:
                p.wait("tensor", tok_xn_cons)
            for grp in range(4):
                if deq_state["tok_Bcons"] is not None:
                    p.wait("tensor", deq_state["tok_Bcons"])
                for j in range(8):
                    ib = grp * 8 + j
                    tktr = p.op("tensor", lambda e, par=par, ib=ib, j=j:
                                e.transpose(
                        ptr[:, j * 128:(j + 1) * 128],
                        xn_tm[par][:, ib * 128:(ib + 1) * 128], t_ident[:]),
                        inc=[(Epe, 1)] if j == 7 else None)
                deq_state["tok_B"] = tktr[0]
                p.wait("vector", tktr[0])
                tok_evac_last = p.op("vector", lambda e, s4=s4, grp=grp:
                                     e.tensor_copy(
                    xn_fm3[:, grp * 8:(grp + 1) * 8, s4 * 128:(s4 + 1) * 128],
                    ptr[:, 0:1024]), inc=[(Eve, 1)])[0]
                deq_state["tok_Bcons"] = tok_evac_last
                yield
            tok_xntm_r[par] = tktr[0]



    lngen = p1_ln(0)
    for _ in lngen:
        pass
    for tt in range(NTT):
        parc = tt % 2
        if tok_cos[parc] is not None:
            p.wait("scalar", tok_cos[parc])
        ldc = dma("scalar", cos_sl[parc][:], d_cos[:, tt * 512:(tt + 1) * 512],
                  Lcs[parc])
        lds = dma("scalar", sin_sl[parc][:], d_sin[:, tt * 512:(tt + 1) * 512],
                  Lcs[parc])

        p.wait("tensor", tok_evac_last)
        p.wait("tensor", tok_S_r[0], tok_S_r[1])
        if tok_q_free is not None:
            p.wait("tensor", tok_q_free)
        for jc in range(4):
            for ib in range(KI):
                tkq = p.op("tensor", lambda e, jc=jc, ib=ib: e.matmul(
                    qb_ap[jc], wq_sb[:, (ib * 4 + jc) * 128:(ib * 4 + jc + 1) * 128],
                    xn_fm[:, ib * 512:(ib + 1) * 512],
                    start=(ib == 0), stop=(ib == KI - 1)),
                    inc=[(Epe, 1)] if (ib == KI - 1) else None)
        tok_proj_q = tkq[0]
        if tok_k_free is not None:
            p.wait("tensor", tok_k_free)
        p.wait("tensor", tok_ctx_free[0])
        for ib in range(KI):
            tkk = p.op("tensor", lambda e, ib=ib: e.matmul(
                pb4[:], wkv_sb[:, ib * 256:ib * 256 + 128],
                xn_fm[:, ib * 512:(ib + 1) * 512],
                start=(ib == 0), stop=(ib == KI - 1)),
                inc=[(Epe, 1)] if (ib == KI - 1) else None)
        tok_proj_k = tkk[0]
        if tok_v_free is not None:
            p.wait("tensor", tok_v_free)
        p.wait("tensor", tok_ctx_free[1])
        for ib in range(KI):
            tkv2 = p.op("tensor", lambda e, ib=ib: e.matmul(
                pb5[:], wkv_sb[:, ib * 256 + 128:ib * 256 + 256],
                xn_fm[:, ib * 512:(ib + 1) * 512],
                start=(ib == 0), stop=(ib == KI - 1)),
                inc=[(Epe, 1)] if (ib == KI - 1) else None)
        tok_proj_v = tkv2[0]
        tok_xn_cons = tok_proj_v

        p.wait("vector", tok_proj_q, ldc, lds)
        for jc in range(4):
            parq = jc % 2
            if tok_stq[parq] is not None:
                p.wait("vector", tok_stq[parq])
            if tok_ropeadd is not None:
                p.wait("vector", tok_ropeadd)
            p.op("vector", lambda e, jc=jc, parc=parc: e.tensor_mul(
                tmpA[:], qb_ap[jc], cos_sl[parc][:]))
            p.op("vector", lambda e, jc=jc, parc=parc: e.tensor_mul(
                tmpB[0:64, :], qb_ap[jc][64:128, :], sin_sl[parc][0:64, :]))
            tkr = p.op("vector", lambda e, jc=jc, parc=parc: e.tensor_mul(
                tmpB[64:128, :], qb_ap[jc][0:64, :], sin_sl[parc][64:128, :]),
                inc=[(Eve, 1)])[0]
            p.wait("vector", tkr)
            tkr = p.op("vector", lambda e, parq=parq: e.tensor_add(
                st_q[parq][:], tmpA[:], tmpB[:]), inc=[(Eve, 1)])[0]
            tok_ropeadd = tkr
            p.wait("gpsimd", tkr)
            tok_stq[parq] = dma("gpsimd", q_scr[jc, :, tt * 512:(tt + 1) * 512],
                                st_q[parq][:], Wq[parq])
            tok_qscr[(jc, tt)] = tok_stq[parq]
            if jc == 1:
                tok_S_r[0] = tkr
            if jc == 3:
                tok_S_r[1] = tkr
                tok_q_free = tkr
        p.wait("vector", tok_proj_k, tok_ropeadd)
        if tok_stk is not None:
            p.wait("vector", tok_stk)
        p.op("vector", lambda e, parc=parc: e.tensor_mul(tmpA[:], pb4[:], cos_sl[parc][:]))
        p.op("vector", lambda e, parc=parc: e.tensor_mul(
            tmpB[0:64, :], pb4[64:128, :], sin_sl[parc][0:64, :]))
        tkr = p.op("vector", lambda e, parc=parc: e.tensor_mul(
            tmpB[64:128, :], pb4[0:64, :], sin_sl[parc][64:128, :]), inc=[(Eve, 1)])[0]
        p.wait("vector", tkr)
        tkr = p.op("vector", lambda e, tt=tt: e.tensor_add(
            K_sb[:, tt * 512:(tt + 1) * 512], tmpA[:], tmpB[:]), inc=[(Eve, 1)])[0]
        tok_ropeadd = tkr
        tok_k_free = tkr
        tok_ctx_free[0] = tkr
        tok_cos[parc] = tkr
        p.wait("vector", tok_proj_v)
        if tok_stv is not None:
            p.wait("vector", tok_stv)
        tkv3 = p.op("vector", lambda e: e.tensor_copy(st_v[:], pb5[:]), inc=[(Eve, 1)])[0]
        tok_v_free = tkv3
        tok_ctx_free[1] = tkv3
        p.wait("tensor", tkv3)
        if deq_state["tok_Bcons"] is not None:
            p.wait("tensor", deq_state["tok_Bcons"])
        for a in range(4):
            tktr = p.op("tensor", lambda e, a=a: e.transpose(
                ptr[:, a * 128:(a + 1) * 128], st_v[:, a * 128:(a + 1) * 128],
                t_ident[:]), inc=[(Epe, 1)] if a == 3 else None)
        deq_state["tok_B"] = tktr[0]
        p.wait("vector", tktr[0])
        tkv3 = p.op("vector", lambda e, tt=tt: e.tensor_copy(
            V_sb[:, tt * 512:(tt + 1) * 512], ptr[:, 0:512]), inc=[(Eve, 1)])[0]
        deq_state["tok_Bcons"] = tkv3
        tok_kv_ready[tt] = tkv3
        if tt == NTT - 1:
            g0 = p4_ln(0)
            p4_pulled[0] = g0
            deq_state["extra_gen"] = g0
        else:
            deq_state["extra_gen"] = p1_ln(tt + 1)
        for hp in range(2):
            emit_attn_pairgroup(hp, tt)
        while deq_state.get("extra_gen") is not None and tt < NTT - 1:
            try:
                next(deq_state["extra_gen"])
            except StopIteration:
                deq_state["extra_gen"] = None

    while deq_state["idx"] < len(deq_jobs):
        emit_deq_chunk()

    p.wait("gpsimd", (Wctx[0], Wctx[0].n), (Wctx[1], Wctx[1].n))

    def do_a2a(e):
        with allow_ici_all_to_all():
            return e.collective_compute(
                "AllToAll", ALU.bypass, replica_groups=[list(range(NCORE))],
                ins=[a2a_in.ap().opt()], outs=[a2a_out.ap().opt()])

    tok_cc_ctx = p.op("gpsimd", do_a2a, inc=[(Ccc, 1)])[0]

    h_act = _carve(AR_BIG, 0, BF16, (128, 43 * 512))
    w1t_sb = [_carve(AR_W, i * 8 * 1024, BF16, (128, 4096)) for i in range(2)]
    w3t_sb = [_carve(AR_W, 16 * 1024 + i * 8 * 1024, BF16, (128, 4096)) for i in range(2)]
    w2t_sb = [_carve(AR_X, i * 11008, BF16, (128, 43 * 128)) for i in range(2)]
    ha_st = [_carve(AR_W, 32 * 1024 + i * 1024, BF16, (128, 512)) for i in range(2)]
    g_st = [_carve(AR_W, 34 * 1024 + i * 1024, BF16, (128, 512)) for i in range(2)]
    gp_st = [_carve(AR_W, 36 * 1024 + i * 2048, F32, (128, 512)) for i in range(2)]
    sum_st = [_carve(AR_W, 40 * 1024 + i * 2048, F32, (128, 512)) for i in range(2)]

    tok_ag_w1 = (Ccc, 1)
    tok_ag_w3 = (Ccc, 2)
    tok_ag_w2 = (Ccc, 3)
    tok_ag_wo = (Ccc, 4)
    tok_cc_all = (Ccc, 5)

    tok_w1sb = [None, None]
    tok_hact_r = None
    tok_ab_free = [None, None]
    tok_g_free = [None, None]
    tok_ha = [None, None]
    tok_gst = [None, None]
    tok_gpst = [None, None]
    tok_sumst = [None, None]
    tok_w2sb = [None, None]
    tok_gpart = [None] * 32
    tok_gscr = {}
    p.wait("vector", tok_qt[0], tok_qt[1], tok_ctx_free[0], tok_ctx_free[1],
           tok_rrow[0], tok_rrow[1], tok_ctxst[0], tok_ctxst[1],
           (Wctx[0], Wctx[0].n), (Wctx[1], Wctx[1].n))
    p.wait("tensor", tok_S_r[0], tok_S_r[1], tok_qt[0], tok_qt[1])
    if deq_state["tok_Bcons"] is not None:
        p.wait("tensor", deq_state["tok_Bcons"])

    for tp in range(2):
        if tp in p4_pulled:
            for _ in p4_pulled[tp]:
                pass
        else:
            for _ in p4_ln(tp):
                pass
        tok_xnh_ready = p4_state[tp]

        ab_ap = [(pbA[:, 0:512], pbB[:, 0:512]), (pbA[:, 512:1024], pbB[:, 512:1024])]
        g_ap = [pb4[:], pb5[:]]
        for hhalf in range(2):
            for hh in range(43):
                hb = hhalf * 43 + hh
                par = hh % 2
                if tok_w1sb[par] is not None:
                    p.wait("sync", tok_w1sb[par])
                p.wait("sync", tok_ag_w1, tok_ag_w3)
                ld1 = dma("sync", w1t_sb[par][:], w1t_out[hb, :, :], Lw1[par])
                ld3 = dma("sync", w3t_sb[par][:], w3t_out[hb, :, :], Lw1[par])
                a_ap, b_ap = ab_ap[par]
                p.wait("tensor", ld1, ld3, tok_xnh_ready)
                if tok_ab_free[par] is not None:
                    p.wait("tensor", tok_ab_free[par])
                if hb == 0 and tp == 0:
                    p.wait("tensor", tok_S_r[0])
                for ib in range(KI):
                    p.op("tensor", lambda e, par=par, ib=ib, a_ap=a_ap: e.matmul(
                        a_ap, w1t_sb[par][:, ib * 128:(ib + 1) * 128],
                        xn_h[:, ib * 512:(ib + 1) * 512],
                        start=(ib == 0), stop=(ib == KI - 1)))
                for ib in range(KI):
                    tkb = p.op("tensor", lambda e, par=par, ib=ib, b_ap=b_ap: e.matmul(
                        b_ap, w3t_sb[par][:, ib * 128:(ib + 1) * 128],
                        xn_h[:, ib * 512:(ib + 1) * 512],
                        start=(ib == 0), stop=(ib == KI - 1)),
                        inc=[(Epe, 1)] if ib == KI - 1 else None)
                tok_w1sb[par] = tkb[0]
                p.wait("scalar", tkb[0])
                if tok_ha[par] is not None:
                    p.wait("scalar", tok_ha[par])
                tks = p.op("scalar", lambda e, par=par, a_ap=a_ap: e.activation(
                    ha_st[par][:], a_ap, AF.Silu, bias=0.0, scale=1.0),
                    inc=[(Esc, 1)])[0]
                p.wait("vector", tks)
                if tok_hact_r is not None and hh < 2:
                    p.wait("vector", tok_hact_r)
                tkv = p.op("vector", lambda e, par=par, hh=hh, b_ap=b_ap: e.tensor_mul(
                    h_act[:, hh * 512:(hh + 1) * 512], ha_st[par][:], b_ap),
                    inc=[(Eve, 1)])[0]
                tok_ha[par] = tkv
                tok_ab_free[par] = tkv
            tok_hact_ready = tkv

            for jb in range(32):
                par = jb % 2
                p.wait("sync", tok_ag_w2)
                if jb == 0:
                    p.wait("sync", tok_xtm[0], tok_xtm[1])
                if tok_w2sb[par] is not None:
                    p.wait("sync", tok_w2sb[par])
                ldw2 = dma("sync", w2t_sb[par][:],
                           w2t_out[jb, :, hhalf * 43 * 128:(hhalf * 43 + 43) * 128],
                           Lw2[par])
                p.wait("tensor", ldw2, tok_hact_ready)
                if tok_g_free[par] is not None:
                    p.wait("tensor", tok_g_free[par])
                if jb < 2 and tp == 0 and hhalf == 0:
                    p.wait("tensor", tok_ctx_free[0], tok_rs_free[0], tok_ctx_free[1], tok_rs_free[1])
                for kq in range(43):
                    tk2 = p.op("tensor", lambda e, par=par, kq=kq: e.matmul(
                        g_ap[par], w2t_sb[par][:, kq * 128:(kq + 1) * 128],
                        h_act[:, kq * 512:(kq + 1) * 512],
                        start=(kq == 0), stop=(kq == 42)),
                        inc=[(Epe, 1)] if kq == 42 else None)
                tok_w2sb[par] = tk2[0]
                tok_hact_r = tk2[0]
                if hhalf == 0:
                    p.wait("vector", tk2[0])
                    if tok_gpst[par] is not None:
                        p.wait("vector", tok_gpst[par])
                    if tok_gpart[jb] is not None:
                        p.wait("vector", tok_gpart[jb])
                    tkv = p.op("vector", lambda e, par=par: e.tensor_copy(
                        gp_st[par][:], g_ap[par]), inc=[(Eve, 1)])[0]
                    tok_g_free[par] = tkv
                    p.wait("gpsimd", tkv)
                    tok_gpst[par] = dma("gpsimd", gpart_scr[jb * 128:(jb + 1) * 128, :],
                                        gp_st[par][:], Wgp[par])
                    tok_gpart[jb] = tok_gpst[par]
                else:
                    if tok_sumst[par] is not None:
                        p.wait("scalar", tok_sumst[par])
                    p.wait("scalar", tok_gpart[jb], tok_gpst[par])
                    ldgp = dma("scalar", gp_st[par][:],
                               gpart_scr[jb * 128:(jb + 1) * 128, :], Lgp[par])
                    p.wait("vector", tk2[0], ldgp)
                    if tok_sumst[par] is not None:
                        p.wait("vector", tok_sumst[par])
                    tkv = p.op("vector", lambda e, par=par: e.tensor_add(
                        sum_st[par][:], gp_st[par][:], g_ap[par]), inc=[(Eve, 1)])[0]
                    tok_g_free[par] = tkv
                    tok_gpart[jb] = tkv
                    p.wait("scalar", tkv)
                    if tok_gst[par] is not None:
                        p.wait("scalar", tok_gst[par])
                    tks = p.op("scalar", lambda e, par=par: e.activation(
                        g_st[par][:], sum_st[par][:], AF.Silu, bias=0.0, scale=1.0),
                        inc=[(Esc, 1)])[0]
                    tok_sumst[par] = tks
                    p.wait("gpsimd", tks)
                    tok_gst[par] = dma("gpsimd", g_scr[jb * 128:(jb + 1) * 128,
                                                       tp * 512:(tp + 1) * 512],
                                       g_st[par][:], Wg[par])
                    tok_gscr[(jb, tp)] = tok_gst[par]
        tok_xnh_cons = tok_hact_r

    ctx_t = [_carve(AR_BIG, 0 + i * 1024, BF16, (128, 512)) for i in range(3)]
    wo_t = [_carve(AR_BIG, 3072 + i * 1536, BF16, (128, 768)) for i in range(3)]
    gg_t = [_carve(AR_BIG, 7680 + i * 1024, BF16, (128, 512)) for i in range(2)]
    sg_st = [_carve(AR_BIG, 9728 + i * 1024, BF16, (128, 512)) for i in range(2)]

    jgroups = [list(range(6 * i, min(6 * i + 6, 32))) for i in range(6)]
    sbanks = [pbA[:, 0:512], pbA[:, 512:1024], pbB[:, 0:512], pbB[:, 512:1024],
              pb4[:], pb5[:]]
    tok_sbank_free = [None] * 6
    tok_ctxt = [None, None, None]
    tok_wot = [None, None, None]
    tok_ggt = [None, None]
    tok_sgst = [None, None]
    tok_sg_last = None
    phase_gate()
    p.wait("sync", tok_cc_ctx, tok_ag_wo)

    for th in range(2):
        for gi, jcs in enumerate(jgroups):
            for ib in range(KI):
                par = ib % 3
                if tok_ctxt[par] is not None:
                    p.wait("sync", tok_ctxt[par])
                ldc = dma("sync", ctx_t[par][:],
                          a2a_out[ib * 128:(ib + 1) * 128, th * 512:(th + 1) * 512],
                          Lctx[par])
                if tok_wot[par] is not None:
                    p.wait("scalar", tok_wot[par])
                ldw = dma("scalar", wo_t[par][:, 0:len(jcs) * 128],
                          wo_out[ib, :, jcs[0] * 128:(jcs[0] + len(jcs)) * 128],
                          Lwo[par])
                p.wait("tensor", ldc, ldw)
                if ib == 0:
                    for idx in range(len(jcs)):
                        if tok_sbank_free[idx] is not None:
                            p.wait("tensor", tok_sbank_free[idx])
                    if th == 0 and gi == 0:
                        p.wait("tensor", tok_g_free[0], tok_g_free[1],
                               tok_ab_free[0], tok_ab_free[1])
                for idx in range(len(jcs)):
                    tko = p.op("tensor", lambda e, idx=idx, par=par, ib=ib: e.matmul(
                        sbanks[idx], wo_t[par][:, idx * 128:(idx + 1) * 128],
                        ctx_t[par][:], start=(ib == 0), stop=(ib == KI - 1)),
                        inc=[(Epe, 1)] if (idx == len(jcs) - 1) else None)
                tok_ctxt[par] = tko[0]
                tok_wot[par] = tko[0]
            tok_omm = tko[0]
            for idx, jc in enumerate(jcs):
                par = (gi * 6 + idx) % 2
                if tok_ggt[par] is not None:
                    p.wait("scalar", tok_ggt[par])
                ldg = dma("scalar", gg_t[par][:],
                          g_scr[jc * 128:(jc + 1) * 128, th * 512:(th + 1) * 512],
                          Lgg[par])
                p.wait("vector", tok_omm, ldg)
                if tok_sgst[par] is not None:
                    p.wait("vector", tok_sgst[par])
                tkv = p.op("vector", lambda e, idx=idx, par=par: e.tensor_mul(
                    sg_st[par][:], sbanks[idx], gg_t[par][:]), inc=[(Eve, 1)])[0]
                tok_sbank_free[idx] = tkv
                tok_ggt[par] = tkv
                p.wait("gpsimd", tkv)
                tok_sgst[par] = dma("gpsimd", sg_scr[jc * 128:(jc + 1) * 128,
                                                     th * 512:(th + 1) * 512],
                                    sg_st[par][:], Wsg[par])
                tok_sg_last = tok_sgst[par]

    sgt_t = [_carve(AR_SM, 17408 + i * 256, BF16, (128, 128)) for i in range(2)]
    out_st = [_carve(AR_BIG, 11776 + i * 16 * 1024, F32, (128, 4096)) for i in range(2)]
    tok_out = [None, None]
    tok_sgt = [None, None]
    tok_outwr = None
    phase_gate()

    for ts in range(8):
        par = ts % 2
        if tok_xtm[par] is not None:
            p.wait("sync", tok_xtm[par])
        ldx = dma("sync", x_tm[par][:], d_x_own[ts * 128:(ts + 1) * 128, :], Lx[par])
        if tok_out[par] is not None:
            p.wait("vector", tok_out[par])
        for grp in range(4):
            if tok_ptr_r[0] is not None:
                p.wait("tensor", tok_ptr_r[0])
            for j in range(8):
                jc = grp * 8 + j
                parj = jc % 2
                if tok_sgt[parj] is not None:
                    p.wait("scalar", tok_sgt[parj])
                ldsg = dma("scalar", sgt_t[parj][:],
                           sg_scr[jc * 128:(jc + 1) * 128, ts * 128:(ts + 1) * 128],
                           Lsg[parj])
                p.wait("tensor", ldsg)
                tktr = p.op("tensor", lambda e, j=j, parj=parj: e.transpose(
                    ptr[:, j * 128:(j + 1) * 128],
                    sgt_t[parj][:], t_ident[:]),
                    inc=[(Epe, 1)])
                tok_sgt[parj] = tktr[0]
            p.wait("vector", tktr[0], ldx)
            tkv = p.op("vector", lambda e, par=par, grp=grp: e.tensor_add(
                out_st[par][:, grp * 1024:(grp + 1) * 1024],
                ptr[:, 0:1024],
                x_tm[par][:, grp * 1024:(grp + 1) * 1024]), inc=[(Eve, 1)])[0]
            tok_ptr_r[0] = tkv
        tok_xtm[par] = tkv
        p.wait("gpsimd", tkv)
        tok_outwr = dma("gpsimd", d_out[ts * 128:(ts + 1) * 128, :], out_st[par][:],
                        Wout[par])
        tok_out[par] = tok_outwr

    tap_tok = None
    for t, dstt in taps.items():
        p.wait("gpsimd", tok_outwr)
        if t in ("a2a_out",):
            p.wait("gpsimd", tok_cc_ctx)
        tap_tok = dma("gpsimd", dstt.ap(), scr_map[t].ap(), Wtap)

    p.wait("gpsimd", tok_outwr)
    if tap_tok is not None:
        p.wait("gpsimd", tap_tok)

    p.finalize()
    es.close()
    return nc



_NC_CACHE = {}


def kernel(**inputs):
    shared, per_core = _host_prep(inputs)
    key = ()
    if key not in _NC_CACHE:
        _NC_CACHE[key] = build_program()
    nc = _NC_CACHE[key]
    in_maps = []
    for c in range(NCORE):
        m = dict(shared)
        m.update(per_core[c])
        in_maps.append(m)
    res = run_bass_kernel_spmd(nc, in_maps, core_ids=list(range(NCORE)))
    out = np.concatenate([res.results[c]["out_own"] for c in range(NCORE)], axis=0)
    return np.ascontiguousarray(out[None, ...])
